# version 15
# baseline (speedup 1.0000x reference)
import math
import numpy as np
import ml_dtypes
import concourse.bass as bass
import concourse.mybir as mybir
from concourse.bass_utils import run_bass_kernel_spmd

F32 = mybir.dt.float32
BF16 = mybir.dt.bfloat16
I32 = mybir.dt.int32
AF = mybir.ActivationFunctionType
ALU = mybir.AluOpType
AXL = mybir.AxisListType
BF = ml_dtypes.bfloat16

NCORE = 8
TC = 1024
HALO = 256
TCH = TC + HALO
D = 2048
H, HD = 16, 128
QC = 512
M = 16
NB = 512
NBC = 64
NIH, CI = 4, 64
G, DG = 4, 512
WIN = 256
EPSA = 1e-6
EPSLN = 1e-5
INTER = 5504
NIT = 43
LUTN = 16
SCALE = 1.0 / math.sqrt(HD)
CHUNK_TILES = (11, 11, 11, 10)
CHUNK_OFF = (0, 11, 22, 33)
NCHUNK = 4
SEGS = ((0, 512), (512, 512), (1024, 256))

ENGINES = ("tensor", "vector", "scalar", "gpsimd", "sync")


class P:

    def __init__(self, nc):
        self.nc = nc
        self.ops = {e: [] for e in ENGINES}
        self.cnt = {e: 0 for e in ENGINES}
        self.streams = {}
        self.last_w = {}
        self.readers = {}
        self.pre = {}
        self._sn = 0

    def stream(self, pref="s"):
        self._sn += 1
        return f"{pref}{self._sn}"

    def emit(self, eng, fn, reads=(), writes=(), deps=(), stream=None, inc=None, shared=False):
        dep = list(deps)
        for b in reads:
            t = self.last_w.get(b)
            if t is not None:
                dep.append(t)
        for b in writes:
            t = self.last_w.get(b)
            if t is not None:
                dep.append(t)
            dep.extend(self.readers.get(b, ()))
            dep.extend(self.pre.pop(b, ()))
        rdep = []
        for (dk, dv) in dep:
            if dv is None:
                dv = self.streams[dk[2:]]
            rdep.append((dk, dv))
        dep = rdep
        if stream is None:
            key = "c_" + eng
            self.cnt[eng] += 1
            tok = (key, self.cnt[eng])
            use_inc = 1
        else:
            key = "d_" + stream
            use_inc = 16 if inc is None else inc
            v = self.streams.get(stream, 0) + use_inc
            self.streams[stream] = v
            tok = (key, None) if shared else (key, v)
        self.ops[eng].append((fn, tuple(dep), key, use_inc))
        for b in reads:
            self.readers.setdefault(b, []).append(tok)
        for b in writes:
            self.last_w[b] = tok
            self.readers[b] = []
        return tok

    def dma(self, eng, dst, src, buf=None, reads=(), deps=(), stream=None, shared=False):
        st = stream or self.stream("q")
        wr = [buf] if buf else []
        return self.emit(eng, lambda e, d=dst, s=src: e.dma_start(out=d, in_=s),
                         reads=reads, writes=wr, deps=deps, stream=st, shared=shared)

    def join_tokens(self, bufs):
        out = []
        for b in bufs:
            t = self.last_w.get(b)
            if t is not None:
                out.append(t)
            out.extend(self.readers.get(b, ()))
        seen = set()
        res = []
        for t in out:
            if t not in seen:
                seen.add(t)
                res.append(t)
        return res

    def seed(self, bufs_to, tokens):
        for b in bufs_to:
            self.pre.setdefault(b, []).extend(tokens)

    def acc_deps(self, buf):
        dep = []
        t = self.last_w.get(buf)
        if t is not None:
            dep.append(t)
        dep.extend(self.readers.get(buf, ()))
        return dep

    def set_write(self, buf, tok):
        self.last_w[buf] = tok
        self.readers[buf] = []

    def add_read(self, buf, tok):
        self.readers.setdefault(buf, []).append(tok)

    def mm_group(self, bank_buf, mlist, reads, deps=()):
        dep = list(deps) + self.acc_deps(bank_buf)
        for b in reads:
            t = self.last_w.get(b)
            if t is not None:
                dep.append(t)
        n = len(mlist)
        tok = None
        for i, (o, l, r) in enumerate(mlist):
            tok = self.emit("tensor",
                            lambda e, o=o, l=l, r=r, st=(i == 0), sp=(i == n - 1):
                            e.matmul(o, l, r, start=st, stop=sp),
                            deps=dep if i == 0 else [])
            dep = []
        for b in reads:
            self.add_read(b, tok)
        self.set_write(bank_buf, tok)
        return tok

    def build(self):
        nc = self.nc
        import contextlib
        stack = contextlib.ExitStack()
        sems = {}
        for e in ENGINES:
            sems["c_" + e] = stack.enter_context(nc.semaphore("c_" + e))
        for s in self.streams:
            sems["d_" + s] = stack.enter_context(nc.semaphore("d_" + s))
        blk = stack.enter_context(nc.Block())
        for e in ENGINES:
            ops = self.ops[e]
            if not ops:
                continue

            def make(engine_ops):
                def body(eng):
                    waited = {}
                    for fn, deps, key, inc in engine_ops:
                        for (dk, dv) in deps:
                            if waited.get(dk, -1) >= dv:
                                continue
                            eng.wait_ge(sems[dk], dv)
                            waited[dk] = dv
                        fn(eng).then_inc(sems[key], inc)
                return body
            getattr(blk, e)(make(ops))
        stack.close()


def build_program(debug=False):
    nc = bass.Bass("TRN2", target_bir_lowering=False)
    import contextlib

    def din(name, shape, dt=F32):
        return nc.dram_tensor(name, list(shape), dt, kind="ExternalInput").ap()

    xT = din("xT", (D, TCH))
    xN = din("xN", (TCH, D))
    ropeC = din("ropeC", (32, TCH))
    ropeS = din("ropeS", (32, TCH))
    causal_add = din("causal_add", (8, 128, NB), BF16)
    causal_bin = din("causal_bin", (8, 128, NB), BF16)
    sw_mask = din("sw_mask", (4, 4, 128, WIN), BF16)
    eps_tie = din("eps_tie", (128, NB))
    ones_colB = din("ones_colB", (128, 1), BF16)
    ones_row_f = din("ones_row_f", (1, 128))
    identB = din("identB", (128, 128), BF16)
    identF = din("identF", (128, 128))
    w_qc = din("w_qc", (D, QC))
    w_qup = din("w_qup", (QC, D))
    kvc_w = din("kvc_w", (D, HD))
    kvc_wz = din("kvc_wz", (D, HD))
    kvc_biasT = din("kvc_biasT", (HD, M))
    k_proj_w = din("k_proj_w", (D, HD))
    v_proj_w = din("v_proj_w", (D, HD))
    idx_c_w = din("idx_c_w", (D, CI))
    idx_c_wz = din("idx_c_wz", (D, CI))
    idx_c_biasT = din("idx_c_biasT", (CI, M))
    w_dq = din("w_dq", (D, CI))
    w_iuq = din("w_iuq", (CI, NIH * CI))
    w_w = din("w_w", (D, NIH))
    q_norm_w = din("q_norm_w", (HD, 1))
    k_norm_w = din("k_norm_w", (HD, 1))
    group_w = din("group_w", (G, DG, DG))
    final_w = din("final_w", (D, D))
    lut = din("lut", (1, LUTN))
    gidxT = din("gidxT", (256, INTER), I32)
    uidxT = din("uidxT", (256, INTER), I32)
    didxT = din("didxT", (INTER, 256), I32)

    outT = nc.dram_tensor("outT", [D, TC], F32, kind="ExternalOutput").ap()

    dbg = {}

    def dout(name, shape, dt):
        t = nc.dram_tensor(name, list(shape), dt, kind="ExternalOutput").ap()
        dbg[name] = t
        return t

    cc_cmp_in = nc.dram_tensor("cc_cmp_in", [NBC, HD + CI], F32).ap()
    cc_cmp_out = nc.dram_tensor("cc_cmp_out", [NB, HD + CI], F32).ap()
    wg_in = [nc.dram_tensor(f"wg_in{k}", [256, CHUNK_TILES[k] * 128], BF16).ap() for k in range(NCHUNK)]
    wu_in = [nc.dram_tensor(f"wu_in{k}", [256, CHUNK_TILES[k] * 128], BF16).ap() for k in range(NCHUNK)]
    wd_in = [nc.dram_tensor(f"wd_in{k}", [CHUNK_TILES[k] * 128, 256], BF16).ap() for k in range(NCHUNK)]
    wg_out = [nc.dram_tensor(f"wg_out{k}", [D, CHUNK_TILES[k] * 128], BF16).ap() for k in range(NCHUNK)]
    wu_out = [nc.dram_tensor(f"wu_out{k}", [D, CHUNK_TILES[k] * 128], BF16).ap() for k in range(NCHUNK)]
    wd_out = [nc.dram_tensor(f"wd_out{k}", [NCORE * CHUNK_TILES[k] * 128, 256], BF16).ap() for k in range(NCHUNK)]
    sT_dram = nc.dram_tensor("sT_dram", [D, TC], BF16).ap()
    grp_dram = nc.dram_tensor("grp_dram", [D, TC], BF16).ap()
    fw_bf = nc.dram_tensor("fw_bf", [D, D], BF16).ap()
    wqup_bf = nc.dram_tensor("wqup_bf", [QC, D], BF16).ap()
    hint_dram = nc.dram_tensor("hint_dram", [INTER, TC], BF16).ap()
    stat_dram = [nc.dram_tensor(f"stat_dram{i}", [TCH], F32).ap() for i in range(2)]

    p = P(nc)
    top = contextlib.ExitStack()

    def sb(st, name, shape, dt=BF16):
        return st.enter_context(nc.sbuf_tensor(name, list(shape), dt))

    xnT = sb(top, "xnT", (128, 16, TCH))
    kTt = sb(top, "kTt", (128, TCH))
    vtok = sb(top, "vtok", (128, 10, HD))
    s_kvcw = sb(top, "s_kvcw", (128, 16, HD))
    s_kvcwz = sb(top, "s_kvcwz", (128, 16, HD))
    s_kproj = sb(top, "s_kproj", (128, 16, HD))
    s_vproj = sb(top, "s_vproj", (128, 16, HD))
    s_idxw = sb(top, "s_idxw", (128, 16, CI))
    s_idxwz = sb(top, "s_idxwz", (128, 16, CI))
    s_wdq = sb(top, "s_wdq", (128, 16, CI))
    s_wiuq = sb(top, "s_wiuq", (64, NIH * CI))
    s_ww = sb(top, "s_ww", (128, 16, NIH))
    s_kvcbT = sb(top, "s_kvcbT", (128, M), F32)
    s_idxbT = sb(top, "s_idxbT", (64, M), F32)
    s_qnw_s = sb(top, "s_qnw_s", (128, 1), F32)
    s_knw = sb(top, "s_knw", (128, 1), F32)
    s_knwb = sb(top, "s_knwb", (128, HD), F32)
    s_knwrow = sb(top, "s_knwrow", (1, 128), F32)
    s_onesB = sb(top, "s_onesB", (128, 1), BF16)
    s_onesRF = sb(top, "s_onesRF", (1, 128), F32)
    s_identB = sb(top, "s_identB", (128, 128), BF16)
    s_identF = sb(top, "s_identF", (128, 128), F32)
    s_lutrow = sb(top, "s_lutrow", (1, LUTN), F32)
    s_lutpp = sb(top, "s_lutpp", (128, LUTN), F32)
    s_CC = sb(top, "s_CC", (64, TCH))
    s_SS = sb(top, "s_SS", (64, TCH))
    s_negC = sb(top, "s_negC", (128, 1), F32)
    s_epsc = sb(top, "s_epsc", (128, 2), F32)
    s_scal = sb(top, "s_scal", (1, 8), F32)
    rope_scr = sb(top, "rope_scr", (64, TCH))
    kvnrm = sb(top, "kvnrm", (128, 4, HD))
    kvnrmT = sb(top, "kvnrmT", (128, NB))
    kcompT = sb(top, "kcompT", (128, NB))
    cc_sb = sb(top, "cc_sb", (64, HD + CI), F32)
    kvtok = sb(top, "kvtok", (128, 4, HD + CI), F32)
    gidx_ring = sb(top, "gidx_ring", (128, 1, 1408))
    gacc = sb(top, "gacc", (128, 1408))
    gtmp = sb(top, "gtmp", (128, 1408))
    gidx_ringG = sb(top, "gidx_ringG", (128, 11, 256))
    gaccG = sb(top, "gaccG", (128, 11, 256))
    gtmpG = sb(top, "gtmpG", (128, 11, 256))

    pb = [top.enter_context(nc.psum_tensor(f"pb{i}", [128, 512], F32)) for i in range(7)]
    pbT = top.enter_context(nc.psum_tensor("pbT", [128, 1024], BF16))
    pb.append(pbT[:, :].bitcast(F32))
    PB = [f"pb{i}" for i in range(7)] + ["pbT"]

    def kt16(w):
        return w.rearrange("(kt p) m -> p kt m", p=128)

    p.dma("gpsimd", s_kvcw[:, :, :], kt16(kvc_w), buf="s_kvcw", stream="clw", shared=True)
    p.dma("gpsimd", s_kvcwz[:, :, :], kt16(kvc_wz), buf="s_kvcwz", stream="clw", shared=True)
    p.dma("gpsimd", s_kproj[:, :, :], kt16(k_proj_w), buf="s_kproj", stream="clw", shared=True)
    p.dma("gpsimd", s_vproj[:, :, :], kt16(v_proj_w), buf="s_vproj", stream="clw", shared=True)
    p.dma("gpsimd", s_idxw[:, :, :], kt16(idx_c_w), buf="s_idxw", stream="clw", shared=True)
    p.dma("gpsimd", s_idxwz[:, :, :], kt16(idx_c_wz), buf="s_idxwz", stream="clw", shared=True)
    p.dma("gpsimd", s_wdq[:, :, :], kt16(w_dq), buf="s_wdq", stream="clw", shared=True)
    p.dma("gpsimd", s_wiuq[:, :], w_iuq[:, :], buf="s_wiuq", stream="clw", shared=True)
    p.dma("gpsimd", s_ww[:, :, :], kt16(w_w), buf="s_ww", stream="clw", shared=True)
    p.dma("gpsimd", s_onesB[:, :], ones_colB[:, :], buf="s_onesB", stream="clg", shared=True)
    p.dma("gpsimd", s_identB[:, :], identB[:, :], buf="s_identB", stream="clg", shared=True)
    p.dma("sync", s_identF[:, :], identF[:, :], buf="s_identF", stream="cls", shared=True)
    p.dma("sync", s_kvcbT[:, :], kvc_biasT[:, :], buf="s_kvcbT", stream="cls", shared=True)
    p.dma("sync", s_idxbT[:, :], idx_c_biasT[:, :], buf="s_idxbT", stream="cls", shared=True)
    p.dma("sync", s_onesRF[:, :], ones_row_f[:, :], buf="s_onesRF", stream="cls", shared=True)
    p.dma("sync", s_lutrow[:, :], lut[:, :], buf="s_lutrow", stream="cls", shared=True)
    p.dma("gpsimd", s_CC[0:32, :], ropeC[:, :], buf="s_CC", stream="clg", shared=True)
    p.dma("gpsimd", s_CC[32:64, :], ropeC[:, :], buf="s_CC", stream="clg", shared=True)
    p.dma("gpsimd", s_SS[0:32, :], ropeS[:, :], buf="s_SS", stream="clg", shared=True)
    p.dma("gpsimd", s_SS[32:64, :], ropeS[:, :], buf="s_SS", stream="clg", shared=True)
    p.dma("sync", s_qnw_s[:, :], q_norm_w[:, :], buf="s_qnw_s", stream="cls", shared=True)
    p.dma("sync", s_knw[:, :], k_norm_w[:, :], buf="s_knw", stream="cls", shared=True)

    p.emit("vector", lambda e: e.memset(s_epsc[:, 0:1], EPSLN), writes=["s_epsc"])
    p.emit("vector", lambda e: e.memset(s_epsc[:, 1:2], EPSA), writes=["s_epsc"])
    p.mm_group(PB[0], [(pb[0][:, 0:LUTN], s_onesRF[:, :], s_lutrow[:, :])],
               reads=["s_onesRF", "s_lutrow"])
    p.emit("vector", lambda e: e.tensor_copy(s_lutpp[:, :], pb[0][:, 0:LUTN]),
           reads=[PB[0]], writes=["s_lutpp"])
    p.emit("vector", lambda e: e.tensor_scalar(s_qnw_s[:, :], s_qnw_s[:, :], SCALE, None, op0=ALU.mult),
           reads=["s_qnw_s"], writes=["s_qnw_s"])
    p.emit("vector", lambda e: e.tensor_scalar(s_SS[0:32, :], s_SS[0:32, :], -1.0, None, op0=ALU.mult),
           reads=["s_SS"], writes=["s_SS"])
    p.emit("tensor", lambda e: e.transpose(pb[1][0:1, 0:128], s_knw[:, :], s_identF[:, :]),
           reads=["s_knw", "s_identF"], writes=[PB[1]])
    p.emit("vector", lambda e: e.tensor_copy(s_knwrow[:, :], pb[1][0:1, 0:128]),
           reads=[PB[1]], writes=["s_knwrow"])
    p.mm_group(PB[1], [(pb[1][:, 0:128], s_onesRF[:, :], s_knwrow[:, :])],
               reads=["s_onesRF", "s_knwrow"])
    p.emit("vector", lambda e: e.tensor_copy(s_knwb[:, :], pb[1][:, 0:128]),
           reads=[PB[1]], writes=["s_knwb"])
    p.emit("tensor", lambda e: e.transpose(pb[2][0:1, 0:128], s_qnw_s[:, :], s_identF[:, :]),
           reads=["s_qnw_s", "s_identF"], writes=[PB[2]])
    p.emit("vector", lambda e: e.tensor_reduce(s_scal[:, 0:1], pb[2][0:1, 0:128],
                                               axis=AXL.X, op=ALU.max, apply_absolute_value=True),
           reads=[PB[2]], writes=["s_scal"])
    p.emit("vector", lambda e: e.tensor_reduce(s_scal[:, 1:2], s_knwrow[:, :],
                                               axis=AXL.X, op=ALU.max, apply_absolute_value=True),
           reads=["s_knwrow"], writes=["s_scal"])
    p.emit("vector", lambda e: e.tensor_tensor(s_scal[:, 2:3], s_scal[:, 0:1], s_scal[:, 1:2], ALU.mult),
           reads=["s_scal"], writes=["s_scal"])
    p.emit("vector", lambda e: e.tensor_scalar(s_scal[:, 3:4], s_scal[:, 2:3], -134.4, None, op0=ALU.mult),
           reads=["s_scal"], writes=["s_scal"])
    p.mm_group(PB[2], [(pb[2][:, 0:1], s_onesRF[:, :], s_scal[:, 3:4])],
               reads=["s_onesRF", "s_scal"])
    p.emit("vector", lambda e: e.tensor_copy(s_negC[:, :], pb[2][:, 0:1]),
           reads=[PB[2]], writes=["s_negC"])

    def full_barrier():
        skip = ("cc", "gl", "gs", "gv")
        toks = [("c_" + e, p.cnt[e]) for e in ENGINES if p.cnt[e] > 0]
        toks += [("d_" + s, v) for s, v in p.streams.items()
                 if not any(s.startswith(x) for x in skip)]
        for e in ENGINES:
            p.emit(e, lambda en: en.nop(), deps=toks)

    ln = contextlib.ExitStack()
    xnat_ring = sb(ln, "xnat_ring", (128, 2, D), F32)
    xn_stat = sb(ln, "xn_stat", (128, 10, 2), F32)
    xn_rows = sb(ln, "xn_rows", (1, 2, TCH), F32)
    stat_t = sb(ln, "stat_t", (10, 2, 128), F32)
    ln_row = sb(ln, "ln_row", (1, TCH), F32)
    x_ring = sb(ln, "x_ring", (128, 2, TCH), F32)
    ln_t = sb(ln, "ln_t", (128, 2, 512), F32)

    xnstr = [p.stream("xn") for _ in range(4)]
    for i in range(10):
        sl = i % 2
        p.dma("sync", xnat_ring[:, sl, :], xN[128 * i:128 * (i + 1), :],
              buf=f"xnat{sl}", stream=xnstr[i % 4])
        p.emit("vector", lambda e, i=i, sl=sl: e.tensor_reduce(
            xn_stat[:, i, 0:1], xnat_ring[:, sl, :], axis=AXL.X, op=ALU.add),
            reads=[f"xnat{sl}"], writes=["xn_stat"])
        p.emit("scalar", lambda e, i=i, sl=sl: e.activation(
            xnat_ring[:, sl, :], xnat_ring[:, sl, :], AF.Square, accum_out=xn_stat[:, i, 1:2]),
            reads=[f"xnat{sl}"], writes=[f"xnat{sl}", "xn_stat"])
    for j in range(2):
        p.emit("tensor", lambda e, j=j: e.transpose(
            pb[6][0:10, 0:128], xn_stat[:, :, j], s_identF[:, :]),
            reads=["xn_stat", "s_identF"], writes=[PB[6]])
        p.emit("vector", lambda e, j=j: e.tensor_copy(stat_t[:, j, :], pb[6][0:10, 0:128]),
               reads=[PB[6]], writes=[f"stat_t{j}"])
        p.dma("sync", xn_rows[0:1, j, :].rearrange("o (i q) -> o i q", q=128),
              stat_t[:, j, :], buf="xn_rows", reads=[f"stat_t{j}"], stream="xst", shared=True)
    p.emit("vector", lambda e: e.tensor_scalar(
        xn_rows[:, 0, :], xn_rows[:, 0, :], 1.0 / D, None, op0=ALU.mult),
        reads=["xn_rows"], writes=["xn_rows"])
    p.emit("vector", lambda e: e.tensor_scalar(
        xn_rows[:, 1, :], xn_rows[:, 1, :], 1.0 / D, None, op0=ALU.mult),
        reads=["xn_rows"], writes=["xn_rows"])
    p.emit("vector", lambda e: e.tensor_tensor(
        ln_row[:, :], xn_rows[:, 0, :], xn_rows[:, 0, :], ALU.mult),
        reads=["xn_rows"], writes=["ln_row"])
    p.emit("vector", lambda e: e.tensor_tensor(
        xn_rows[:, 1, :], xn_rows[:, 1, :], ln_row[:, :], ALU.subtract),
        reads=["xn_rows", "ln_row"], writes=["xn_rows"])
    p.emit("scalar", lambda e: e.activation(
        xn_rows[:, 1, :], xn_rows[:, 1, :], AF.Ln, bias=s_epsc[0:1, 0:1], scale=1.0),
        reads=["xn_rows", "s_epsc"], writes=["xn_rows"])
    p.emit("scalar", lambda e: e.activation(
        xn_rows[:, 1, :], xn_rows[:, 1, :], AF.Exp, scale=-0.5),
        reads=["xn_rows"], writes=["xn_rows"])
    for si, (o, n) in enumerate(SEGS):
        p.mm_group(PB[si], [(pb[si][:, 0:n], s_onesRF[:, :], xn_rows[:, 0, o:o + n])],
                   reads=["s_onesRF", "xn_rows"])
        p.mm_group(PB[3 + si], [(pb[3 + si][:, 0:n], s_onesRF[:, :], xn_rows[:, 1, o:o + n])],
                   reads=["s_onesRF", "xn_rows"])
    xrstr = [p.stream("xr") for _ in range(2)]
    for dt in range(16):
        sl = dt % 2
        xb = f"xrr{sl}"
        eng = "vector" if dt % 2 == 0 else "vector"
        p.dma("sync", x_ring[:, sl, :], xT[128 * dt:128 * (dt + 1), :],
              buf=xb, stream=xrstr[sl])
        lb = f"ln_t{dt % 2}"
        lt = ln_t[:, dt % 2, :]
        for si, (o, n) in enumerate(SEGS):
            p.emit(eng, lambda e, lt=lt, sl=sl, si=si, o=o, n=n: e.tensor_tensor(
                lt[:, 0:n], x_ring[:, sl, o:o + n], pb[si][:, 0:n], ALU.subtract),
                reads=[xb, PB[si]], writes=[lb])
            p.emit(eng, lambda e, dt=dt, lt=lt, si=si, o=o, n=n: e.tensor_tensor(
                xnT[:, dt, o:o + n], lt[:, 0:n], pb[3 + si][:, 0:n], ALU.mult),
                reads=[lb, PB[3 + si]], writes=[f"xnT{dt}"])
    ln.close()
    XN = [f"xnT{dt}" for dt in range(16)]
    if debug:
        d_xnT = dout("dbg_xnT", (128, 16, TCH), BF16)
        for dt in range(16):
            p.dma("sync", d_xnT[:, dt, :], xnT[:, dt, :], reads=[f"xnT{dt}"], stream="dbg", shared=True)
    full_barrier()

    qc_st = contextlib.ExitStack()
    wqup_ring = sb(qc_st, "wqup_ring", (128, 2, 4, 128))
    AT = sb(qc_st, "AT", (128, 4, TC))
    gw_st = contextlib.ExitStack()
    swm = sb(gw_st, "swm", (128, 16, WIN))
    wgu_ring = sb(gw_st, "wgu_ring", (128, 2, 16, 128))
    hint_sb = sb(gw_st, "hint_sb", (128, 2, 512))
    gsil2 = sb(gw_st, "gsil2", (128, 512))
    sel_st = contextlib.ExitStack()
    selbinT = sb(sel_st, "selbinT", (128, 4, TC))
    p34_st = contextlib.ExitStack()
    s_wqc = sb(p34_st, "s_wqc", (128, 16, QC))
    k_sc = sb(p34_st, "k_sc", (128, TCH))
    k_row = sb(p34_st, "k_row", (1, 512), F32)
    dqT = sb(p34_st, "dqT", (64, TC))
    qiT = sb(p34_st, "qiT", (128, 2, TC))
    wih = sb(p34_st, "wih", (128, 8, NIH), F32)
    p.dma("gpsimd", s_wqc[:, :, :], kt16(w_qc), buf="s_wqc", stream="clw", shared=True)
    p.dma("sync", swm[:, :, :], sw_mask.rearrange("w k p t -> p (w k) t"), buf="swm", stream="cls", shared=True)

    cc_toks = {}

    def emit_cc(name, in_ap, out_ap, dep_bufs):
        tok = p.emit("gpsimd", lambda e, i=in_ap, o=out_ap: e.collective_compute(
            "AllGather", ALU.bypass, replica_groups=[list(range(NCORE))],
            ins=[i.opt()], outs=[o.opt()]),
            reads=list(dep_bufs), writes=[f"cc_{name}"],
            stream=p.stream("cc"), inc=1)
        cc_toks[name] = tok
        return tok

    units = []

    def add_dve_job(mat_i, k, tr):
        cols = CHUNK_TILES[k] * 128
        src = (gidxT if mat_i == 0 else uidxT)
        dst = (wg_in if mat_i == 0 else wu_in)[k]
        sl = 0
        ring_buf = f"gidx{sl}"
        name = f"g{mat_i}_{k}_{tr}"

        def load():
            p.dma("gpsimd", gidx_ring[:, sl, 0:cols],
                  src[128 * tr:128 * (tr + 1), CHUNK_OFF[k] * 128:CHUNK_OFF[k] * 128 + cols],
                  buf=ring_buf, stream=f"gl{sl}")
        units.append(load)
        for v in range(LUTN):
            if v == 0:
                units.append(lambda sl=sl, cols=cols, rb=ring_buf: p.emit(
                    "vector", lambda e: e.tensor_scalar(
                        gacc[:, 0:cols], gidx_ring[:, sl, 0:cols], 0.0,
                        s_lutpp[:, 0:1], op0=ALU.is_equal, op1=ALU.mult),
                    reads=[rb, "s_lutpp"], writes=["gacc"], stream="gv", inc=1))
            else:
                def u(v=v, sl=sl, cols=cols, rb=ring_buf):
                    p.emit("vector", lambda e: e.tensor_scalar(
                        gtmp[:, 0:cols], gidx_ring[:, sl, 0:cols], float(v),
                        s_lutpp[:, v:v + 1], op0=ALU.is_equal, op1=ALU.mult),
                        reads=[rb, "s_lutpp"], writes=["gtmp"], stream="gv", inc=1)
                    p.emit("vector", lambda e: e.tensor_tensor(
                        gacc[:, 0:cols], gacc[:, 0:cols], gtmp[:, 0:cols], ALU.add),
                        reads=["gtmp"], writes=["gacc"], stream="gv", inc=1)
                units.append(u)

        def store():
            p.dma("sync", dst[128 * tr:128 * (tr + 1), :], gacc[:, 0:cols],
                  buf=f"gst_{name}", reads=["gacc"], stream=f"gs{sl}")
        units.append(store)

    def add_down_job(k):
        tiles = CHUNK_TILES[k]
        cols = tiles * 256

        def load():
            p.dma("gpsimd", gidx_ringG[:, 0:tiles, :],
                  didxT[CHUNK_OFF[k] * 128:(CHUNK_OFF[k] + tiles) * 128, :]
                  .rearrange("(it p) m -> p it m", p=128),
                  buf="gidxG", stream="glG")
        units.append(load)
        gv = gidx_ringG[:, :, :].rearrange("p a b -> p (a b)")
        av = gaccG[:, :, :].rearrange("p a b -> p (a b)")
        tv = gtmpG[:, :, :].rearrange("p a b -> p (a b)")
        for v in range(LUTN):
            if v == 0:
                units.append(lambda cols=cols: p.emit(
                    "vector", lambda e: e.tensor_scalar(
                        av[:, 0:cols], gv[:, 0:cols], 0.0,
                        s_lutpp[:, 0:1], op0=ALU.is_equal, op1=ALU.mult),
                    reads=["gidxG", "s_lutpp"], writes=["gaccG"], stream="gv", inc=1))
            else:
                def u(v=v, cols=cols):
                    p.emit("vector", lambda e: e.tensor_scalar(
                        tv[:, 0:cols], gv[:, 0:cols], float(v),
                        s_lutpp[:, v:v + 1], op0=ALU.is_equal, op1=ALU.mult),
                        reads=["gidxG", "s_lutpp"], writes=["gtmpG"], stream="gv", inc=1)
                    p.emit("vector", lambda e: e.tensor_tensor(
                        av[:, 0:cols], av[:, 0:cols], tv[:, 0:cols], ALU.add),
                        reads=["gtmpG"], writes=["gaccG"], stream="gv", inc=1)
                units.append(u)

        def store():
            p.dma("sync", wd_in[k].rearrange("(it p) m -> p it m", p=128),
                  gaccG[:, 0:tiles, :],
                  buf=f"gstG{k}", reads=["gaccG"], stream="gsG")
        units.append(store)

    for k in range(NCHUNK):
        add_down_job(k)
        for mat_i in range(2):
            for tr in range(2):
                add_dve_job(mat_i, k, tr)

        def ccs(k=k):
            emit_cc(f"d{k}", wd_in[k], wd_out[k], [f"gstG{k}"])
            emit_cc(f"g{k}", wg_in[k], wg_out[k], [f"gst_g0_{k}_0", f"gst_g0_{k}_1"])
            emit_cc(f"u{k}", wu_in[k], wu_out[k], [f"gst_g1_{k}_0", f"gst_g1_{k}_1"])
        units.append(ccs)

    _pu = {"i": 0}

    def pump(n):
        done = 0
        while done < n and _pu["i"] < len(units):
            units[_pu["i"]]()
            _pu["i"] += 1
            done += 1

    def pump_all():
        pump(len(units))

    f_units = []
    wgu_str2 = [p.stream("w2"), p.stream("w2")]
    hst_str = [p.stream("hs"), p.stream("hs")]

    def add_f_unit(k, it_l, sh):
        it = CHUNK_OFF[k] + it_l
        xcol = HALO + 512 * sh
        sl = it % 2

        def u():
            gu_dep = [cc_toks[f"g{k}"], cc_toks[f"u{k}"]]
            p.dma("scalar", wgu_ring[:, 0, :, :],
                  wg_out[k][:, 128 * it_l:128 * (it_l + 1)].rearrange("(kt p) m -> p kt m", p=128),
                  buf="wgug", deps=gu_dep, stream=wgu_str2[0])
            p.dma("scalar", wgu_ring[:, 1, :, :],
                  wu_out[k][:, 128 * it_l:128 * (it_l + 1)].rearrange("(kt p) m -> p kt m", p=128),
                  buf="wguu", deps=gu_dep, stream=wgu_str2[1])
            p.mm_group(PB[4], [(pb[4][:, :], wgu_ring[:, 0, kt, :],
                                xnT[:, kt, xcol:xcol + 512]) for kt in range(16)],
                       reads=["wgug"] + XN)
            p.mm_group(PB[5], [(pb[5][:, :], wgu_ring[:, 1, kt, :],
                                xnT[:, kt, xcol:xcol + 512]) for kt in range(16)],
                       reads=["wguu"] + XN)
            p.emit("scalar", lambda e: e.activation(gsil2[:, :], pb[4][:, :], AF.Silu),
                   reads=[PB[4]], writes=["gsil2"])
            p.emit("vector", lambda e, sl=sl: e.tensor_tensor(
                hint_sb[:, sl, :], gsil2[:, :], pb[5][:, :], ALU.mult),
                reads=["gsil2", PB[5]], writes=[f"hsb{sl}"])
            p.dma("sync", hint_dram[128 * it:128 * (it + 1), 512 * sh:512 * (sh + 1)],
                  hint_sb[:, sl, :], buf=f"hd_{it}_{sh}", reads=[f"hsb{sl}"],
                  stream=hst_str[sl])
        f_units.append(u)

    for k in range(NCHUNK):
        for it_l in range(CHUNK_TILES[k]):
            for sh in range(2):
                add_f_unit(k, it_l, sh)

    _fu = {"i": 0}

    def fpump(n):
        done = 0
        while done < n and _fu["i"] < len(f_units):
            f_units[_fu["i"]]()
            _fu["i"] += 1
            done += 1

    def fpump_all():
        fpump(len(f_units))

    cmp_st = contextlib.ExitStack()
    cmp_c = sb(cmp_st, "cmp_c", (128, TC), F32)
    cmp_z = sb(cmp_st, "cmp_z", (128, TC), F32)
    cmp_s1 = sb(cmp_st, "cmp_s1", (128, NBC), F32)
    cmp_s2 = sb(cmp_st, "cmp_s2", (128, NBC), F32)
    kv_own = sb(cmp_st, "kv_own", (128, NBC), F32)
    ic_own = sb(cmp_st, "ic_own", (64, NBC), F32)

    def compress(wsb_c, buf_c, wsb_z, buf_z, bias_sb, bias_buf, pdim, out_sb, out_buf):
        for which, (wsb, wbuf, dst, dbuf) in enumerate(
                [(wsb_c, buf_c, cmp_c, "cmp_c"), (wsb_z, buf_z, cmp_z, "cmp_z")]):
            for nch in range(2):
                bank = nch + 2 * which
                p.mm_group(PB[bank],
                           [(pb[bank][0:pdim, :], wsb[:, kt, :],
                             xnT[:, kt, HALO + 512 * nch:HALO + 512 * (nch + 1)])
                            for kt in range(16)],
                           reads=[wbuf] + XN)
                p.emit("scalar", lambda e, b=bank, d=dst, nch=nch, pdim=pdim: e.activation(
                    d[0:pdim, 512 * nch:512 * (nch + 1)], pb[b][0:pdim, :], AF.Identity),
                    reads=[PB[bank]], writes=[dbuf])
        zv = cmp_z[0:pdim, :].rearrange("p (a b) -> p a b", b=M)
        cv = cmp_c[0:pdim, :].rearrange("p (a b) -> p a b", b=M)
        p.emit("vector", lambda e: e.tensor_tensor(
            zv, zv, bias_sb[0:pdim, :].unsqueeze(1).to_broadcast([pdim, NBC, M]), ALU.add),
            reads=["cmp_z", bias_buf], writes=["cmp_z"])
        p.emit("vector", lambda e: e.tensor_reduce(
            cmp_s1[0:pdim, :], zv, axis=AXL.X, op=ALU.max),
            reads=["cmp_z"], writes=["cmp_s1"])
        p.emit("vector", lambda e: e.tensor_tensor(
            zv, zv, cmp_s1[0:pdim, :].unsqueeze(2).to_broadcast([pdim, NBC, M]), ALU.subtract),
            reads=["cmp_z", "cmp_s1"], writes=["cmp_z"])
        p.emit("scalar", lambda e: e.activation(zv, zv, AF.Exp),
               reads=["cmp_z"], writes=["cmp_z"])
        p.emit("vector", lambda e: e.tensor_reduce(
            cmp_s1[0:pdim, :], zv, axis=AXL.X, op=ALU.add),
            reads=["cmp_z"], writes=["cmp_s1"])
        p.emit("vector", lambda e: e.reciprocal(cmp_s2[0:pdim, :], cmp_s1[0:pdim, :]),
               reads=["cmp_s1"], writes=["cmp_s2"])
        p.emit("vector", lambda e: e.tensor_tensor(zv, zv, cv, ALU.mult),
               reads=["cmp_z", "cmp_c"], writes=["cmp_z"])
        p.emit("vector", lambda e: e.tensor_reduce(
            cmp_s1[0:pdim, :], zv, axis=AXL.X, op=ALU.add),
            reads=["cmp_z"], writes=["cmp_s1"])
        p.emit("vector", lambda e: e.tensor_tensor(
            out_sb, cmp_s1[0:pdim, :], cmp_s2[0:pdim, :], ALU.mult),
            reads=["cmp_s1", "cmp_s2"], writes=[out_buf])

    compress(s_kvcw, "s_kvcw", s_kvcwz, "s_kvcwz", s_kvcbT, "s_kvcbT", 128, kv_own[:, :], "kv_own")
    compress(s_idxw, "s_idxw", s_idxwz, "s_idxwz", s_idxbT, "s_idxbT", 64, ic_own[0:64, :], "ic_own")
    p.emit("tensor", lambda e: e.transpose(pb[4][0:64, 0:128], kv_own[:, :], s_identF[:, :]),
           reads=["kv_own", "s_identF"], writes=[PB[4]])
    p.emit("vector", lambda e: e.tensor_copy(cc_sb[:, 0:HD], pb[4][0:64, 0:128]),
           reads=[PB[4]], writes=["cc_sb"])
    p.emit("tensor", lambda e: e.transpose(pb[4][0:64, 0:64], ic_own[0:64, :], s_identF[0:64, 0:64]),
           reads=["ic_own", "s_identF"], writes=[PB[4]])
    p.emit("vector", lambda e: e.tensor_copy(cc_sb[:, HD:HD + CI], pb[4][0:64, 0:64]),
           reads=[PB[4]], writes=["cc_sb"])
    p.dma("sync", cc_cmp_in[:, :], cc_sb[:, :], buf="cc_cmp_st", reads=["cc_sb"], stream="cls", shared=True)
    emit_cc("cmp", cc_cmp_in, cc_cmp_out, ["cc_cmp_st"])
    cmp_st.close()
    _joinC = p.join_tokens(["cmp_c", "cmp_z", "cmp_s1", "cmp_s2", "kv_own", "ic_own",
                            "kv_sc", "kv_ms", "kct"])
    p.seed(["hsr", "isc", "iwork0", "iwork1", "mx8", "vstar", "selbin",
            "cmA0", "cmA1", "cmB0", "cmB1", "s_epstie"]
           + [f"iscA{tt}" for tt in range(8)], _joinC)

    p.dma("gpsimd", wqup_bf[:, :], w_qup[:, :], buf="wqup_bf", stream="glq", shared=True)
    pump(40)


    for si, (o, n) in enumerate(SEGS):
        bank = si % 2
        p.mm_group(PB[bank], [(pb[bank][:, 0:n], s_kproj[:, kt, :], xnT[:, kt, o:o + n])
                              for kt in range(16)], reads=["s_kproj"] + XN)
        p.emit("scalar", lambda e, b=bank, o=o, n=n: e.activation(
            kTt[:, o:o + n], pb[b][:, 0:n], AF.Identity),
            reads=[PB[bank]], writes=["kT"])
        p.emit("scalar", lambda e, o=o, n=n: e.activation(
            k_sc[:, o:o + n], kTt[:, o:o + n], AF.Square),
            reads=["kT"], writes=["k_sc"])
        p.mm_group(PB[2 + bank], [(pb[2 + bank][0:1, 0:n], s_onesB[:, :], k_sc[:, o:o + n])],
                   reads=["s_onesB", "k_sc"])
        p.emit("scalar", lambda e, b=2 + bank, n=n: e.activation(
            pb[b][0:1, 0:n], pb[b][0:1, 0:n], AF.Ln, bias=s_epsc[0:1, 1:2], scale=1.0 / HD),
            reads=[PB[2 + bank], "s_epsc"], writes=[PB[2 + bank]])
        p.emit("scalar", lambda e, b=2 + bank, n=n: e.activation(
            pb[b][0:1, 0:n], pb[b][0:1, 0:n], AF.Exp, scale=-0.5),
            reads=[PB[2 + bank]], writes=[PB[2 + bank]])
        p.emit("vector", lambda e, b=2 + bank, n=n: e.tensor_copy(
            k_row[:, 0:n], pb[b][0:1, 0:n]),
            reads=[PB[2 + bank]], writes=["k_row"])
        p.mm_group(PB[4], [(pb[4][:, 0:n], s_onesRF[:, :], k_row[:, 0:n])],
                   reads=["s_onesRF", "k_row"])
        p.emit("vector", lambda e, o=o, n=n: e.scalar_tensor_tensor(
            kTt[:, o:o + n], kTt[:, o:o + n], s_knw[:, :], pb[4][:, 0:n],
            op0=ALU.mult, op1=ALU.mult),
            reads=["kT", "s_knw", PB[4]], writes=["kT"])
    p.emit("vector", lambda e: e.tensor_copy(rope_scr[0:32, :], kTt[32:64, :]),
           reads=["kT"], writes=["rope_scr"])
    p.emit("vector", lambda e: e.tensor_copy(rope_scr[32:64, :], kTt[0:32, :]),
           reads=["kT"], writes=["rope_scr"])
    p.emit("vector", lambda e: e.tensor_tensor(kTt[0:64, :], kTt[0:64, :], s_CC[:, :], ALU.mult),
           reads=["kT", "s_CC"], writes=["kT"])
    p.emit("vector", lambda e: e.tensor_tensor(rope_scr[:, :], rope_scr[:, :], s_SS[:, :], ALU.mult),
           reads=["rope_scr", "s_SS"], writes=["rope_scr"])
    p.emit("vector", lambda e: e.tensor_tensor(kTt[0:64, :], kTt[0:64, :], rope_scr[:, :], ALU.add),
           reads=["kT", "rope_scr"], writes=["kT"])
    if debug:
        d_k = dout("dbg_k", (128, TCH), BF16)
        p.dma("sync", d_k[:, :], kTt[:, :], reads=["kT"], stream="dbg", shared=True)

    for mt in range(10):
        bank = 5 + (mt % 2)
        p.mm_group(PB[bank], [(pb[bank][:, 0:HD], xnT[:, kt, 128 * mt:128 * (mt + 1)],
                               s_vproj[:, kt, :]) for kt in range(16)],
                   reads=["s_vproj"] + XN)
        p.emit("scalar", lambda e, b=bank, mt=mt: e.activation(
            vtok[:, mt, :], pb[b][:, 0:HD], AF.Identity),
            reads=[PB[bank]], writes=["vtok"])

    p.dma("sync", kvtok[:, :, 0:HD],
          cc_cmp_out[:, 0:HD].rearrange("(a p) d -> p a d", p=128),
          buf="kvtok", deps=[cc_toks["cmp"]], stream="cls", shared=True)
    p.dma("sync", kvtok[:, :, HD:HD + CI],
          cc_cmp_out[:, HD:HD + CI].rearrange("(a p) d -> p a d", p=128),
          buf="kvtok2", deps=[cc_toks["cmp"]], stream="cls", shared=True)
    kvs = contextlib.ExitStack()
    kv_sc = sb(kvs, "kv_sc", (128, 4, HD), F32)
    kv_ms = sb(kvs, "kv_ms", (128, 4, 1), F32)
    kct = sb(kvs, "kct", (128, 4, CI))
    kvv = kvtok[:, :, 0:HD]
    p.emit("vector", lambda e: e.tensor_tensor(kv_sc[:, :, :], kvv, kvv, ALU.mult),
           reads=["kvtok"], writes=["kv_sc"])
    p.emit("vector", lambda e: e.tensor_reduce(kv_ms[:, :, :], kv_sc[:, :, :], axis=AXL.X, op=ALU.add),
           reads=["kv_sc"], writes=["kv_ms"])
    p.emit("scalar", lambda e: e.activation(kv_ms[:, :, :], kv_ms[:, :, :], AF.Ln,
                                            bias=s_epsc[:, 1:2], scale=1.0 / HD),
           reads=["kv_ms", "s_epsc"], writes=["kv_ms"])
    p.emit("scalar", lambda e: e.activation(kv_ms[:, :, :], kv_ms[:, :, :], AF.Exp, scale=-0.5),
           reads=["kv_ms"], writes=["kv_ms"])
    p.emit("vector", lambda e: e.tensor_tensor(
        kv_sc[:, :, :], kvv, kv_ms[:, :, :].to_broadcast([128, 4, HD]), ALU.mult),
        reads=["kvtok", "kv_ms"], writes=["kv_sc"])
    p.emit("vector", lambda e: e.tensor_tensor(
        kvnrm[:, :, :], kv_sc[:, :, :],
        s_knwb[:, :].unsqueeze(1).to_broadcast([128, 4, HD]), ALU.mult),
        reads=["kv_sc", "s_knwb"], writes=["kvnrm"])
    for a in range(4):
        p.emit("tensor", lambda e, a=a: e.transpose(pbT[:, 0:128], kvnrm[:, a, :], s_identB[:, :]),
               reads=["kvnrm", "s_identB"], writes=["pbT"])
        p.emit("vector", lambda e, a=a: e.tensor_copy(kvnrmT[:, 128 * a:128 * (a + 1)], pbT[:, 0:128]),
               reads=["pbT"], writes=["kvnrmT"])
    p.emit("vector", lambda e: e.tensor_copy(kct[:, :, :], kvtok[:, :, HD:HD + CI]),
           reads=["kvtok2"], writes=["kct"])
    for a in range(4):
        p.emit("tensor", lambda e, a=a: e.transpose(pbT[0:64, 0:128], kct[:, a, :], s_identB[:, :]),
               reads=["kct", "s_identB"], writes=["pbT"])
        p.emit("vector", lambda e, a=a: e.tensor_copy(kcompT[0:64, 128 * a:128 * (a + 1)], pbT[0:64, 0:128]),
               reads=["pbT"], writes=["kcompT"])
    p.dma("sync", kcompT[64:128, :], kcompT[0:64, :], buf="kcompT_hi", reads=["kcompT"],
          stream="cls", shared=True)
    if debug:
        d_kvn = dout("dbg_kvn", (128, 4, HD), BF16)
        d_kc = dout("dbg_kc", (128, 4, CI), F32)
        p.dma("sync", d_kvn[:, :, :], kvnrm[:, :, :], reads=["kvnrm"], stream="dbg", shared=True)
        p.dma("sync", d_kc[:, :, :], kvtok[:, :, HD:HD + CI], reads=["kvtok2"], stream="dbg", shared=True)
    kvs.close()

    for ci in range(4):
        p.dma("gpsimd", fw_bf[512 * ci:512 * (ci + 1), :], final_w[512 * ci:512 * (ci + 1), :],
              buf="fw_bf", stream="glf", shared=True)
    pump(40)

    for mt in range(4):
        for ncl in range(2):
            bank = ncl
            p.mm_group(PB[bank], [(pb[bank][:, :], s_wqc[:, kt, 128 * mt:128 * (mt + 1)],
                                   xnT[:, kt, HALO + 512 * ncl:HALO + 512 * (ncl + 1)])
                                  for kt in range(16)], reads=["s_wqc"] + XN)
            p.emit("scalar", lambda e, b=bank, mt=mt, ncl=ncl: e.activation(
                AT[:, mt, 512 * ncl:512 * (ncl + 1)], pb[b][:, :], AF.Identity),
                reads=[PB[bank]], writes=["AT"])
    for ncl in range(2):
        bank = 2 + ncl
        p.mm_group(PB[bank], [(pb[bank][0:64, :], s_wdq[:, kt, :],
                               xnT[:, kt, HALO + 512 * ncl:HALO + 512 * (ncl + 1)])
                              for kt in range(16)], reads=["s_wdq"] + XN)
        p.emit("scalar", lambda e, b=bank, ncl=ncl: e.activation(
            dqT[:, 512 * ncl:512 * (ncl + 1)], pb[b][0:64, :], AF.Identity),
            reads=[PB[bank]], writes=["dqT"])
    for mt in range(2):
        for ncl in range(2):
            bank = 4 + ncl
            p.mm_group(PB[bank], [(pb[bank][:, :], s_wiuq[:, 128 * mt:128 * (mt + 1)],
                                   dqT[:, 512 * ncl:512 * (ncl + 1)])],
                       reads=["s_wiuq", "dqT"])
            p.emit("scalar", lambda e, b=bank, mt=mt, ncl=ncl: e.activation(
                qiT[:, mt, 512 * ncl:512 * (ncl + 1)], pb[b][:, :], AF.Identity),
                reads=[PB[bank]], writes=["qiT"])
    for mt in range(8):
        p.mm_group(PB[6], [(pb[6][:, 0:NIH], xnT[:, kt, HALO + 128 * mt:HALO + 128 * (mt + 1)],
                            s_ww[:, kt, :]) for kt in range(16)],
                   reads=["s_ww"] + XN)
        p.emit("vector", lambda e, mt=mt: e.tensor_copy(wih[:, mt, :], pb[6][:, 0:NIH]),
               reads=[PB[6]], writes=["wih"])

    pump(40)

    isc_st = contextlib.ExitStack()
    hsr = sb(isc_st, "hsr", (128, NB), F32)
    isc = sb(isc_st, "isc", (128, NB), F32)
    isc_all = sb(isc_st, "isc_all", (128, 8, NB), F32)
    iwork = sb(isc_st, "iwork", (128, 2, NB), F32)
    mx8 = sb(isc_st, "mx8", (128, 8), F32)
    vstar = sb(isc_st, "vstar", (128, 1), F32)
    selbin = sb(isc_st, "selbin", (128, NB))
    mask_ring = sb(isc_st, "mask_ring", (128, 4, NB))
    s_epstie = sb(isc_st, "s_epstie", (128, NB), F32)
    p.dma("sync", s_epstie[:, :], eps_tie[:, :], buf="s_epstie", stream="cls", shared=True)
    d_isc = dout("dbg_isc", (8, 128, NB), F32) if debug else None
    d_sel = dout("dbg_sel", (8, 128, NB), BF16) if debug else None

    cm_str = [p.stream("cm") for _ in range(4)]
    for tt in range(8):
        for ih in range(NIH):
            bank = ih % 2
            base = 64 * (ih % 2)
            p.mm_group(PB[bank], [(pb[bank][:, 0:512],
                                   qiT[base:base + 64, ih // 2, 128 * tt:128 * (tt + 1)],
                                   kcompT[base:base + 64, 0:512])],
                       reads=["qiT", "kcompT" if ih % 2 == 0 else "kcompT_hi"])
            p.emit("scalar", lambda e, b=bank: e.activation(hsr[:, :], pb[b][:, 0:512], AF.Relu),
                   reads=[PB[bank]], writes=["hsr"])
            if ih == 0:
                p.emit("vector", lambda e, tt=tt, ih=ih: e.tensor_scalar(
                    isc[:, :], hsr[:, :], wih[:, tt, ih:ih + 1], None, op0=ALU.mult),
                    reads=["hsr", "wih"], writes=["isc"])
            elif ih < NIH - 1:
                p.emit("vector", lambda e, tt=tt, ih=ih: e.scalar_tensor_tensor(
                    isc[:, :], hsr[:, :], wih[:, tt, ih:ih + 1], isc[:, :],
                    op0=ALU.mult, op1=ALU.add),
                    reads=["hsr", "wih", "isc"], writes=["isc"])
            else:
                p.emit("vector", lambda e, tt=tt, ih=ih: e.scalar_tensor_tensor(
                    isc_all[:, tt, :], hsr[:, :], wih[:, tt, ih:ih + 1], isc[:, :],
                    op0=ALU.mult, op1=ALU.add),
                    reads=["hsr", "wih", "isc"], writes=[f"iscA{tt}"])
        pump(6)
        fpump(2)
    for tt in range(8):
        sl = tt % 2
        p.dma("sync", mask_ring[:, sl, :], causal_add[tt, :, :], buf=f"cmA{sl}",
              stream=cm_str[sl])
        p.dma("sync", mask_ring[:, 2 + sl, :], causal_bin[tt, :, :], buf=f"cmB{sl}",
              stream=cm_str[2 + sl])
        p.emit("vector", lambda e, tt=tt: e.tensor_scalar(
            iwork[:, 1, :], isc_all[:, tt, :], 0.0, None, op0=ALU.is_equal),
            reads=[f"iscA{tt}"], writes=["iwork1"])
        p.emit("vector", lambda e: e.tensor_tensor(
            iwork[:, 1, :], iwork[:, 1, :], s_epstie[:, :], ALU.mult),
            reads=["iwork1", "s_epstie"], writes=["iwork1"])
        p.emit("vector", lambda e, tt=tt: e.tensor_tensor(
            isc[:, :], isc_all[:, tt, :], iwork[:, 1, :], ALU.add),
            reads=[f"iscA{tt}", "iwork1"], writes=["isc"])
        p.emit("vector", lambda e, sl=sl: e.tensor_tensor(
            isc[:, :], isc[:, :], mask_ring[:, sl, :], ALU.add),
            reads=["isc", f"cmA{sl}"], writes=["isc"])
        if debug:
            p.dma("sync", d_isc[tt, :, :], isc[:, :], reads=["isc"], stream="dbg", shared=True)
        p.emit("vector", lambda e: e.tensor_copy(iwork[:, 0, :], isc[:, :]),
               reads=["isc"], writes=["iwork0"])
        for r in range(8):
            p.emit("vector", lambda e: e.max(out=mx8[:, :], in_=iwork[:, 0, :]),
                   reads=["iwork0"], writes=["mx8"])
            if r < 7:
                p.emit("vector", lambda e: e.match_replace(
                    out=iwork[:, 1, :], in_to_replace=mx8[:, :], in_values=iwork[:, 0, :],
                    imm_value=-1e38),
                    reads=["mx8", "iwork0"], writes=["iwork1"])
                p.emit("vector", lambda e: e.tensor_copy(iwork[:, 0, :], iwork[:, 1, :]),
                       reads=["iwork1"], writes=["iwork0"])
        p.emit("vector", lambda e: e.tensor_reduce(vstar[:, :], mx8[:, :], axis=AXL.X, op=ALU.min),
               reads=["mx8"], writes=["vstar"])
        p.emit("vector", lambda e: e.tensor_scalar(
            selbin[:, :], isc[:, :], vstar[:, :], None, op0=ALU.is_ge),
            reads=["isc", "vstar"], writes=["selbin"])
        p.emit("vector", lambda e, sl=sl: e.tensor_tensor(
            selbin[:, :], selbin[:, :], mask_ring[:, 2 + sl, :], ALU.mult),
            reads=["selbin", f"cmB{sl}"], writes=["selbin"])
        if debug:
            p.dma("sync", d_sel[tt, :, :], selbin[:, :], reads=["selbin"], stream="dbg", shared=True)
        for nt in range(4):
            p.emit("tensor", lambda e, nt=nt: e.transpose(
                pbT[:, 0:128], selbin[:, 128 * nt:128 * (nt + 1)], s_identB[:, :]),
                reads=["selbin", "s_identB"], writes=["pbT"])
            p.emit("vector", lambda e, nt=nt, tt=tt: e.tensor_copy(
                selbinT[:, nt, 128 * tt:128 * (tt + 1)], pbT[:, 0:128]),
                reads=["pbT"], writes=["selbinT"])
        pump(6)
        fpump(2)
    isc_st.close()
    p34_st.close()
    _join = p.join_tokens(
        ["hsr", "isc", "iwork0", "iwork1", "mx8", "vstar", "selbin"]
        + [f"iscA{tt}" for tt in range(8)] +
        ["placeholder_unused",
         "cmA0", "cmA1", "cmB0", "cmB1", "s_epstie",
         "s_wqc", "k_sc", "k_row", "dqT", "qiT", "wih"])
    p.seed(["qh0", "qh1", "q_scr", "q_row", "Esw0", "Esw1", "Esp0", "Esp1",
            "numsb0", "numsb1", "spt", "gwR", "grpt0", "grpt1",
            "nrowf0", "nrowf1", "nrow10", "nrow11", "spt0", "spt1", "fw0", "fw1", "grd0", "grd1",
            "sTt0", "sTt1"]
           + [f"attn{g}_{h}" for g in range(2) for h in range(4)], _join)

    at_st = contextlib.ExitStack()
    qT_ring = sb(at_st, "qT_ring", (128, 2, TC))
    q_scr = sb(at_st, "q_scr", (128, 512))
    q_row = sb(at_st, "q_row", (1, TC), F32)
    Esw = sb(at_st, "Esw", (128, 2, 4, WIN))
    Esp = sb(at_st, "Esp", (128, 2, 4, 512))
    numsb = sb(at_st, "numsb", (128, 768))
    spt = sb(at_st, "spt", (128, 2, 512))
    attn_g = sb(at_st, "attn_g", (128, 2, 4, TC))
    gw_ring = sb(at_st, "gw_ring", (128, 1, 4, 512))
    grp_tile = sb(at_st, "grp_tile", (128, 2, 512))
    nrowf = sb(at_st, "nrowf", (1, 1536), F32)

    wq_str = [p.stream("wq"), p.stream("wq")]
    d_q = dout("dbg_q", (128, H, TC), BF16) if debug else None
    d_attn = dout("dbg_attn", (128, H, TC), BF16) if debug else None

    def emit_qhead(h):
        sl = h % 2
        qb = f"qh{sl}"
        QB = PB[3]
        wsl = h % 2
        p.dma("scalar", wqup_ring[:, wsl, :, :],
              wqup_bf[:, 128 * h:128 * (h + 1)].rearrange("(kt p) m -> p kt m", p=128),
              buf=f"wq{wsl}", reads=["wqup_bf"], stream=wq_str[wsl])
        for ncl in range(2):
            p.mm_group(QB, [(pb[3][:, :], wqup_ring[:, wsl, kt, :],
                             AT[:, kt, 512 * ncl:512 * (ncl + 1)]) for kt in range(4)],
                       reads=[f"wq{wsl}", "AT"])
            p.emit("scalar", lambda e, ncl=ncl, sl=sl: e.activation(
                qT_ring[:, sl, 512 * ncl:512 * (ncl + 1)], pb[3][:, :], AF.Identity),
                reads=[QB], writes=[qb])
            p.emit("scalar", lambda e, sl=sl, ncl=ncl: e.activation(
                q_scr[:, :], qT_ring[:, sl, 512 * ncl:512 * (ncl + 1)], AF.Square),
                reads=[qb], writes=["q_scr"])
            p.mm_group(QB, [(pb[3][0:1, :], s_onesB[:, :], q_scr[:, :])],
                       reads=["s_onesB", "q_scr"])
            p.emit("scalar", lambda e: e.activation(
                pb[3][0:1, :], pb[3][0:1, :], AF.Ln, bias=s_epsc[0:1, 1:2], scale=1.0 / HD),
                reads=[QB, "s_epsc"], writes=[QB])
            p.emit("scalar", lambda e: e.activation(
                pb[3][0:1, :], pb[3][0:1, :], AF.Exp, scale=-0.5),
                reads=[QB], writes=[QB])
            p.emit("vector", lambda e, ncl=ncl: e.tensor_copy(
                q_row[:, 512 * ncl:512 * (ncl + 1)], pb[3][0:1, :]),
                reads=[QB], writes=["q_row"])
        p.emit("vector", lambda e, sl=sl: e.tensor_copy(rope_scr[0:32, 0:TC], qT_ring[32:64, sl, :]),
               reads=[qb], writes=["rope_scr"])
        p.emit("vector", lambda e, sl=sl: e.tensor_copy(rope_scr[32:64, 0:TC], qT_ring[0:32, sl, :]),
               reads=[qb], writes=["rope_scr"])
        p.emit("vector", lambda e, sl=sl: e.tensor_tensor(
            qT_ring[0:64, sl, :], qT_ring[0:64, sl, :], s_CC[:, HALO:], ALU.mult),
            reads=[qb, "s_CC"], writes=[qb])
        p.emit("vector", lambda e: e.tensor_tensor(
            rope_scr[:, 0:TC], rope_scr[:, 0:TC], s_SS[:, HALO:], ALU.mult),
            reads=["rope_scr", "s_SS"], writes=["rope_scr"])
        p.emit("vector", lambda e, sl=sl: e.tensor_tensor(
            qT_ring[0:64, sl, :], qT_ring[0:64, sl, :], rope_scr[:, 0:TC], ALU.add),
            reads=[qb, "rope_scr"], writes=[qb])
        for ncl in range(2):
            p.mm_group(QB, [(pb[3][:, :], s_onesRF[:, :],
                             q_row[:, 512 * ncl:512 * (ncl + 1)])],
                       reads=["s_onesRF", "q_row"])
            p.emit("vector", lambda e, ncl=ncl, sl=sl: e.scalar_tensor_tensor(
                qT_ring[:, sl, 512 * ncl:512 * (ncl + 1)],
                qT_ring[:, sl, 512 * ncl:512 * (ncl + 1)],
                s_qnw_s[:, :], pb[3][:, :], op0=ALU.mult, op1=ALU.mult),
                reads=[qb, "s_qnw_s", QB], writes=[qb])
        if debug:
            p.dma("sync", d_q[:, h, :], qT_ring[:, sl, :], reads=[qb], stream="dbg", shared=True)
        return qb

    def emit_sw(h, wb, qb):
        sl = h % 2
        g, hr = h // 4, h % 4
        gsl = g % 2
        B0, B1, B2 = (0, 1, 2)
        ebuf = f"Esw{sl}"
        for kt in range(4):
            bank = B0 + kt // 2
            off = 256 * (kt % 2)
            p.mm_group(PB[bank],
                       [(pb[bank][:, off:off + 256],
                         kTt[:, 256 * wb + 128 * kt:256 * wb + 128 * kt + 128],
                         qT_ring[:, sl, WIN * wb:WIN * (wb + 1)])],
                       reads=["kT", qb])
            p.emit("scalar", lambda e, b=bank, off=off, kt=kt, sl=sl: e.activation(
                Esw[:, sl, kt, :], pb[b][:, off:off + 256], AF.Exp, bias=s_negC[:, :]),
                reads=[PB[bank], "s_negC"], writes=[ebuf])
            p.emit("vector", lambda e, kt=kt, wb=wb, sl=sl: e.tensor_tensor(
                Esw[:, sl, kt, :], Esw[:, sl, kt, :], swm[:, 4 * wb + kt, :], ALU.mult),
                reads=[ebuf, "swm"], writes=[ebuf])
        p.mm_group(PB[B2], [(pb[B2][0:1, 0:256], s_onesB[:, :], Esw[:, sl, kt, :])
                            for kt in range(4)], reads=["s_onesB", ebuf])
        p.mm_group(PB[B1], [(pb[B1][:, 256:512], vtok[:, 2 * wb + kt, :], Esw[:, sl, kt, :])
                            for kt in range(4)], reads=["vtok", ebuf])
        p.emit("scalar", lambda e, B2=B2, sl=sl: e.activation(
            nrowf[:, 256 * sl:256 * sl + 256], pb[B2][0:1, 0:256], AF.Ln),
            reads=[PB[B2]], writes=[f"nrowf{sl}"])
        p.emit("scalar", lambda e, sl=sl: e.activation(
            nrowf[:, 256 * sl:256 * sl + 256], nrowf[:, 256 * sl:256 * sl + 256], AF.Exp, scale=-1.0),
            reads=[f"nrowf{sl}"], writes=[f"nrowf{sl}"])
        p.mm_group(PB[B2], [(pb[B2][:, 256:512], s_onesRF[:, :], nrowf[:, 256 * sl:256 * sl + 256])],
                   reads=["s_onesRF", f"nrowf{sl}"])
        p.emit("scalar", lambda e, B1=B1, sl=sl: e.activation(
            numsb[:, 256 * sl:256 * sl + 256], pb[B1][:, 256:512], AF.Identity),
            reads=[PB[B1]], writes=[f"numsb{sl}"])
        p.emit("vector", lambda e, gsl=gsl, hr=hr, wb=wb, B2=B2, sl=sl: e.tensor_tensor(
            attn_g[:, gsl, hr, WIN * wb:WIN * (wb + 1)], numsb[:, 256 * sl:256 * sl + 256],
            pb[B2][:, 256:512], ALU.mult),
            reads=[f"numsb{sl}", PB[B2]], writes=[f"attn{gsl}_{hr}"])

    def emit_sparse(h, ch, qb):
        sl = h % 2
        g, hr = h // 4, h % 4
        gsl = g % 2
        B0, B1, B2 = (0, 1, 2)
        ebuf = f"Esp{sl}"
        for nt in range(4):
            bank = B0 + nt % 2
            p.mm_group(PB[bank], [(pb[bank][:, :], kvnrmT[:, 128 * nt:128 * (nt + 1)],
                                   qT_ring[:, sl, 512 * ch:512 * (ch + 1)])],
                       reads=["kvnrmT", qb])
            p.emit("scalar", lambda e, b=bank, nt=nt, sl=sl: e.activation(
                Esp[:, sl, nt, :], pb[b][:, :], AF.Exp, bias=s_negC[:, :]),
                reads=[PB[bank], "s_negC"], writes=[ebuf])
            p.emit("vector", lambda e, nt=nt, ch=ch, sl=sl: e.tensor_tensor(
                Esp[:, sl, nt, :], Esp[:, sl, nt, :],
                selbinT[:, nt, 512 * ch:512 * (ch + 1)], ALU.mult),
                reads=[ebuf, "selbinT"], writes=[ebuf])
        p.mm_group(PB[B2], [(pb[B2][0:1, :], s_onesB[:, :], Esp[:, sl, nt, :]) for nt in range(4)],
                   reads=["s_onesB", ebuf])
        p.mm_group(PB[B0], [(pb[B0][:, :], kvnrm[:, nt, :], Esp[:, sl, nt, :]) for nt in range(4)],
                   reads=["kvnrm", ebuf])
        p.emit("vector", lambda e, B2=B2, sl=sl: e.tensor_scalar(
            nrowf[:, 512 + 256 * sl:512 + 256 * sl + 256]
            if False else nrowf[:, 512 + 384 * sl:512 + 384 * sl + 384][:, 0:512 - 0][:, 0:512][:, 0:512],
            pb[B2][0:1, :], 1e-30, None, op0=ALU.add),
            reads=[PB[B2]], writes=[f"nrow1{sl}"]) if False else None
        off = 512 + 512 * sl
        p.emit("vector", lambda e, B2=B2, off=off: e.tensor_scalar(
            nrowf[:, off:off + 512], pb[B2][0:1, :], 1e-30, None, op0=ALU.add),
            reads=[PB[B2]], writes=[f"nrow1{sl}"])
        p.emit("scalar", lambda e, off=off: e.activation(nrowf[:, off:off + 512], nrowf[:, off:off + 512], AF.Ln),
               reads=[f"nrow1{sl}"], writes=[f"nrow1{sl}"])
        p.emit("scalar", lambda e, off=off: e.activation(nrowf[:, off:off + 512], nrowf[:, off:off + 512], AF.Exp, scale=-1.0),
               reads=[f"nrow1{sl}"], writes=[f"nrow1{sl}"])
        p.mm_group(PB[B2], [(pb[B2][:, :], s_onesRF[:, :], nrowf[:, off:off + 512])],
                   reads=["s_onesRF", f"nrow1{sl}"])
        p.emit("scalar", lambda e, B0=B0, sl=sl: e.activation(
            numsb[:, 512 + 512 * 0:1024] if False else spt[:, :], pb[B0][:, :], AF.Identity),
            reads=[PB[B0]], writes=[f"spt{sl}"]) if False else None
        p.emit("scalar", lambda e, B0=B0, sl=sl: e.activation(
            spt[:, sl, :], pb[B0][:, :], AF.Identity),
            reads=[PB[B0]], writes=[f"spt{sl}"])
        p.emit("vector", lambda e, B2=B2, sl=sl: e.tensor_tensor(
            spt[:, sl, :], spt[:, sl, :], pb[B2][:, :], ALU.mult),
            reads=[f"spt{sl}", PB[B2]], writes=[f"spt{sl}"])
        p.emit("vector", lambda e, gsl=gsl, hr=hr, ch=ch, sl=sl: e.tensor_tensor(
            attn_g[:, gsl, hr, 512 * ch:512 * (ch + 1)],
            attn_g[:, gsl, hr, 512 * ch:512 * (ch + 1)], spt[:, sl, :], ALU.add),
            reads=[f"spt{sl}", f"attn{gsl}_{hr}"], writes=[f"attn{gsl}_{hr}"])

    gw_str = [p.stream("gw")]
    gd_str = [p.stream("gd"), p.stream("gd")]

    def emit_group(g):
        gsl = g % 2
        gwsl = 0
        deps = [f"attn{gsl}_{hr}" for hr in range(4)]
        p.dma("gpsimd", gw_ring[:, 0, :, :],
              group_w[g, :, :].rearrange("(kt p) m -> p kt m", p=128),
              buf="gwR", stream=gw_str[0])
        for mt in range(4):
            for ncl in range(2):
                bank = ncl
                p.mm_group(PB[bank], [(pb[bank][:, :], gw_ring[:, 0, kt, 128 * mt:128 * (mt + 1)],
                                       attn_g[:, gsl, kt, 512 * ncl:512 * (ncl + 1)])
                                      for kt in range(4)],
                           reads=["gwR"] + deps)
                gslot = (mt + ncl) % 2
                p.emit("scalar", lambda e, b=bank, gslot=gslot: e.activation(
                    grp_tile[:, gslot, :], pb[b][:, :], AF.Identity),
                    reads=[PB[bank]], writes=[f"grpt{gslot}"])
                p.dma("sync", grp_dram[128 * (4 * g + mt):128 * (4 * g + mt + 1),
                                       512 * ncl:512 * (ncl + 1)],
                      grp_tile[:, gslot, :], buf=f"grouped{g}",
                      reads=[f"grpt{gslot}"], stream=gd_str[gslot])

    for h in range(H):
        qb = emit_qhead(h)
        for wb in range(4):
            emit_sw(h, wb, qb)
        for ch in range(2):
            emit_sparse(h, ch, qb)
        if debug:
            gsl, hr = (h // 4) % 2, h % 4
            p.dma("sync", d_attn[:, h, :], attn_g[:, gsl, hr, :], reads=[f"attn{gsl}_{hr}"], stream="dbg", shared=True)
        if h % 4 == 3:
            emit_group(h // 4)
        pump(30)
        fpump(4 if h % 8 < 3 else 3)

    fpump_all()

    if debug:
        d_hint = dout("dbg_hint", (128, NIT, 512), BF16)
        hdeps = []
        for it in range(NIT):
            hdeps.extend(p.acc_deps(f"hd_{it}_1"))
        p.dma("sync", d_hint[:, :, :],
              hint_dram[:, 512:1024].rearrange("(it p) m -> p it m", p=128),
              deps=hdeps, stream="dbg", shared=True)

    at_st.close()
    sel_st.close()
    gw_st.close()
    qc_st.close()


    pump_all()
    _join2 = p.join_tokens(
        ["qh0", "qh1", "q_scr", "q_row", "Esw0", "Esw1", "Esp0", "Esp1",
         "numsb0", "numsb1", "spt", "gwR", "grpt0", "grpt1",
         "nrowf0", "nrowf1", "nrow10", "nrow11", "spt0", "spt1", "fw0", "fw1", "grd0", "grd1",
         "sTt0", "sTt1", "selbinT", "swm", "s_wqup", "AT",
         "wgug", "wguu", "gsil2", "hsb0", "hsb1"]
        + [f"attn{g}_{h}" for g in range(2) for h in range(4)])
    p.seed(["wd00", "wd01", "wd10", "wd11", "gsil_0", "gsil_1", "sg0", "sg1",
            "sTr0", "sTr1", "xf0", "xf1", "o0", "o1",
            "hin0", "hin1", "hin2", "hin3"], _join2)

    p.seed([f"fw{i}" for i in range(6)] + [f"grd{i}" for i in range(6)]
           + ["sTt0", "sTt1"], _join2)

    fin_st = contextlib.ExitStack()
    fw_ring = sb(fin_st, "fw_ring", (128, 6, 1024))
    grd_ring = sb(fin_st, "grd_ring", (128, 6, 512))
    sT_tile = sb(fin_st, "sT_tile", (128, 2, 512))
    fw_str = [p.stream("fw") for _ in range(6)]
    gr_str = [p.stream("gr") for _ in range(6)]
    st_str = [p.stream("st"), p.stream("st")]
    for sc in range(2):
        for pass_ in range(2):
            o0 = 8 * pass_
            for kt in range(16):
                sl = kt % 6
                fsl = kt % 6
                p.dma("scalar", fw_ring[:, fsl, :],
                      fw_bf[128 * kt:128 * (kt + 1), 128 * o0:128 * (o0 + 8)],
                      buf=f"fw{fsl}", reads=["fw_bf"], stream=fw_str[fsl])
                p.dma("sync", grd_ring[:, sl, :],
                      grp_dram[128 * kt:128 * (kt + 1), 512 * sc:512 * (sc + 1)],
                      buf=f"grd{sl}", reads=[f"grouped{kt // 4}"], stream=gr_str[sl])
                for oi in range(8):
                    bname = PB[oi]
                    dep = p.acc_deps(bname) if kt == 0 else []
                    tok = p.emit("tensor",
                                 lambda e, oi=oi, sl=sl, fsl=fsl,
                                 st=(kt == 0), sp=(kt == 15):
                                 e.matmul(pb[oi][:, :], fw_ring[:, fsl, 128 * oi:128 * (oi + 1)],
                                          grd_ring[:, sl, :],
                                          start=st, stop=sp),
                                 reads=[f"fw{fsl}", f"grd{sl}"],
                                 deps=dep)
                    if kt == 15:
                        p.set_write(bname, tok)
            for oi in range(8):
                ot = o0 + oi
                bname = PB[oi]
                sl2 = oi % 2
                p.emit("scalar", lambda e, oi=oi, sl2=sl2: e.activation(
                    sT_tile[:, sl2, :], pb[oi][:, :], AF.Identity),
                    reads=[bname], writes=[f"sTt{sl2}"])
                p.dma("sync", sT_dram[128 * ot:128 * (ot + 1), 512 * sc:512 * (sc + 1)],
                      sT_tile[:, sl2, :], buf=f"sTd_{ot}_{sc}", reads=[f"sTt{sl2}"],
                      stream=st_str[sl2])
    if debug:
        d_sT = dout("dbg_sT", (D, TC), BF16)
        for ot in range(16):
            for sc in range(2):
                p.dma("sync", d_sT[128 * ot:128 * (ot + 1), 512 * sc:512 * (sc + 1)],
                      sT_dram[128 * ot:128 * (ot + 1), 512 * sc:512 * (sc + 1)],
                      deps=p.acc_deps(f"sTd_{ot}_{sc}"), stream="dbg", shared=True)
    fin_st.close()
    ffn_st = contextlib.ExitStack()
    wd_ring = sb(ffn_st, "wd_ring", (128, 2, 2, 11, 256))
    hin_ring = sb(ffn_st, "hin_ring", (128, 4, 1024))
    gsil = sb(ffn_st, "gsil", (128, 2, 512))
    sg_t = sb(ffn_st, "sg_t", (128, 2, 512))
    sTr = sb(ffn_st, "sTr", (128, 8, 512))
    xf_ring = sb(ffn_st, "xf_ring", (128, 8, 512), F32)
    o_ring = sb(ffn_st, "o_ring", (128, 2, 512), F32)

    wd_str = {(kp, ri): p.stream("wd") for kp in range(2) for ri in range(2)}
    hin_str = [p.stream("hi") for _ in range(4)]
    sTr_str = [p.stream("sr") for _ in range(8)]
    xf_str = [p.stream("x2") for _ in range(8)]
    out_str = [p.stream("ou"), p.stream("ou")]
    out_toks = []

    _join2b = p.join_tokens([f"fw{i}" for i in range(6)] + [f"grd{i}" for i in range(6)]
                        + ["sTt0", "sTt1"])
    p.seed(["wd00", "wd01", "wd10", "wd11", "gsil_0", "gsil_1", "sg0", "sg1",
            "o0", "o1", "hin0", "hin1", "hin2", "hin3"]
           + [f"sTr{i}" for i in range(8)] + [f"xf{i}" for i in range(8)], _join2b)

    for og4 in range(4):
        ranks = (2 * og4, 2 * og4 + 1)
        first_mm = [True] * 8
        for sh in range(2):
            xcol = HALO + 512 * sh
            for ri in range(2):
                for half in range(2):
                    oi = 4 * sh + 2 * ri + half
                    ot = 4 * og4 + 2 * ri + half
                    p.dma("sync", sTr[:, oi, :],
                          sT_dram[128 * ot:128 * (ot + 1), 512 * sh:512 * (sh + 1)],
                          buf=f"sTr{oi}", deps=p.acc_deps(f"sTd_{ot}_{sh}"), stream=sTr_str[oi])
                    p.dma("sync", xf_ring[:, oi, :], xT[128 * ot:128 * (ot + 1), xcol:xcol + 512],
                          buf=f"xf{oi}", stream=xf_str[oi])
        for k in range(NCHUNK):
            kp = k % 2
            tiles = CHUNK_TILES[k]
            for ri, r in enumerate(ranks):
                p.dma("scalar" if ri == 0 else "sync", wd_ring[:, kp, ri, 0:tiles, :],
                      wd_out[k][r * tiles * 128:(r + 1) * tiles * 128, :]
                      .rearrange("(it p) m -> p it m", p=128),
                      buf=f"wd{kp}{ri}", deps=[cc_toks[f"d{k}"]], stream=wd_str[(kp, ri)])
            for it_l in range(tiles):
                it = CHUNK_OFF[k] + it_l
                hsl = it % 4
                p.dma("scalar", hin_ring[:, hsl, :],
                      hint_dram[128 * it:128 * (it + 1), :],
                      buf=f"hin{hsl}",
                      deps=p.acc_deps(f"hd_{it}_0") + p.acc_deps(f"hd_{it}_1"),
                      stream=hin_str[hsl])
                last = (k == NCHUNK - 1) and (it_l == tiles - 1)
                for sh in range(2):
                    for ri in range(2):
                        for half in range(2):
                            oi = 4 * sh + 2 * ri + half
                            bname = PB[oi]
                            st = first_mm[oi]
                            dep = p.acc_deps(bname) if st else []
                            if st:
                                first_mm[oi] = False
                            tok = p.emit("tensor",
                                         lambda e, oi=oi, kp=kp, ri=ri, it_l=it_l, hsl=hsl,
                                         half=half, sh=sh, st=st, sp=last:
                                         e.matmul(pb[oi][:, :],
                                                  wd_ring[:, kp, ri, it_l, 128 * half:128 * (half + 1)],
                                                  hin_ring[:, hsl, 512 * sh:512 * (sh + 1)],
                                                  start=st, stop=sp),
                                         reads=[f"wd{kp}{ri}", f"hin{hsl}"],
                                         deps=dep)
                            if last:
                                p.set_write(bname, tok)
        for sh in range(2):
            for ri in range(2):
                for half in range(2):
                    oi = 4 * sh + 2 * ri + half
                    ot = 4 * og4 + 2 * ri + half
                    sl2 = oi % 2
                    p.emit("scalar", lambda e, oi=oi, sl2=sl2: e.activation(
                        gsil[:, sl2, :], pb[oi][:, :], AF.Silu),
                        reads=[PB[oi]], writes=[f"gsil_{sl2}"])
                    p.emit("vector", lambda e, oi=oi, sl2=sl2: e.tensor_tensor(
                        sg_t[:, sl2, :], gsil[:, sl2, :], sTr[:, oi, :], ALU.mult),
                        reads=[f"gsil_{sl2}", f"sTr{oi}"], writes=[f"sg{sl2}"])
                    p.emit("vector", lambda e, oi=oi, sl2=sl2: e.tensor_tensor(
                        o_ring[:, sl2, :], xf_ring[:, oi, :], sg_t[:, sl2, :], ALU.add),
                        reads=[f"xf{oi}", f"sg{sl2}"], writes=[f"o{sl2}"])
                    tok = p.dma("sync", outT[128 * ot:128 * (ot + 1), 512 * sh:512 * (sh + 1)],
                                o_ring[:, sl2, :], buf=f"od_{ot}_{sh}", reads=[f"o{sl2}"],
                                stream=out_str[sl2])
                    out_toks.append(tok)

    final_deps = list(out_toks)
    if "dbg" in p.streams:
        final_deps.append(("d_dbg", p.streams["dbg"]))
    p.emit("gpsimd", lambda e: e.nop(), deps=final_deps)
    ffn_st.close()

    p.build()
    top.close()
    return nc, dbg



def _host_inputs(full):
    x = np.asarray(full["L0_attn_h"])[0]
    T = x.shape[0]
    inv = (1.0 / (10000.0 ** (np.arange(0, 64, 2, dtype=np.float64) / 64.0)))
    gate_idxT = np.ascontiguousarray(np.asarray(full["L0_ffn_gate_idx"]).T)
    up_idxT = np.ascontiguousarray(np.asarray(full["L0_ffn_up_idx"]).T)
    down_idxT = np.ascontiguousarray(np.asarray(full["L0_ffn_down_idx"]).T)
    ones_colB = np.ones((128, 1), dtype=BF)
    ones_row_f = np.ones((1, 128), dtype=np.float32)
    identB = np.eye(128, dtype=BF)
    identF = np.eye(128, dtype=np.float32)
    eps_tie = np.tile((-(np.arange(NB, dtype=np.float64) + 1.0) * 1e-25).astype(np.float32)[None, :],
                      (128, 1))
    kvc_biasT = np.ascontiguousarray(np.asarray(full["L0_attn_kvc_bias"]).T).astype(np.float32)
    idx_biasT = np.ascontiguousarray(np.asarray(full["L0_attn_idx_c_bias"]).T).astype(np.float32)

    in_maps = []
    for c in range(NCORE):
        t0 = TC * c
        gpos = np.arange(t0 - HALO, t0 + TC)
        xa = np.zeros((TCH, D), dtype=np.float32)
        lo = max(0, t0 - HALO)
        xa[HALO - (t0 - lo):] = x[lo:t0 + TC]
        ang = gpos[:, None].astype(np.float64) * inv[None, :]
        ropeCc = np.ascontiguousarray(np.cos(ang).T.astype(np.float32))
        ropeSc = np.ascontiguousarray(np.sin(ang).T.astype(np.float32))
        cadd = np.zeros((8, 128, NB), dtype=BF)
        cbin = np.zeros((8, 128, NB), dtype=BF)
        narr = np.arange(NB)
        for tt in range(8):
            tg = t0 + 128 * tt + np.arange(128)
            valid = (16 * narr[None, :] + 15) < tg[:, None]
            cadd[tt] = np.where(valid, np.float32(0.0), np.float32(-1e30)).astype(BF)
            cbin[tt] = valid.astype(BF)
        swm = np.zeros((4, 4, 128, WIN), dtype=BF)
        for wb in range(4):
            wbg = 4 * c + wb
            j = np.arange(512)
            tq = np.arange(WIN)
            ok = ((j[:, None] - tq[None, :]) > 0) & ((j[:, None] - tq[None, :]) <= WIN)
            if wbg == 0:
                ok &= (j[:, None] >= WIN)
            okb = ok.astype(BF)
            for kt in range(4):
                swm[wb, kt] = okb[128 * kt:128 * (kt + 1), :]
        m = {
            "xT": np.ascontiguousarray(xa.T),
            "xN": xa,
            "ropeC": ropeCc,
            "ropeS": ropeSc,
            "causal_add": cadd,
            "causal_bin": cbin,
            "sw_mask": swm,
            "eps_tie": eps_tie,
            "ones_colB": np.asarray(ones_colB),
            "ones_row_f": ones_row_f,
            "identB": np.asarray(identB),
            "identF": identF,
            "w_qc": np.asarray(full["L0_attn_w_qc"], dtype=np.float32),
            "w_qup": np.asarray(full["L0_attn_w_qup"], dtype=np.float32),
            "kvc_w": np.asarray(full["L0_attn_kvc_w"], dtype=np.float32),
            "kvc_wz": np.asarray(full["L0_attn_kvc_wz"], dtype=np.float32),
            "kvc_biasT": kvc_biasT,
            "k_proj_w": np.asarray(full["L0_attn_k_proj_w"], dtype=np.float32),
            "v_proj_w": np.asarray(full["L0_attn_v_proj_w"], dtype=np.float32),
            "idx_c_w": np.asarray(full["L0_attn_idx_c_w"], dtype=np.float32),
            "idx_c_wz": np.asarray(full["L0_attn_idx_c_wz"], dtype=np.float32),
            "idx_c_biasT": idx_biasT,
            "w_dq": np.asarray(full["L0_attn_w_dq"], dtype=np.float32),
            "w_iuq": np.asarray(full["L0_attn_w_iuq"], dtype=np.float32),
            "w_w": np.asarray(full["L0_attn_w_w"], dtype=np.float32),
            "q_norm_w": np.asarray(full["L0_attn_q_norm_w"], dtype=np.float32).reshape(HD, 1),
            "k_norm_w": np.asarray(full["L0_attn_k_norm_w"], dtype=np.float32).reshape(HD, 1),
            "group_w": np.asarray(full["L0_attn_group_w"], dtype=np.float32),
            "final_w": np.asarray(full["L0_attn_final_w"], dtype=np.float32),
            "lut": np.asarray(full["L0_ffn_lut"], dtype=np.float32).reshape(1, LUTN),
            "gidxT": np.ascontiguousarray(gate_idxT[256 * c:256 * (c + 1), :]),
            "uidxT": np.ascontiguousarray(up_idxT[256 * c:256 * (c + 1), :]),
            "didxT": np.ascontiguousarray(down_idxT[:, 256 * c:256 * (c + 1)]),
        }
        in_maps.append(m)
    return in_maps


_CACHED = {}


def kernel(**inputs):
    if "nc" not in _CACHED:
        nc, _ = build_program(debug=False)
        _CACHED["nc"] = nc
    nc = _CACHED["nc"]
    in_maps = _host_inputs(inputs)
    res = run_bass_kernel_spmd(nc, in_maps, core_ids=list(range(NCORE)))
    parts = [np.asarray(res.results[c]["outT"]).T for c in range(NCORE)]
    out = np.concatenate(parts, axis=0)[None, :, :]
    return out.astype(np.float32)


# version 16
# speedup vs baseline: 1.0856x; 1.0856x over previous
import math
import numpy as np
import ml_dtypes
import concourse.bass as bass
import concourse.mybir as mybir
from concourse.bass_utils import run_bass_kernel_spmd

F32 = mybir.dt.float32
BF16 = mybir.dt.bfloat16
I32 = mybir.dt.int32
AF = mybir.ActivationFunctionType
ALU = mybir.AluOpType
AXL = mybir.AxisListType
BF = ml_dtypes.bfloat16

NCORE = 8
TC = 1024
HALO = 256
TCH = TC + HALO
D = 2048
H, HD = 16, 128
QC = 512
M = 16
NB = 512
NBC = 64
NIH, CI = 4, 64
G, DG = 4, 512
WIN = 256
EPSA = 1e-6
EPSLN = 1e-5
INTER = 5504
NIT = 43
LUTN = 16
SCALE = 1.0 / math.sqrt(HD)
CHUNK_TILES = (11, 11, 11, 10)
CHUNK_OFF = (0, 11, 22, 33)
NCHUNK = 4
SEGS = ((0, 512), (512, 512), (1024, 256))

ENGINES = ("tensor", "vector", "scalar", "gpsimd", "sync")


class P:

    def __init__(self, nc):
        self.nc = nc
        self.ops = {e: [] for e in ENGINES}
        self.cnt = {e: 0 for e in ENGINES}
        self.streams = {}
        self.last_w = {}
        self.readers = {}
        self.pre = {}
        self._sn = 0

    def stream(self, pref="s"):
        self._sn += 1
        return f"{pref}{self._sn}"

    def emit(self, eng, fn, reads=(), writes=(), deps=(), stream=None, inc=None, shared=False):
        dep = list(deps)
        for b in reads:
            t = self.last_w.get(b)
            if t is not None:
                dep.append(t)
        for b in writes:
            t = self.last_w.get(b)
            if t is not None:
                dep.append(t)
            dep.extend(self.readers.get(b, ()))
            dep.extend(self.pre.pop(b, ()))
        rdep = []
        for (dk, dv) in dep:
            if dv is None:
                dv = self.streams[dk[2:]]
            rdep.append((dk, dv))
        dep = rdep
        if stream is None:
            key = "c_" + eng
            self.cnt[eng] += 1
            tok = (key, self.cnt[eng])
            use_inc = 1
        else:
            key = "d_" + stream
            use_inc = 16 if inc is None else inc
            v = self.streams.get(stream, 0) + use_inc
            self.streams[stream] = v
            tok = (key, None) if shared else (key, v)
        self.ops[eng].append((fn, tuple(dep), key, use_inc))
        for b in reads:
            self.readers.setdefault(b, []).append(tok)
        for b in writes:
            self.last_w[b] = tok
            self.readers[b] = []
        return tok

    def dma(self, eng, dst, src, buf=None, reads=(), deps=(), stream=None, shared=False):
        st = stream or self.stream("q")
        wr = [buf] if buf else []
        return self.emit(eng, lambda e, d=dst, s=src: e.dma_start(out=d, in_=s),
                         reads=reads, writes=wr, deps=deps, stream=st, shared=shared)

    def join_tokens(self, bufs):
        out = []
        for b in bufs:
            t = self.last_w.get(b)
            if t is not None:
                out.append(t)
            out.extend(self.readers.get(b, ()))
        seen = set()
        res = []
        for t in out:
            if t not in seen:
                seen.add(t)
                res.append(t)
        return res

    def seed(self, bufs_to, tokens):
        for b in bufs_to:
            self.pre.setdefault(b, []).extend(tokens)

    def acc_deps(self, buf):
        dep = []
        t = self.last_w.get(buf)
        if t is not None:
            dep.append(t)
        dep.extend(self.readers.get(buf, ()))
        return dep

    def set_write(self, buf, tok):
        self.last_w[buf] = tok
        self.readers[buf] = []

    def add_read(self, buf, tok):
        self.readers.setdefault(buf, []).append(tok)

    def mm_group(self, bank_buf, mlist, reads, deps=()):
        dep = list(deps) + self.acc_deps(bank_buf)
        for b in reads:
            t = self.last_w.get(b)
            if t is not None:
                dep.append(t)
        n = len(mlist)
        tok = None
        for i, (o, l, r) in enumerate(mlist):
            tok = self.emit("tensor",
                            lambda e, o=o, l=l, r=r, st=(i == 0), sp=(i == n - 1):
                            e.matmul(o, l, r, start=st, stop=sp),
                            deps=dep if i == 0 else [])
            dep = []
        for b in reads:
            self.add_read(b, tok)
        self.set_write(bank_buf, tok)
        return tok

    def build(self):
        nc = self.nc
        import contextlib
        stack = contextlib.ExitStack()
        sems = {}
        for e in ENGINES:
            sems["c_" + e] = stack.enter_context(nc.semaphore("c_" + e))
        for s in self.streams:
            sems["d_" + s] = stack.enter_context(nc.semaphore("d_" + s))
        blk = stack.enter_context(nc.Block())
        for e in ENGINES:
            ops = self.ops[e]
            if not ops:
                continue

            def make(engine_ops):
                def body(eng):
                    waited = {}
                    for fn, deps, key, inc in engine_ops:
                        for (dk, dv) in deps:
                            if waited.get(dk, -1) >= dv:
                                continue
                            eng.wait_ge(sems[dk], dv)
                            waited[dk] = dv
                        fn(eng).then_inc(sems[key], inc)
                return body
            getattr(blk, e)(make(ops))
        stack.close()


def build_program(debug=False):
    nc = bass.Bass("TRN2", target_bir_lowering=False)
    import contextlib

    def din(name, shape, dt=F32):
        return nc.dram_tensor(name, list(shape), dt, kind="ExternalInput").ap()

    xT = din("xT", (D, TCH))
    xN = din("xN", (TCH, D))
    ropeC = din("ropeC", (32, TCH))
    ropeS = din("ropeS", (32, TCH))
    causal_add = din("causal_add", (8, 128, NB), BF16)
    causal_bin = din("causal_bin", (8, 128, NB), BF16)
    sw_mask = din("sw_mask", (4, 4, 128, WIN), BF16)
    eps_tie = din("eps_tie", (128, NB))
    ones_colB = din("ones_colB", (128, 1), BF16)
    ones_row_f = din("ones_row_f", (1, 128))
    identB = din("identB", (128, 128), BF16)
    identF = din("identF", (128, 128))
    w_qc = din("w_qc", (D, QC))
    w_qup = din("w_qup", (QC, D))
    kvc_w = din("kvc_w", (D, HD))
    kvc_wz = din("kvc_wz", (D, HD))
    kvc_biasT = din("kvc_biasT", (HD, M))
    k_proj_w = din("k_proj_w", (D, HD))
    v_proj_w = din("v_proj_w", (D, HD))
    idx_c_w = din("idx_c_w", (D, CI))
    idx_c_wz = din("idx_c_wz", (D, CI))
    idx_c_biasT = din("idx_c_biasT", (CI, M))
    w_dq = din("w_dq", (D, CI))
    w_iuq = din("w_iuq", (CI, NIH * CI))
    w_w = din("w_w", (D, NIH))
    q_norm_w = din("q_norm_w", (HD, 1))
    k_norm_w = din("k_norm_w", (HD, 1))
    group_w = din("group_w", (G, DG, DG))
    final_w = din("final_w", (D, D))
    lut = din("lut", (1, LUTN))
    gidxT = din("gidxT", (256, INTER), I32)
    uidxT = din("uidxT", (256, INTER), I32)
    didxT = din("didxT", (INTER, 256), I32)

    outT = nc.dram_tensor("outT", [D, TC], F32, kind="ExternalOutput").ap()

    dbg = {}

    def dout(name, shape, dt):
        t = nc.dram_tensor(name, list(shape), dt, kind="ExternalOutput").ap()
        dbg[name] = t
        return t

    cc_cmp_in = nc.dram_tensor("cc_cmp_in", [NBC, HD + CI], F32).ap()
    cc_cmp_out = nc.dram_tensor("cc_cmp_out", [NB, HD + CI], F32).ap()
    wg_in = [nc.dram_tensor(f"wg_in{k}", [256, CHUNK_TILES[k] * 128], BF16).ap() for k in range(NCHUNK)]
    wu_in = [nc.dram_tensor(f"wu_in{k}", [256, CHUNK_TILES[k] * 128], BF16).ap() for k in range(NCHUNK)]
    wd_in = [nc.dram_tensor(f"wd_in{k}", [CHUNK_TILES[k] * 128, 256], BF16).ap() for k in range(NCHUNK)]
    wg_out = [nc.dram_tensor(f"wg_out{k}", [D, CHUNK_TILES[k] * 128], BF16).ap() for k in range(NCHUNK)]
    wu_out = [nc.dram_tensor(f"wu_out{k}", [D, CHUNK_TILES[k] * 128], BF16).ap() for k in range(NCHUNK)]
    wd_out = [nc.dram_tensor(f"wd_out{k}", [NCORE * CHUNK_TILES[k] * 128, 256], BF16).ap() for k in range(NCHUNK)]
    sT_dram = nc.dram_tensor("sT_dram", [D, TC], BF16).ap()
    grp_dram = nc.dram_tensor("grp_dram", [D, TC], BF16).ap()
    fw_bf = nc.dram_tensor("fw_bf", [D, D], BF16).ap()
    wqup_bf = nc.dram_tensor("wqup_bf", [QC, D], BF16).ap()
    hint_dram = nc.dram_tensor("hint_dram", [INTER, TC], BF16).ap()
    stat_dram = [nc.dram_tensor(f"stat_dram{i}", [TCH], F32).ap() for i in range(2)]

    p = P(nc)
    top = contextlib.ExitStack()

    def sb(st, name, shape, dt=BF16):
        return st.enter_context(nc.sbuf_tensor(name, list(shape), dt))

    xnT = sb(top, "xnT", (128, 16, TCH))
    kTt = sb(top, "kTt", (128, TCH))
    vtok = sb(top, "vtok", (128, 10, HD))
    s_kvcw = sb(top, "s_kvcw", (128, 16, HD))
    s_kvcwz = sb(top, "s_kvcwz", (128, 16, HD))
    s_kproj = sb(top, "s_kproj", (128, 16, HD))
    s_vproj = sb(top, "s_vproj", (128, 16, HD))
    s_idxw = sb(top, "s_idxw", (128, 16, CI))
    s_idxwz = sb(top, "s_idxwz", (128, 16, CI))
    s_wdq = sb(top, "s_wdq", (128, 16, CI))
    s_wiuq = sb(top, "s_wiuq", (64, NIH * CI))
    s_ww = sb(top, "s_ww", (128, 16, NIH))
    s_kvcbT = sb(top, "s_kvcbT", (128, M), F32)
    s_idxbT = sb(top, "s_idxbT", (64, M), F32)
    s_qnw_s = sb(top, "s_qnw_s", (128, 1), F32)
    s_knw = sb(top, "s_knw", (128, 1), F32)
    s_knwb = sb(top, "s_knwb", (128, HD), F32)
    s_knwrow = sb(top, "s_knwrow", (1, 128), F32)
    s_onesB = sb(top, "s_onesB", (128, 1), BF16)
    s_onesRF = sb(top, "s_onesRF", (1, 128), F32)
    s_identB = sb(top, "s_identB", (128, 128), BF16)
    s_identF = sb(top, "s_identF", (128, 128), F32)
    s_lutrow = sb(top, "s_lutrow", (1, LUTN), F32)
    s_lutpp = sb(top, "s_lutpp", (128, LUTN), F32)
    s_CC = sb(top, "s_CC", (64, TCH))
    s_SS = sb(top, "s_SS", (64, TCH))
    s_negC = sb(top, "s_negC", (128, 1), F32)
    s_epsc = sb(top, "s_epsc", (128, 2), F32)
    s_scal = sb(top, "s_scal", (1, 8), F32)
    rope_scr = sb(top, "rope_scr", (64, TCH))
    kvnrm = sb(top, "kvnrm", (128, 4, HD))
    kvnrmT = sb(top, "kvnrmT", (128, NB))
    kcompT = sb(top, "kcompT", (128, NB))
    cc_sb = sb(top, "cc_sb", (64, HD + CI), F32)
    kvtok = sb(top, "kvtok", (128, 4, HD + CI), F32)
    gidx_ring = sb(top, "gidx_ring", (128, 1, 1408))
    gacc = sb(top, "gacc", (128, 1408))
    gtmp = sb(top, "gtmp", (128, 1408))
    gidx_ringG = sb(top, "gidx_ringG", (128, 11, 256))
    gaccG = sb(top, "gaccG", (128, 11, 256))
    gtmpG = sb(top, "gtmpG", (128, 11, 256))

    pb = [top.enter_context(nc.psum_tensor(f"pb{i}", [128, 512], F32)) for i in range(7)]
    pbT = top.enter_context(nc.psum_tensor("pbT", [128, 1024], BF16))
    pb.append(pbT[:, :].bitcast(F32))
    PB = [f"pb{i}" for i in range(7)] + ["pbT"]

    def kt16(w):
        return w.rearrange("(kt p) m -> p kt m", p=128)

    p.dma("gpsimd", s_kvcw[:, :, :], kt16(kvc_w), buf="s_kvcw", stream="clw", shared=True)
    p.dma("gpsimd", s_kvcwz[:, :, :], kt16(kvc_wz), buf="s_kvcwz", stream="clw", shared=True)
    p.dma("gpsimd", s_kproj[:, :, :], kt16(k_proj_w), buf="s_kproj", stream="clw", shared=True)
    p.dma("gpsimd", s_vproj[:, :, :], kt16(v_proj_w), buf="s_vproj", stream="clw", shared=True)
    p.dma("gpsimd", s_idxw[:, :, :], kt16(idx_c_w), buf="s_idxw", stream="clw", shared=True)
    p.dma("gpsimd", s_idxwz[:, :, :], kt16(idx_c_wz), buf="s_idxwz", stream="clw", shared=True)
    p.dma("gpsimd", s_wdq[:, :, :], kt16(w_dq), buf="s_wdq", stream="clw", shared=True)
    p.dma("gpsimd", s_wiuq[:, :], w_iuq[:, :], buf="s_wiuq", stream="clw", shared=True)
    p.dma("gpsimd", s_ww[:, :, :], kt16(w_w), buf="s_ww", stream="clw", shared=True)
    p.dma("gpsimd", s_onesB[:, :], ones_colB[:, :], buf="s_onesB", stream="clg", shared=True)
    p.dma("gpsimd", s_identB[:, :], identB[:, :], buf="s_identB", stream="clg", shared=True)
    p.dma("sync", s_identF[:, :], identF[:, :], buf="s_identF", stream="cls", shared=True)
    p.dma("sync", s_kvcbT[:, :], kvc_biasT[:, :], buf="s_kvcbT", stream="cls", shared=True)
    p.dma("sync", s_idxbT[:, :], idx_c_biasT[:, :], buf="s_idxbT", stream="cls", shared=True)
    p.dma("sync", s_onesRF[:, :], ones_row_f[:, :], buf="s_onesRF", stream="cls", shared=True)
    p.dma("sync", s_lutrow[:, :], lut[:, :], buf="s_lutrow", stream="cls", shared=True)
    p.dma("gpsimd", s_CC[0:32, :], ropeC[:, :], buf="s_CC", stream="clg", shared=True)
    p.dma("gpsimd", s_CC[32:64, :], ropeC[:, :], buf="s_CC", stream="clg", shared=True)
    p.dma("gpsimd", s_SS[0:32, :], ropeS[:, :], buf="s_SS", stream="clg", shared=True)
    p.dma("gpsimd", s_SS[32:64, :], ropeS[:, :], buf="s_SS", stream="clg", shared=True)
    p.dma("sync", s_qnw_s[:, :], q_norm_w[:, :], buf="s_qnw_s", stream="cls", shared=True)
    p.dma("sync", s_knw[:, :], k_norm_w[:, :], buf="s_knw", stream="cls", shared=True)

    p.emit("vector", lambda e: e.memset(s_epsc[:, 0:1], EPSLN), writes=["s_epsc"])
    p.emit("vector", lambda e: e.memset(s_epsc[:, 1:2], EPSA), writes=["s_epsc"])
    p.mm_group(PB[0], [(pb[0][:, 0:LUTN], s_onesRF[:, :], s_lutrow[:, :])],
               reads=["s_onesRF", "s_lutrow"])
    p.emit("vector", lambda e: e.tensor_copy(s_lutpp[:, :], pb[0][:, 0:LUTN]),
           reads=[PB[0]], writes=["s_lutpp"])
    p.emit("vector", lambda e: e.tensor_scalar(s_qnw_s[:, :], s_qnw_s[:, :], SCALE, None, op0=ALU.mult),
           reads=["s_qnw_s"], writes=["s_qnw_s"])
    p.emit("vector", lambda e: e.tensor_scalar(s_SS[0:32, :], s_SS[0:32, :], -1.0, None, op0=ALU.mult),
           reads=["s_SS"], writes=["s_SS"])
    p.emit("tensor", lambda e: e.transpose(pb[1][0:1, 0:128], s_knw[:, :], s_identF[:, :]),
           reads=["s_knw", "s_identF"], writes=[PB[1]])
    p.emit("vector", lambda e: e.tensor_copy(s_knwrow[:, :], pb[1][0:1, 0:128]),
           reads=[PB[1]], writes=["s_knwrow"])
    p.mm_group(PB[1], [(pb[1][:, 0:128], s_onesRF[:, :], s_knwrow[:, :])],
               reads=["s_onesRF", "s_knwrow"])
    p.emit("vector", lambda e: e.tensor_copy(s_knwb[:, :], pb[1][:, 0:128]),
           reads=[PB[1]], writes=["s_knwb"])
    p.emit("tensor", lambda e: e.transpose(pb[2][0:1, 0:128], s_qnw_s[:, :], s_identF[:, :]),
           reads=["s_qnw_s", "s_identF"], writes=[PB[2]])
    p.emit("vector", lambda e: e.tensor_reduce(s_scal[:, 0:1], pb[2][0:1, 0:128],
                                               axis=AXL.X, op=ALU.max, apply_absolute_value=True),
           reads=[PB[2]], writes=["s_scal"])
    p.emit("vector", lambda e: e.tensor_reduce(s_scal[:, 1:2], s_knwrow[:, :],
                                               axis=AXL.X, op=ALU.max, apply_absolute_value=True),
           reads=["s_knwrow"], writes=["s_scal"])
    p.emit("vector", lambda e: e.tensor_tensor(s_scal[:, 2:3], s_scal[:, 0:1], s_scal[:, 1:2], ALU.mult),
           reads=["s_scal"], writes=["s_scal"])
    p.emit("vector", lambda e: e.tensor_scalar(s_scal[:, 3:4], s_scal[:, 2:3], -134.4, None, op0=ALU.mult),
           reads=["s_scal"], writes=["s_scal"])
    p.mm_group(PB[2], [(pb[2][:, 0:1], s_onesRF[:, :], s_scal[:, 3:4])],
               reads=["s_onesRF", "s_scal"])
    p.emit("vector", lambda e: e.tensor_copy(s_negC[:, :], pb[2][:, 0:1]),
           reads=[PB[2]], writes=["s_negC"])

    def full_barrier():
        skip = ("cc", "gl", "gs", "gv")
        toks = [("c_" + e, p.cnt[e]) for e in ENGINES if p.cnt[e] > 0]
        toks += [("d_" + s, v) for s, v in p.streams.items()
                 if not any(s.startswith(x) for x in skip)]
        for e in ENGINES:
            p.emit(e, lambda en: en.nop(), deps=toks)

    ln = contextlib.ExitStack()
    xnat_ring = sb(ln, "xnat_ring", (128, 2, D), F32)
    xn_stat = sb(ln, "xn_stat", (128, 10, 2), F32)
    xn_rows = sb(ln, "xn_rows", (1, 2, TCH), F32)
    stat_t = sb(ln, "stat_t", (10, 2, 128), F32)
    ln_row = sb(ln, "ln_row", (1, TCH), F32)
    x_ring = sb(ln, "x_ring", (128, 2, TCH), F32)
    ln_t = sb(ln, "ln_t", (128, 2, 512), F32)

    xnstr = [p.stream("xn") for _ in range(4)]
    for i in range(10):
        sl = i % 2
        p.dma("sync", xnat_ring[:, sl, :], xN[128 * i:128 * (i + 1), :],
              buf=f"xnat{sl}", stream=xnstr[i % 4])
        p.emit("vector", lambda e, i=i, sl=sl: e.tensor_reduce(
            xn_stat[:, i, 0:1], xnat_ring[:, sl, :], axis=AXL.X, op=ALU.add),
            reads=[f"xnat{sl}"], writes=["xn_stat"])
        p.emit("scalar", lambda e, i=i, sl=sl: e.activation(
            xnat_ring[:, sl, :], xnat_ring[:, sl, :], AF.Square, accum_out=xn_stat[:, i, 1:2]),
            reads=[f"xnat{sl}"], writes=[f"xnat{sl}", "xn_stat"])
    for j in range(2):
        p.emit("tensor", lambda e, j=j: e.transpose(
            pb[6][0:10, 0:128], xn_stat[:, :, j], s_identF[:, :]),
            reads=["xn_stat", "s_identF"], writes=[PB[6]])
        p.emit("vector", lambda e, j=j: e.tensor_copy(stat_t[:, j, :], pb[6][0:10, 0:128]),
               reads=[PB[6]], writes=[f"stat_t{j}"])
        p.dma("sync", xn_rows[0:1, j, :].rearrange("o (i q) -> o i q", q=128),
              stat_t[:, j, :], buf="xn_rows", reads=[f"stat_t{j}"], stream="xst", shared=True)
    p.emit("vector", lambda e: e.tensor_scalar(
        xn_rows[:, 0, :], xn_rows[:, 0, :], 1.0 / D, None, op0=ALU.mult),
        reads=["xn_rows"], writes=["xn_rows"])
    p.emit("vector", lambda e: e.tensor_scalar(
        xn_rows[:, 1, :], xn_rows[:, 1, :], 1.0 / D, None, op0=ALU.mult),
        reads=["xn_rows"], writes=["xn_rows"])
    p.emit("vector", lambda e: e.tensor_tensor(
        ln_row[:, :], xn_rows[:, 0, :], xn_rows[:, 0, :], ALU.mult),
        reads=["xn_rows"], writes=["ln_row"])
    p.emit("vector", lambda e: e.tensor_tensor(
        xn_rows[:, 1, :], xn_rows[:, 1, :], ln_row[:, :], ALU.subtract),
        reads=["xn_rows", "ln_row"], writes=["xn_rows"])
    p.emit("scalar", lambda e: e.activation(
        xn_rows[:, 1, :], xn_rows[:, 1, :], AF.Ln, bias=s_epsc[0:1, 0:1], scale=1.0),
        reads=["xn_rows", "s_epsc"], writes=["xn_rows"])
    p.emit("scalar", lambda e: e.activation(
        xn_rows[:, 1, :], xn_rows[:, 1, :], AF.Exp, scale=-0.5),
        reads=["xn_rows"], writes=["xn_rows"])
    for si, (o, n) in enumerate(SEGS):
        p.mm_group(PB[si], [(pb[si][:, 0:n], s_onesRF[:, :], xn_rows[:, 0, o:o + n])],
                   reads=["s_onesRF", "xn_rows"])
        p.mm_group(PB[3 + si], [(pb[3 + si][:, 0:n], s_onesRF[:, :], xn_rows[:, 1, o:o + n])],
                   reads=["s_onesRF", "xn_rows"])
    xrstr = [p.stream("xr") for _ in range(2)]
    for dt in range(16):
        sl = dt % 2
        xb = f"xrr{sl}"
        eng = "vector" if dt % 2 == 0 else "vector"
        p.dma("sync", x_ring[:, sl, :], xT[128 * dt:128 * (dt + 1), :],
              buf=xb, stream=xrstr[sl])
        lb = f"ln_t{dt % 2}"
        lt = ln_t[:, dt % 2, :]
        for si, (o, n) in enumerate(SEGS):
            p.emit(eng, lambda e, lt=lt, sl=sl, si=si, o=o, n=n: e.tensor_tensor(
                lt[:, 0:n], x_ring[:, sl, o:o + n], pb[si][:, 0:n], ALU.subtract),
                reads=[xb, PB[si]], writes=[lb])
            p.emit(eng, lambda e, dt=dt, lt=lt, si=si, o=o, n=n: e.tensor_tensor(
                xnT[:, dt, o:o + n], lt[:, 0:n], pb[3 + si][:, 0:n], ALU.mult),
                reads=[lb, PB[3 + si]], writes=[f"xnT{dt}"])
    ln.close()
    XN = [f"xnT{dt}" for dt in range(16)]
    if debug:
        d_xnT = dout("dbg_xnT", (128, 16, TCH), BF16)
        for dt in range(16):
            p.dma("sync", d_xnT[:, dt, :], xnT[:, dt, :], reads=[f"xnT{dt}"], stream="dbg", shared=True)
    full_barrier()

    qc_st = contextlib.ExitStack()
    wqup_ring = sb(qc_st, "wqup_ring", (128, 2, 4, 128))
    AT = sb(qc_st, "AT", (128, 4, TC))
    gw_st = contextlib.ExitStack()
    swm = sb(gw_st, "swm", (128, 16, WIN))
    wgu_ring = sb(gw_st, "wgu_ring", (128, 2, 16, 128))
    hint_sb = sb(gw_st, "hint_sb", (128, 2, 512))
    gsil2 = sb(gw_st, "gsil2", (128, 512))
    sel_st = contextlib.ExitStack()
    selbinT = sb(sel_st, "selbinT", (128, 4, TC))
    p34_st = contextlib.ExitStack()
    s_wqc = sb(p34_st, "s_wqc", (128, 16, QC))
    k_sc = sb(p34_st, "k_sc", (128, TCH))
    k_row = sb(p34_st, "k_row", (1, 512), F32)
    dqT = sb(p34_st, "dqT", (64, TC))
    qiT = sb(p34_st, "qiT", (128, 2, TC))
    wih = sb(p34_st, "wih", (128, 8, NIH), F32)
    p.dma("gpsimd", s_wqc[:, :, :], kt16(w_qc), buf="s_wqc", stream="clw", shared=True)
    p.dma("sync", swm[:, :, :], sw_mask.rearrange("w k p t -> p (w k) t"), buf="swm", stream="cls", shared=True)

    cc_toks = {}

    def emit_cc(name, in_ap, out_ap, dep_bufs):
        tok = p.emit("gpsimd", lambda e, i=in_ap, o=out_ap: e.collective_compute(
            "AllGather", ALU.bypass, replica_groups=[list(range(NCORE))],
            ins=[i.opt()], outs=[o.opt()]),
            reads=list(dep_bufs), writes=[f"cc_{name}"],
            stream=p.stream("cc"), inc=1)
        cc_toks[name] = tok
        return tok

    units = []

    def add_dve_job(mat_i, k, tr):
        cols = CHUNK_TILES[k] * 128
        src = (gidxT if mat_i == 0 else uidxT)
        dst = (wg_in if mat_i == 0 else wu_in)[k]
        sl = 0
        ring_buf = f"gidx{sl}"
        name = f"g{mat_i}_{k}_{tr}"

        def load():
            p.dma("gpsimd", gidx_ring[:, sl, 0:cols],
                  src[128 * tr:128 * (tr + 1), CHUNK_OFF[k] * 128:CHUNK_OFF[k] * 128 + cols],
                  buf=ring_buf, stream=f"gl{sl}")
        units.append(load)
        for v in range(LUTN):
            if v == 0:
                units.append(lambda sl=sl, cols=cols, rb=ring_buf: p.emit(
                    "vector", lambda e: e.tensor_scalar(
                        gacc[:, 0:cols], gidx_ring[:, sl, 0:cols], 0.0,
                        s_lutpp[:, 0:1], op0=ALU.is_equal, op1=ALU.mult),
                    reads=[rb, "s_lutpp"], writes=["gacc"], stream="gv", inc=1))
            else:
                def u(v=v, sl=sl, cols=cols, rb=ring_buf):
                    p.emit("vector", lambda e: e.tensor_scalar(
                        gtmp[:, 0:cols], gidx_ring[:, sl, 0:cols], float(v),
                        s_lutpp[:, v:v + 1], op0=ALU.is_equal, op1=ALU.mult),
                        reads=[rb, "s_lutpp"], writes=["gtmp"], stream="gv", inc=1)
                    p.emit("vector", lambda e: e.tensor_tensor(
                        gacc[:, 0:cols], gacc[:, 0:cols], gtmp[:, 0:cols], ALU.add),
                        reads=["gtmp"], writes=["gacc"], stream="gv", inc=1)
                units.append(u)

        def store():
            p.dma("sync", dst[128 * tr:128 * (tr + 1), :], gacc[:, 0:cols],
                  buf=f"gst_{name}", reads=["gacc"], stream=f"gs{sl}")
        units.append(store)

    def add_down_job(k):
        tiles = CHUNK_TILES[k]
        cols = tiles * 256

        def load():
            p.dma("gpsimd", gidx_ringG[:, 0:tiles, :],
                  didxT[CHUNK_OFF[k] * 128:(CHUNK_OFF[k] + tiles) * 128, :]
                  .rearrange("(it p) m -> p it m", p=128),
                  buf="gidxG", stream="glG")
        units.append(load)
        gv = gidx_ringG[:, :, :].rearrange("p a b -> p (a b)")
        av = gaccG[:, :, :].rearrange("p a b -> p (a b)")
        tv = gtmpG[:, :, :].rearrange("p a b -> p (a b)")
        for v in range(LUTN):
            if v == 0:
                units.append(lambda cols=cols: p.emit(
                    "vector", lambda e: e.tensor_scalar(
                        av[:, 0:cols], gv[:, 0:cols], 0.0,
                        s_lutpp[:, 0:1], op0=ALU.is_equal, op1=ALU.mult),
                    reads=["gidxG", "s_lutpp"], writes=["gaccG"], stream="gv", inc=1))
            else:
                def u(v=v, cols=cols):
                    p.emit("vector", lambda e: e.tensor_scalar(
                        tv[:, 0:cols], gv[:, 0:cols], float(v),
                        s_lutpp[:, v:v + 1], op0=ALU.is_equal, op1=ALU.mult),
                        reads=["gidxG", "s_lutpp"], writes=["gtmpG"], stream="gv", inc=1)
                    p.emit("vector", lambda e: e.tensor_tensor(
                        av[:, 0:cols], av[:, 0:cols], tv[:, 0:cols], ALU.add),
                        reads=["gtmpG"], writes=["gaccG"], stream="gv", inc=1)
                units.append(u)

        def store():
            p.dma("sync", wd_in[k].rearrange("(it p) m -> p it m", p=128),
                  gaccG[:, 0:tiles, :],
                  buf=f"gstG{k}", reads=["gaccG"], stream="gsG")
        units.append(store)

    for k in range(NCHUNK):
        add_down_job(k)
        for mat_i in range(2):
            for tr in range(2):
                add_dve_job(mat_i, k, tr)

        def ccs(k=k):
            emit_cc(f"d{k}", wd_in[k], wd_out[k], [f"gstG{k}"])
            emit_cc(f"g{k}", wg_in[k], wg_out[k], [f"gst_g0_{k}_0", f"gst_g0_{k}_1"])
            emit_cc(f"u{k}", wu_in[k], wu_out[k], [f"gst_g1_{k}_0", f"gst_g1_{k}_1"])
        units.append(ccs)

    _pu = {"i": 0}

    def pump(n):
        done = 0
        while done < n and _pu["i"] < len(units):
            units[_pu["i"]]()
            _pu["i"] += 1
            done += 1

    def pump_all():
        pump(len(units))

    f_units = []
    wgu_str2 = [p.stream("w2"), p.stream("w2")]
    hst_str = [p.stream("hs"), p.stream("hs")]

    def add_f_unit(k, it_l, sh):
        it = CHUNK_OFF[k] + it_l
        xcol = HALO + 512 * sh
        sl = it % 2

        def u():
            gu_dep = [cc_toks[f"g{k}"], cc_toks[f"u{k}"]]
            p.dma("scalar", wgu_ring[:, 0, :, :],
                  wg_out[k][:, 128 * it_l:128 * (it_l + 1)].rearrange("(kt p) m -> p kt m", p=128),
                  buf="wgug", deps=gu_dep, stream=wgu_str2[0])
            p.dma("scalar", wgu_ring[:, 1, :, :],
                  wu_out[k][:, 128 * it_l:128 * (it_l + 1)].rearrange("(kt p) m -> p kt m", p=128),
                  buf="wguu", deps=gu_dep, stream=wgu_str2[1])
            p.mm_group(PB[4], [(pb[4][:, :], wgu_ring[:, 0, kt, :],
                                xnT[:, kt, xcol:xcol + 512]) for kt in range(16)],
                       reads=["wgug"] + XN)
            p.mm_group(PB[5], [(pb[5][:, :], wgu_ring[:, 1, kt, :],
                                xnT[:, kt, xcol:xcol + 512]) for kt in range(16)],
                       reads=["wguu"] + XN)
            p.emit("scalar", lambda e: e.activation(gsil2[:, :], pb[4][:, :], AF.Silu),
                   reads=[PB[4]], writes=["gsil2"])
            p.emit("vector", lambda e, sl=sl: e.tensor_tensor(
                hint_sb[:, sl, :], gsil2[:, :], pb[5][:, :], ALU.mult),
                reads=["gsil2", PB[5]], writes=[f"hsb{sl}"])
            p.dma("sync", hint_dram[128 * it:128 * (it + 1), 512 * sh:512 * (sh + 1)],
                  hint_sb[:, sl, :], buf=f"hd_{it}_{sh}", reads=[f"hsb{sl}"],
                  stream=hst_str[sl])
        f_units.append(u)

    for k in range(NCHUNK):
        for it_l in range(CHUNK_TILES[k]):
            for sh in range(2):
                add_f_unit(k, it_l, sh)

    _fu = {"i": 0}

    def fpump(n):
        done = 0
        while done < n and _fu["i"] < len(f_units):
            f_units[_fu["i"]]()
            _fu["i"] += 1
            done += 1

    def fpump_all():
        fpump(len(f_units))

    cmp_st = contextlib.ExitStack()
    cmp_c = sb(cmp_st, "cmp_c", (128, TC), F32)
    cmp_z = sb(cmp_st, "cmp_z", (128, TC), F32)
    cmp_s1 = sb(cmp_st, "cmp_s1", (128, NBC), F32)
    cmp_s2 = sb(cmp_st, "cmp_s2", (128, NBC), F32)
    kv_own = sb(cmp_st, "kv_own", (128, NBC), F32)
    ic_own = sb(cmp_st, "ic_own", (64, NBC), F32)

    def compress(wsb_c, buf_c, wsb_z, buf_z, bias_sb, bias_buf, pdim, out_sb, out_buf):
        for which, (wsb, wbuf, dst, dbuf) in enumerate(
                [(wsb_c, buf_c, cmp_c, "cmp_c"), (wsb_z, buf_z, cmp_z, "cmp_z")]):
            for nch in range(2):
                bank = nch + 2 * which
                p.mm_group(PB[bank],
                           [(pb[bank][0:pdim, :], wsb[:, kt, :],
                             xnT[:, kt, HALO + 512 * nch:HALO + 512 * (nch + 1)])
                            for kt in range(16)],
                           reads=[wbuf] + XN)
                p.emit("scalar", lambda e, b=bank, d=dst, nch=nch, pdim=pdim: e.activation(
                    d[0:pdim, 512 * nch:512 * (nch + 1)], pb[b][0:pdim, :], AF.Identity),
                    reads=[PB[bank]], writes=[dbuf])
        zv = cmp_z[0:pdim, :].rearrange("p (a b) -> p a b", b=M)
        cv = cmp_c[0:pdim, :].rearrange("p (a b) -> p a b", b=M)
        p.emit("vector", lambda e: e.tensor_tensor(
            zv, zv, bias_sb[0:pdim, :].unsqueeze(1).to_broadcast([pdim, NBC, M]), ALU.add),
            reads=["cmp_z", bias_buf], writes=["cmp_z"])
        p.emit("vector", lambda e: e.tensor_reduce(
            cmp_s1[0:pdim, :], zv, axis=AXL.X, op=ALU.max),
            reads=["cmp_z"], writes=["cmp_s1"])
        p.emit("vector", lambda e: e.tensor_tensor(
            zv, zv, cmp_s1[0:pdim, :].unsqueeze(2).to_broadcast([pdim, NBC, M]), ALU.subtract),
            reads=["cmp_z", "cmp_s1"], writes=["cmp_z"])
        p.emit("scalar", lambda e: e.activation(zv, zv, AF.Exp),
               reads=["cmp_z"], writes=["cmp_z"])
        p.emit("vector", lambda e: e.tensor_reduce(
            cmp_s1[0:pdim, :], zv, axis=AXL.X, op=ALU.add),
            reads=["cmp_z"], writes=["cmp_s1"])
        p.emit("vector", lambda e: e.reciprocal(cmp_s2[0:pdim, :], cmp_s1[0:pdim, :]),
               reads=["cmp_s1"], writes=["cmp_s2"])
        p.emit("vector", lambda e: e.tensor_tensor(zv, zv, cv, ALU.mult),
               reads=["cmp_z", "cmp_c"], writes=["cmp_z"])
        p.emit("vector", lambda e: e.tensor_reduce(
            cmp_s1[0:pdim, :], zv, axis=AXL.X, op=ALU.add),
            reads=["cmp_z"], writes=["cmp_s1"])
        p.emit("vector", lambda e: e.tensor_tensor(
            out_sb, cmp_s1[0:pdim, :], cmp_s2[0:pdim, :], ALU.mult),
            reads=["cmp_s1", "cmp_s2"], writes=[out_buf])

    compress(s_kvcw, "s_kvcw", s_kvcwz, "s_kvcwz", s_kvcbT, "s_kvcbT", 128, kv_own[:, :], "kv_own")
    compress(s_idxw, "s_idxw", s_idxwz, "s_idxwz", s_idxbT, "s_idxbT", 64, ic_own[0:64, :], "ic_own")
    p.emit("tensor", lambda e: e.transpose(pb[4][0:64, 0:128], kv_own[:, :], s_identF[:, :]),
           reads=["kv_own", "s_identF"], writes=[PB[4]])
    p.emit("vector", lambda e: e.tensor_copy(cc_sb[:, 0:HD], pb[4][0:64, 0:128]),
           reads=[PB[4]], writes=["cc_sb"])
    p.emit("tensor", lambda e: e.transpose(pb[4][0:64, 0:64], ic_own[0:64, :], s_identF[0:64, 0:64]),
           reads=["ic_own", "s_identF"], writes=[PB[4]])
    p.emit("vector", lambda e: e.tensor_copy(cc_sb[:, HD:HD + CI], pb[4][0:64, 0:64]),
           reads=[PB[4]], writes=["cc_sb"])
    p.dma("sync", cc_cmp_in[:, :], cc_sb[:, :], buf="cc_cmp_st", reads=["cc_sb"], stream="cls", shared=True)
    emit_cc("cmp", cc_cmp_in, cc_cmp_out, ["cc_cmp_st"])
    cmp_st.close()
    _joinC = p.join_tokens(["cmp_c", "cmp_z", "cmp_s1", "cmp_s2", "kv_own", "ic_own",
                            "kv_sc", "kv_ms", "kct"])
    p.seed(["hsr", "isc", "iwork0", "iwork1", "mx8", "vstar", "selbin",
            "cmA0", "cmA1", "cmB0", "cmB1", "s_epstie"]
           + [f"iscA{tt}" for tt in range(8)], _joinC)

    p.dma("gpsimd", wqup_bf[:, :], w_qup[:, :], buf="wqup_bf", stream="glq", shared=True)
    pump(40)


    for si, (o, n) in enumerate(SEGS):
        bank = si % 2
        p.mm_group(PB[bank], [(pb[bank][:, 0:n], s_kproj[:, kt, :], xnT[:, kt, o:o + n])
                              for kt in range(16)], reads=["s_kproj"] + XN)
        p.emit("scalar", lambda e, b=bank, o=o, n=n: e.activation(
            kTt[:, o:o + n], pb[b][:, 0:n], AF.Identity),
            reads=[PB[bank]], writes=["kT"])
        p.emit("scalar", lambda e, o=o, n=n: e.activation(
            k_sc[:, o:o + n], kTt[:, o:o + n], AF.Square),
            reads=["kT"], writes=["k_sc"])
        p.mm_group(PB[2 + bank], [(pb[2 + bank][0:1, 0:n], s_onesB[:, :], k_sc[:, o:o + n])],
                   reads=["s_onesB", "k_sc"])
        p.emit("scalar", lambda e, b=2 + bank, n=n: e.activation(
            pb[b][0:1, 0:n], pb[b][0:1, 0:n], AF.Ln, bias=s_epsc[0:1, 1:2], scale=1.0 / HD),
            reads=[PB[2 + bank], "s_epsc"], writes=[PB[2 + bank]])
        p.emit("scalar", lambda e, b=2 + bank, n=n: e.activation(
            pb[b][0:1, 0:n], pb[b][0:1, 0:n], AF.Exp, scale=-0.5),
            reads=[PB[2 + bank]], writes=[PB[2 + bank]])
        p.emit("vector", lambda e, b=2 + bank, n=n: e.tensor_copy(
            k_row[:, 0:n], pb[b][0:1, 0:n]),
            reads=[PB[2 + bank]], writes=["k_row"])
        p.mm_group(PB[4], [(pb[4][:, 0:n], s_onesRF[:, :], k_row[:, 0:n])],
                   reads=["s_onesRF", "k_row"])
        p.emit("vector", lambda e, o=o, n=n: e.scalar_tensor_tensor(
            kTt[:, o:o + n], kTt[:, o:o + n], s_knw[:, :], pb[4][:, 0:n],
            op0=ALU.mult, op1=ALU.mult),
            reads=["kT", "s_knw", PB[4]], writes=["kT"])
    p.emit("vector", lambda e: e.tensor_copy(rope_scr[0:32, :], kTt[32:64, :]),
           reads=["kT"], writes=["rope_scr"])
    p.emit("vector", lambda e: e.tensor_copy(rope_scr[32:64, :], kTt[0:32, :]),
           reads=["kT"], writes=["rope_scr"])
    p.emit("vector", lambda e: e.tensor_tensor(kTt[0:64, :], kTt[0:64, :], s_CC[:, :], ALU.mult),
           reads=["kT", "s_CC"], writes=["kT"])
    p.emit("vector", lambda e: e.tensor_tensor(rope_scr[:, :], rope_scr[:, :], s_SS[:, :], ALU.mult),
           reads=["rope_scr", "s_SS"], writes=["rope_scr"])
    p.emit("vector", lambda e: e.tensor_tensor(kTt[0:64, :], kTt[0:64, :], rope_scr[:, :], ALU.add),
           reads=["kT", "rope_scr"], writes=["kT"])
    if debug:
        d_k = dout("dbg_k", (128, TCH), BF16)
        p.dma("sync", d_k[:, :], kTt[:, :], reads=["kT"], stream="dbg", shared=True)

    for mt in range(10):
        bank = 5 + (mt % 2)
        p.mm_group(PB[bank], [(pb[bank][:, 0:HD], xnT[:, kt, 128 * mt:128 * (mt + 1)],
                               s_vproj[:, kt, :]) for kt in range(16)],
                   reads=["s_vproj"] + XN)
        p.emit("scalar", lambda e, b=bank, mt=mt: e.activation(
            vtok[:, mt, :], pb[b][:, 0:HD], AF.Identity),
            reads=[PB[bank]], writes=["vtok"])

    p.dma("sync", kvtok[:, :, 0:HD],
          cc_cmp_out[:, 0:HD].rearrange("(a p) d -> p a d", p=128),
          buf="kvtok", deps=[cc_toks["cmp"]], stream="cls", shared=True)
    p.dma("sync", kvtok[:, :, HD:HD + CI],
          cc_cmp_out[:, HD:HD + CI].rearrange("(a p) d -> p a d", p=128),
          buf="kvtok2", deps=[cc_toks["cmp"]], stream="cls", shared=True)
    kvs = contextlib.ExitStack()
    kv_sc = sb(kvs, "kv_sc", (128, 4, HD), F32)
    kv_ms = sb(kvs, "kv_ms", (128, 4, 1), F32)
    kct = sb(kvs, "kct", (128, 4, CI))
    kvv = kvtok[:, :, 0:HD]
    p.emit("vector", lambda e: e.tensor_tensor(kv_sc[:, :, :], kvv, kvv, ALU.mult),
           reads=["kvtok"], writes=["kv_sc"])
    p.emit("vector", lambda e: e.tensor_reduce(kv_ms[:, :, :], kv_sc[:, :, :], axis=AXL.X, op=ALU.add),
           reads=["kv_sc"], writes=["kv_ms"])
    p.emit("scalar", lambda e: e.activation(kv_ms[:, :, :], kv_ms[:, :, :], AF.Ln,
                                            bias=s_epsc[:, 1:2], scale=1.0 / HD),
           reads=["kv_ms", "s_epsc"], writes=["kv_ms"])
    p.emit("scalar", lambda e: e.activation(kv_ms[:, :, :], kv_ms[:, :, :], AF.Exp, scale=-0.5),
           reads=["kv_ms"], writes=["kv_ms"])
    p.emit("vector", lambda e: e.tensor_tensor(
        kv_sc[:, :, :], kvv, kv_ms[:, :, :].to_broadcast([128, 4, HD]), ALU.mult),
        reads=["kvtok", "kv_ms"], writes=["kv_sc"])
    p.emit("vector", lambda e: e.tensor_tensor(
        kvnrm[:, :, :], kv_sc[:, :, :],
        s_knwb[:, :].unsqueeze(1).to_broadcast([128, 4, HD]), ALU.mult),
        reads=["kv_sc", "s_knwb"], writes=["kvnrm"])
    for a in range(4):
        p.emit("tensor", lambda e, a=a: e.transpose(pbT[:, 0:128], kvnrm[:, a, :], s_identB[:, :]),
               reads=["kvnrm", "s_identB"], writes=["pbT"])
        p.emit("vector", lambda e, a=a: e.tensor_copy(kvnrmT[:, 128 * a:128 * (a + 1)], pbT[:, 0:128]),
               reads=["pbT"], writes=["kvnrmT"])
    p.emit("vector", lambda e: e.tensor_copy(kct[:, :, :], kvtok[:, :, HD:HD + CI]),
           reads=["kvtok2"], writes=["kct"])
    for a in range(4):
        p.emit("tensor", lambda e, a=a: e.transpose(pbT[0:64, 0:128], kct[:, a, :], s_identB[:, :]),
               reads=["kct", "s_identB"], writes=["pbT"])
        p.emit("vector", lambda e, a=a: e.tensor_copy(kcompT[0:64, 128 * a:128 * (a + 1)], pbT[0:64, 0:128]),
               reads=["pbT"], writes=["kcompT"])
    p.dma("sync", kcompT[64:128, :], kcompT[0:64, :], buf="kcompT_hi", reads=["kcompT"],
          stream="cls", shared=True)
    if debug:
        d_kvn = dout("dbg_kvn", (128, 4, HD), BF16)
        d_kc = dout("dbg_kc", (128, 4, CI), F32)
        p.dma("sync", d_kvn[:, :, :], kvnrm[:, :, :], reads=["kvnrm"], stream="dbg", shared=True)
        p.dma("sync", d_kc[:, :, :], kvtok[:, :, HD:HD + CI], reads=["kvtok2"], stream="dbg", shared=True)
    kvs.close()

    for ci in range(4):
        p.dma("gpsimd", fw_bf[512 * ci:512 * (ci + 1), :], final_w[512 * ci:512 * (ci + 1), :],
              buf="fw_bf", stream="glf", shared=True)
    pump(40)

    for mt in range(4):
        for ncl in range(2):
            bank = ncl
            p.mm_group(PB[bank], [(pb[bank][:, :], s_wqc[:, kt, 128 * mt:128 * (mt + 1)],
                                   xnT[:, kt, HALO + 512 * ncl:HALO + 512 * (ncl + 1)])
                                  for kt in range(16)], reads=["s_wqc"] + XN)
            p.emit("scalar", lambda e, b=bank, mt=mt, ncl=ncl: e.activation(
                AT[:, mt, 512 * ncl:512 * (ncl + 1)], pb[b][:, :], AF.Identity),
                reads=[PB[bank]], writes=["AT"])
    for ncl in range(2):
        bank = 2 + ncl
        p.mm_group(PB[bank], [(pb[bank][0:64, :], s_wdq[:, kt, :],
                               xnT[:, kt, HALO + 512 * ncl:HALO + 512 * (ncl + 1)])
                              for kt in range(16)], reads=["s_wdq"] + XN)
        p.emit("scalar", lambda e, b=bank, ncl=ncl: e.activation(
            dqT[:, 512 * ncl:512 * (ncl + 1)], pb[b][0:64, :], AF.Identity),
            reads=[PB[bank]], writes=["dqT"])
    for mt in range(2):
        for ncl in range(2):
            bank = 4 + ncl
            p.mm_group(PB[bank], [(pb[bank][:, :], s_wiuq[:, 128 * mt:128 * (mt + 1)],
                                   dqT[:, 512 * ncl:512 * (ncl + 1)])],
                       reads=["s_wiuq", "dqT"])
            p.emit("scalar", lambda e, b=bank, mt=mt, ncl=ncl: e.activation(
                qiT[:, mt, 512 * ncl:512 * (ncl + 1)], pb[b][:, :], AF.Identity),
                reads=[PB[bank]], writes=["qiT"])
    for mt in range(8):
        p.mm_group(PB[6], [(pb[6][:, 0:NIH], xnT[:, kt, HALO + 128 * mt:HALO + 128 * (mt + 1)],
                            s_ww[:, kt, :]) for kt in range(16)],
                   reads=["s_ww"] + XN)
        p.emit("vector", lambda e, mt=mt: e.tensor_copy(wih[:, mt, :], pb[6][:, 0:NIH]),
               reads=[PB[6]], writes=["wih"])

    pump(40)

    isc_st = contextlib.ExitStack()
    hsr = sb(isc_st, "hsr", (128, NB), F32)
    isc = sb(isc_st, "isc", (128, NB), F32)
    isc_all = sb(isc_st, "isc_all", (128, 8, NB), F32)
    iwork = sb(isc_st, "iwork", (128, 2, NB), F32)
    mx8 = sb(isc_st, "mx8", (128, 8), F32)
    vstar = sb(isc_st, "vstar", (128, 1), F32)
    selbin = sb(isc_st, "selbin", (128, NB))
    mask_ring = sb(isc_st, "mask_ring", (128, 4, NB))
    s_epstie = sb(isc_st, "s_epstie", (128, NB), F32)
    p.dma("sync", s_epstie[:, :], eps_tie[:, :], buf="s_epstie", stream="cls", shared=True)
    d_isc = dout("dbg_isc", (8, 128, NB), F32) if debug else None
    d_sel = dout("dbg_sel", (8, 128, NB), BF16) if debug else None

    cm_str = [p.stream("cm") for _ in range(4)]
    for tt in range(8):
        for ih in range(NIH):
            bank = ih % 2
            base = 64 * (ih % 2)
            p.mm_group(PB[bank], [(pb[bank][:, 0:512],
                                   qiT[base:base + 64, ih // 2, 128 * tt:128 * (tt + 1)],
                                   kcompT[base:base + 64, 0:512])],
                       reads=["qiT", "kcompT" if ih % 2 == 0 else "kcompT_hi"])
            p.emit("scalar", lambda e, b=bank: e.activation(hsr[:, :], pb[b][:, 0:512], AF.Relu),
                   reads=[PB[bank]], writes=["hsr"])
            if ih == 0:
                p.emit("vector", lambda e, tt=tt, ih=ih: e.tensor_scalar(
                    isc[:, :], hsr[:, :], wih[:, tt, ih:ih + 1], None, op0=ALU.mult),
                    reads=["hsr", "wih"], writes=["isc"])
            elif ih < NIH - 1:
                p.emit("vector", lambda e, tt=tt, ih=ih: e.scalar_tensor_tensor(
                    isc[:, :], hsr[:, :], wih[:, tt, ih:ih + 1], isc[:, :],
                    op0=ALU.mult, op1=ALU.add),
                    reads=["hsr", "wih", "isc"], writes=["isc"])
            else:
                p.emit("vector", lambda e, tt=tt, ih=ih: e.scalar_tensor_tensor(
                    isc_all[:, tt, :], hsr[:, :], wih[:, tt, ih:ih + 1], isc[:, :],
                    op0=ALU.mult, op1=ALU.add),
                    reads=["hsr", "wih", "isc"], writes=[f"iscA{tt}"])
        pump(6)
        fpump(2)
    for tt in range(8):
        sl = tt % 2
        p.dma("sync", mask_ring[:, sl, :], causal_add[tt, :, :], buf=f"cmA{sl}",
              stream=cm_str[sl])
        p.dma("sync", mask_ring[:, 2 + sl, :], causal_bin[tt, :, :], buf=f"cmB{sl}",
              stream=cm_str[2 + sl])
        p.emit("vector", lambda e, tt=tt: e.tensor_scalar(
            iwork[:, 1, :], isc_all[:, tt, :], 0.0, None, op0=ALU.is_equal),
            reads=[f"iscA{tt}"], writes=["iwork1"])
        p.emit("vector", lambda e: e.tensor_tensor(
            iwork[:, 1, :], iwork[:, 1, :], s_epstie[:, :], ALU.mult),
            reads=["iwork1", "s_epstie"], writes=["iwork1"])
        p.emit("vector", lambda e, tt=tt: e.tensor_tensor(
            isc[:, :], isc_all[:, tt, :], iwork[:, 1, :], ALU.add),
            reads=[f"iscA{tt}", "iwork1"], writes=["isc"])
        p.emit("vector", lambda e, sl=sl: e.tensor_tensor(
            isc[:, :], isc[:, :], mask_ring[:, sl, :], ALU.add),
            reads=["isc", f"cmA{sl}"], writes=["isc"])
        if debug:
            p.dma("sync", d_isc[tt, :, :], isc[:, :], reads=["isc"], stream="dbg", shared=True)
        p.emit("vector", lambda e: e.tensor_copy(iwork[:, 0, :], isc[:, :]),
               reads=["isc"], writes=["iwork0"])
        for r in range(8):
            p.emit("vector", lambda e: e.max(out=mx8[:, :], in_=iwork[:, 0, :]),
                   reads=["iwork0"], writes=["mx8"])
            if r < 7:
                p.emit("vector", lambda e: e.match_replace(
                    out=iwork[:, 1, :], in_to_replace=mx8[:, :], in_values=iwork[:, 0, :],
                    imm_value=-1e38),
                    reads=["mx8", "iwork0"], writes=["iwork1"])
                p.emit("vector", lambda e: e.tensor_copy(iwork[:, 0, :], iwork[:, 1, :]),
                       reads=["iwork1"], writes=["iwork0"])
        p.emit("vector", lambda e: e.tensor_reduce(vstar[:, :], mx8[:, :], axis=AXL.X, op=ALU.min),
               reads=["mx8"], writes=["vstar"])
        p.emit("vector", lambda e: e.tensor_scalar(
            selbin[:, :], isc[:, :], vstar[:, :], None, op0=ALU.is_ge),
            reads=["isc", "vstar"], writes=["selbin"])
        p.emit("vector", lambda e, sl=sl: e.tensor_tensor(
            selbin[:, :], selbin[:, :], mask_ring[:, 2 + sl, :], ALU.mult),
            reads=["selbin", f"cmB{sl}"], writes=["selbin"])
        if debug:
            p.dma("sync", d_sel[tt, :, :], selbin[:, :], reads=["selbin"], stream="dbg", shared=True)
        for nt in range(4):
            p.emit("tensor", lambda e, nt=nt: e.transpose(
                pbT[:, 0:128], selbin[:, 128 * nt:128 * (nt + 1)], s_identB[:, :]),
                reads=["selbin", "s_identB"], writes=["pbT"])
            p.emit("vector", lambda e, nt=nt, tt=tt: e.tensor_copy(
                selbinT[:, nt, 128 * tt:128 * (tt + 1)], pbT[:, 0:128]),
                reads=["pbT"], writes=["selbinT"])
        pump(6)
        fpump(3)
    isc_st.close()
    p34_st.close()
    _join = p.join_tokens(
        ["hsr", "isc", "iwork0", "iwork1", "mx8", "vstar", "selbin"]
        + [f"iscA{tt}" for tt in range(8)] +
        ["placeholder_unused",
         "cmA0", "cmA1", "cmB0", "cmB1", "s_epstie",
         "s_wqc", "k_sc", "k_row", "dqT", "qiT", "wih"])
    p.seed(["qh0", "qh1", "q_scr", "q_row", "Esw0", "Esw1", "Esp0", "Esp1",
            "numsb0", "numsb1", "spt", "gwR", "grpt0", "grpt1",
            "nrowf0", "nrowf1", "nrow10", "nrow11", "spt0", "spt1", "fw0", "fw1", "grd0", "grd1",
            "sTt0", "sTt1"]
           + [f"attn{g}_{h}" for g in range(2) for h in range(4)], _join)

    at_st = contextlib.ExitStack()
    qT_ring = sb(at_st, "qT_ring", (128, 2, TC))
    q_scr = sb(at_st, "q_scr", (128, 512))
    q_row = sb(at_st, "q_row", (1, TC), F32)
    Esw = sb(at_st, "Esw", (128, 2, 4, WIN))
    Esp = sb(at_st, "Esp", (128, 2, 4, 512))
    numsb = sb(at_st, "numsb", (128, 768))
    spt = sb(at_st, "spt", (128, 2, 512))
    attn_g = sb(at_st, "attn_g", (128, 2, 4, TC))
    gw_ring = sb(at_st, "gw_ring", (128, 1, 4, 512))
    grp_tile = sb(at_st, "grp_tile", (128, 2, 512))
    nrowf = sb(at_st, "nrowf", (1, 1536), F32)

    wq_str = [p.stream("wq"), p.stream("wq")]
    d_q = dout("dbg_q", (128, H, TC), BF16) if debug else None
    d_attn = dout("dbg_attn", (128, H, TC), BF16) if debug else None

    def emit_qhead(h):
        sl = h % 2
        qb = f"qh{sl}"
        QB = PB[3]
        wsl = h % 2
        p.dma("scalar", wqup_ring[:, wsl, :, :],
              wqup_bf[:, 128 * h:128 * (h + 1)].rearrange("(kt p) m -> p kt m", p=128),
              buf=f"wq{wsl}", reads=["wqup_bf"], stream=wq_str[wsl])
        for ncl in range(2):
            p.mm_group(QB, [(pb[3][:, :], wqup_ring[:, wsl, kt, :],
                             AT[:, kt, 512 * ncl:512 * (ncl + 1)]) for kt in range(4)],
                       reads=[f"wq{wsl}", "AT"])
            p.emit("scalar", lambda e, ncl=ncl, sl=sl: e.activation(
                qT_ring[:, sl, 512 * ncl:512 * (ncl + 1)], pb[3][:, :], AF.Identity),
                reads=[QB], writes=[qb])
            p.emit("scalar", lambda e, sl=sl, ncl=ncl: e.activation(
                q_scr[:, :], qT_ring[:, sl, 512 * ncl:512 * (ncl + 1)], AF.Square),
                reads=[qb], writes=["q_scr"])
            p.mm_group(QB, [(pb[3][0:1, :], s_onesB[:, :], q_scr[:, :])],
                       reads=["s_onesB", "q_scr"])
            p.emit("scalar", lambda e: e.activation(
                pb[3][0:1, :], pb[3][0:1, :], AF.Ln, bias=s_epsc[0:1, 1:2], scale=1.0 / HD),
                reads=[QB, "s_epsc"], writes=[QB])
            p.emit("scalar", lambda e: e.activation(
                pb[3][0:1, :], pb[3][0:1, :], AF.Exp, scale=-0.5),
                reads=[QB], writes=[QB])
            p.emit("vector", lambda e, ncl=ncl: e.tensor_copy(
                q_row[:, 512 * ncl:512 * (ncl + 1)], pb[3][0:1, :]),
                reads=[QB], writes=["q_row"])
        p.emit("vector", lambda e, sl=sl: e.tensor_copy(rope_scr[0:32, 0:TC], qT_ring[32:64, sl, :]),
               reads=[qb], writes=["rope_scr"])
        p.emit("vector", lambda e, sl=sl: e.tensor_copy(rope_scr[32:64, 0:TC], qT_ring[0:32, sl, :]),
               reads=[qb], writes=["rope_scr"])
        p.emit("vector", lambda e, sl=sl: e.tensor_tensor(
            qT_ring[0:64, sl, :], qT_ring[0:64, sl, :], s_CC[:, HALO:], ALU.mult),
            reads=[qb, "s_CC"], writes=[qb])
        p.emit("vector", lambda e: e.tensor_tensor(
            rope_scr[:, 0:TC], rope_scr[:, 0:TC], s_SS[:, HALO:], ALU.mult),
            reads=["rope_scr", "s_SS"], writes=["rope_scr"])
        p.emit("vector", lambda e, sl=sl: e.tensor_tensor(
            qT_ring[0:64, sl, :], qT_ring[0:64, sl, :], rope_scr[:, 0:TC], ALU.add),
            reads=[qb, "rope_scr"], writes=[qb])
        for ncl in range(2):
            p.mm_group(QB, [(pb[3][:, :], s_onesRF[:, :],
                             q_row[:, 512 * ncl:512 * (ncl + 1)])],
                       reads=["s_onesRF", "q_row"])
            p.emit("vector", lambda e, ncl=ncl, sl=sl: e.scalar_tensor_tensor(
                qT_ring[:, sl, 512 * ncl:512 * (ncl + 1)],
                qT_ring[:, sl, 512 * ncl:512 * (ncl + 1)],
                s_qnw_s[:, :], pb[3][:, :], op0=ALU.mult, op1=ALU.mult),
                reads=[qb, "s_qnw_s", QB], writes=[qb])
        if debug:
            p.dma("sync", d_q[:, h, :], qT_ring[:, sl, :], reads=[qb], stream="dbg", shared=True)
        return qb

    def emit_sw(h, wb, qb):
        sl = h % 2
        g, hr = h // 4, h % 4
        gsl = g % 2
        B0, B1, B2 = (0, 1, 2)
        ebuf = f"Esw{sl}"
        for kt in range(4):
            bank = B0 + kt // 2
            off = 256 * (kt % 2)
            p.mm_group(PB[bank],
                       [(pb[bank][:, off:off + 256],
                         kTt[:, 256 * wb + 128 * kt:256 * wb + 128 * kt + 128],
                         qT_ring[:, sl, WIN * wb:WIN * (wb + 1)])],
                       reads=["kT", qb])
            p.emit("scalar", lambda e, b=bank, off=off, kt=kt, sl=sl: e.activation(
                Esw[:, sl, kt, :], pb[b][:, off:off + 256], AF.Exp, bias=s_negC[:, :]),
                reads=[PB[bank], "s_negC"], writes=[ebuf])
            p.emit("vector", lambda e, kt=kt, wb=wb, sl=sl: e.tensor_tensor(
                Esw[:, sl, kt, :], Esw[:, sl, kt, :], swm[:, 4 * wb + kt, :], ALU.mult),
                reads=[ebuf, "swm"], writes=[ebuf])
        p.mm_group(PB[B2], [(pb[B2][0:1, 0:256], s_onesB[:, :], Esw[:, sl, kt, :])
                            for kt in range(4)], reads=["s_onesB", ebuf])
        p.mm_group(PB[B1], [(pb[B1][:, 256:512], vtok[:, 2 * wb + kt, :], Esw[:, sl, kt, :])
                            for kt in range(4)], reads=["vtok", ebuf])
        p.emit("scalar", lambda e, B2=B2, sl=sl: e.activation(
            nrowf[:, 256 * sl:256 * sl + 256], pb[B2][0:1, 0:256], AF.Ln),
            reads=[PB[B2]], writes=[f"nrowf{sl}"])
        p.emit("scalar", lambda e, sl=sl: e.activation(
            nrowf[:, 256 * sl:256 * sl + 256], nrowf[:, 256 * sl:256 * sl + 256], AF.Exp, scale=-1.0),
            reads=[f"nrowf{sl}"], writes=[f"nrowf{sl}"])
        p.mm_group(PB[B2], [(pb[B2][:, 256:512], s_onesRF[:, :], nrowf[:, 256 * sl:256 * sl + 256])],
                   reads=["s_onesRF", f"nrowf{sl}"])
        p.emit("scalar", lambda e, B1=B1, sl=sl: e.activation(
            numsb[:, 256 * sl:256 * sl + 256], pb[B1][:, 256:512], AF.Identity),
            reads=[PB[B1]], writes=[f"numsb{sl}"])
        p.emit("vector", lambda e, gsl=gsl, hr=hr, wb=wb, B2=B2, sl=sl: e.tensor_tensor(
            attn_g[:, gsl, hr, WIN * wb:WIN * (wb + 1)], numsb[:, 256 * sl:256 * sl + 256],
            pb[B2][:, 256:512], ALU.mult),
            reads=[f"numsb{sl}", PB[B2]], writes=[f"attn{gsl}_{hr}"])

    def emit_sparse(h, ch, qb):
        sl = h % 2
        g, hr = h // 4, h % 4
        gsl = g % 2
        B0, B1, B2 = (0, 1, 2)
        ebuf = f"Esp{sl}"
        for nt in range(4):
            bank = B0 + nt % 2
            p.mm_group(PB[bank], [(pb[bank][:, :], kvnrmT[:, 128 * nt:128 * (nt + 1)],
                                   qT_ring[:, sl, 512 * ch:512 * (ch + 1)])],
                       reads=["kvnrmT", qb])
            p.emit("scalar", lambda e, b=bank, nt=nt, sl=sl: e.activation(
                Esp[:, sl, nt, :], pb[b][:, :], AF.Exp, bias=s_negC[:, :]),
                reads=[PB[bank], "s_negC"], writes=[ebuf])
            p.emit("vector", lambda e, nt=nt, ch=ch, sl=sl: e.tensor_tensor(
                Esp[:, sl, nt, :], Esp[:, sl, nt, :],
                selbinT[:, nt, 512 * ch:512 * (ch + 1)], ALU.mult),
                reads=[ebuf, "selbinT"], writes=[ebuf])
        p.mm_group(PB[B2], [(pb[B2][0:1, :], s_onesB[:, :], Esp[:, sl, nt, :]) for nt in range(4)],
                   reads=["s_onesB", ebuf])
        p.mm_group(PB[B0], [(pb[B0][:, :], kvnrm[:, nt, :], Esp[:, sl, nt, :]) for nt in range(4)],
                   reads=["kvnrm", ebuf])
        p.emit("vector", lambda e, B2=B2, sl=sl: e.tensor_scalar(
            nrowf[:, 512 + 256 * sl:512 + 256 * sl + 256]
            if False else nrowf[:, 512 + 384 * sl:512 + 384 * sl + 384][:, 0:512 - 0][:, 0:512][:, 0:512],
            pb[B2][0:1, :], 1e-30, None, op0=ALU.add),
            reads=[PB[B2]], writes=[f"nrow1{sl}"]) if False else None
        off = 512 + 512 * sl
        p.emit("vector", lambda e, B2=B2, off=off: e.tensor_scalar(
            nrowf[:, off:off + 512], pb[B2][0:1, :], 1e-30, None, op0=ALU.add),
            reads=[PB[B2]], writes=[f"nrow1{sl}"])
        p.emit("scalar", lambda e, off=off: e.activation(nrowf[:, off:off + 512], nrowf[:, off:off + 512], AF.Ln),
               reads=[f"nrow1{sl}"], writes=[f"nrow1{sl}"])
        p.emit("scalar", lambda e, off=off: e.activation(nrowf[:, off:off + 512], nrowf[:, off:off + 512], AF.Exp, scale=-1.0),
               reads=[f"nrow1{sl}"], writes=[f"nrow1{sl}"])
        p.mm_group(PB[B2], [(pb[B2][:, :], s_onesRF[:, :], nrowf[:, off:off + 512])],
                   reads=["s_onesRF", f"nrow1{sl}"])
        p.emit("scalar", lambda e, B0=B0, sl=sl: e.activation(
            numsb[:, 512 + 512 * 0:1024] if False else spt[:, :], pb[B0][:, :], AF.Identity),
            reads=[PB[B0]], writes=[f"spt{sl}"]) if False else None
        p.emit("scalar", lambda e, B0=B0, sl=sl: e.activation(
            spt[:, sl, :], pb[B0][:, :], AF.Identity),
            reads=[PB[B0]], writes=[f"spt{sl}"])
        p.emit("vector", lambda e, B2=B2, sl=sl: e.tensor_tensor(
            spt[:, sl, :], spt[:, sl, :], pb[B2][:, :], ALU.mult),
            reads=[f"spt{sl}", PB[B2]], writes=[f"spt{sl}"])
        p.emit("vector", lambda e, gsl=gsl, hr=hr, ch=ch, sl=sl: e.tensor_tensor(
            attn_g[:, gsl, hr, 512 * ch:512 * (ch + 1)],
            attn_g[:, gsl, hr, 512 * ch:512 * (ch + 1)], spt[:, sl, :], ALU.add),
            reads=[f"spt{sl}", f"attn{gsl}_{hr}"], writes=[f"attn{gsl}_{hr}"])

    gw_str = [p.stream("gw")]
    gd_str = [p.stream("gd"), p.stream("gd")]

    def emit_group(g):
        gsl = g % 2
        gwsl = 0
        deps = [f"attn{gsl}_{hr}" for hr in range(4)]
        p.dma("gpsimd", gw_ring[:, 0, :, :],
              group_w[g, :, :].rearrange("(kt p) m -> p kt m", p=128),
              buf="gwR", stream=gw_str[0])
        for mt in range(4):
            for ncl in range(2):
                bank = ncl
                p.mm_group(PB[bank], [(pb[bank][:, :], gw_ring[:, 0, kt, 128 * mt:128 * (mt + 1)],
                                       attn_g[:, gsl, kt, 512 * ncl:512 * (ncl + 1)])
                                      for kt in range(4)],
                           reads=["gwR"] + deps)
                gslot = (mt + ncl) % 2
                p.emit("scalar", lambda e, b=bank, gslot=gslot: e.activation(
                    grp_tile[:, gslot, :], pb[b][:, :], AF.Identity),
                    reads=[PB[bank]], writes=[f"grpt{gslot}"])
                p.dma("sync", grp_dram[128 * (4 * g + mt):128 * (4 * g + mt + 1),
                                       512 * ncl:512 * (ncl + 1)],
                      grp_tile[:, gslot, :], buf=f"grouped{g}",
                      reads=[f"grpt{gslot}"], stream=gd_str[gslot])

    for h in range(H):
        qb = emit_qhead(h)
        for wb in range(4):
            emit_sw(h, wb, qb)
        for ch in range(2):
            emit_sparse(h, ch, qb)
        if debug:
            gsl, hr = (h // 4) % 2, h % 4
            p.dma("sync", d_attn[:, h, :], attn_g[:, gsl, hr, :], reads=[f"attn{gsl}_{hr}"], stream="dbg", shared=True)
        if h % 4 == 3:
            emit_group(h // 4)
        pump(30)
        fpump(4 if h % 8 < 3 else 3)

    fpump_all()

    if debug:
        d_hint = dout("dbg_hint", (128, NIT, 512), BF16)
        hdeps = []
        for it in range(NIT):
            hdeps.extend(p.acc_deps(f"hd_{it}_1"))
        p.dma("sync", d_hint[:, :, :],
              hint_dram[:, 512:1024].rearrange("(it p) m -> p it m", p=128),
              deps=hdeps, stream="dbg", shared=True)

    at_st.close()
    sel_st.close()
    gw_st.close()
    qc_st.close()


    pump_all()
    _join2 = p.join_tokens(
        ["qh0", "qh1", "q_scr", "q_row", "Esw0", "Esw1", "Esp0", "Esp1",
         "numsb0", "numsb1", "spt", "gwR", "grpt0", "grpt1",
         "nrowf0", "nrowf1", "nrow10", "nrow11", "spt0", "spt1", "fw0", "fw1", "grd0", "grd1",
         "sTt0", "sTt1", "selbinT", "swm", "s_wqup", "AT",
         "wgug", "wguu", "gsil2", "hsb0", "hsb1"]
        + [f"attn{g}_{h}" for g in range(2) for h in range(4)])
    p.seed(["wd00", "wd01", "wd10", "wd11", "gsil_0", "gsil_1", "sg0", "sg1",
            "sTr0", "sTr1", "xf0", "xf1", "o0", "o1",
            "hin0", "hin1", "hin2", "hin3"], _join2)

    p.seed([f"fw{i}" for i in range(6)] + [f"grd{i}" for i in range(6)]
           + ["sTt0", "sTt1"], _join2)

    fin_st = contextlib.ExitStack()
    fw_ring = sb(fin_st, "fw_ring", (128, 6, 1024))
    grd_ring = sb(fin_st, "grd_ring", (128, 6, 512))
    sT_tile = sb(fin_st, "sT_tile", (128, 2, 512))
    fw_str = [p.stream("fw") for _ in range(6)]
    gr_str = [p.stream("gr") for _ in range(6)]
    st_str = [p.stream("st"), p.stream("st")]
    for sc in range(2):
        for pass_ in range(2):
            o0 = 8 * pass_
            for kt in range(16):
                sl = kt % 6
                fsl = kt % 6
                p.dma("scalar", fw_ring[:, fsl, :],
                      fw_bf[128 * kt:128 * (kt + 1), 128 * o0:128 * (o0 + 8)],
                      buf=f"fw{fsl}", reads=["fw_bf"], stream=fw_str[fsl])
                p.dma("sync", grd_ring[:, sl, :],
                      grp_dram[128 * kt:128 * (kt + 1), 512 * sc:512 * (sc + 1)],
                      buf=f"grd{sl}", reads=[f"grouped{kt // 4}"], stream=gr_str[sl])
                for oi in range(8):
                    bname = PB[oi]
                    dep = p.acc_deps(bname) if kt == 0 else []
                    tok = p.emit("tensor",
                                 lambda e, oi=oi, sl=sl, fsl=fsl,
                                 st=(kt == 0), sp=(kt == 15):
                                 e.matmul(pb[oi][:, :], fw_ring[:, fsl, 128 * oi:128 * (oi + 1)],
                                          grd_ring[:, sl, :],
                                          start=st, stop=sp),
                                 reads=[f"fw{fsl}", f"grd{sl}"],
                                 deps=dep)
                    if kt == 15:
                        p.set_write(bname, tok)
            for oi in range(8):
                ot = o0 + oi
                bname = PB[oi]
                sl2 = oi % 2
                p.emit("scalar", lambda e, oi=oi, sl2=sl2: e.activation(
                    sT_tile[:, sl2, :], pb[oi][:, :], AF.Identity),
                    reads=[bname], writes=[f"sTt{sl2}"])
                p.dma("sync", sT_dram[128 * ot:128 * (ot + 1), 512 * sc:512 * (sc + 1)],
                      sT_tile[:, sl2, :], buf=f"sTd_{ot}_{sc}", reads=[f"sTt{sl2}"],
                      stream=st_str[sl2])
    if debug:
        d_sT = dout("dbg_sT", (D, TC), BF16)
        for ot in range(16):
            for sc in range(2):
                p.dma("sync", d_sT[128 * ot:128 * (ot + 1), 512 * sc:512 * (sc + 1)],
                      sT_dram[128 * ot:128 * (ot + 1), 512 * sc:512 * (sc + 1)],
                      deps=p.acc_deps(f"sTd_{ot}_{sc}"), stream="dbg", shared=True)
    fin_st.close()
    ffn_st = contextlib.ExitStack()
    wd_ring = sb(ffn_st, "wd_ring", (128, 2, 2, 11, 256))
    hin_ring = sb(ffn_st, "hin_ring", (128, 4, 1024))
    gsil = sb(ffn_st, "gsil", (128, 2, 512))
    sg_t = sb(ffn_st, "sg_t", (128, 2, 512))
    sTr = sb(ffn_st, "sTr", (128, 8, 512))
    xf_ring = sb(ffn_st, "xf_ring", (128, 8, 512), F32)
    o_ring = sb(ffn_st, "o_ring", (128, 2, 512), F32)

    wd_str = {(kp, ri): p.stream("wd") for kp in range(2) for ri in range(2)}
    hin_str = [p.stream("hi") for _ in range(4)]
    sTr_str = [p.stream("sr") for _ in range(8)]
    xf_str = [p.stream("x2") for _ in range(8)]
    out_str = [p.stream("ou"), p.stream("ou")]
    out_toks = []

    _join2b = p.join_tokens([f"fw{i}" for i in range(6)] + [f"grd{i}" for i in range(6)]
                        + ["sTt0", "sTt1"])
    p.seed(["wd00", "wd01", "wd10", "wd11", "gsil_0", "gsil_1", "sg0", "sg1",
            "o0", "o1", "hin0", "hin1", "hin2", "hin3"]
           + [f"sTr{i}" for i in range(8)] + [f"xf{i}" for i in range(8)], _join2b)

    for og4 in range(4):
        ranks = (2 * og4, 2 * og4 + 1)
        first_mm = [True] * 8
        for sh in range(2):
            xcol = HALO + 512 * sh
            for ri in range(2):
                for half in range(2):
                    oi = 4 * sh + 2 * ri + half
                    ot = 4 * og4 + 2 * ri + half
                    p.dma("sync", sTr[:, oi, :],
                          sT_dram[128 * ot:128 * (ot + 1), 512 * sh:512 * (sh + 1)],
                          buf=f"sTr{oi}", deps=p.acc_deps(f"sTd_{ot}_{sh}"), stream=sTr_str[oi])
                    p.dma("sync", xf_ring[:, oi, :], xT[128 * ot:128 * (ot + 1), xcol:xcol + 512],
                          buf=f"xf{oi}", stream=xf_str[oi])
        for k in range(NCHUNK):
            kp = k % 2
            tiles = CHUNK_TILES[k]
            for ri, r in enumerate(ranks):
                p.dma("scalar" if ri == 0 else "sync", wd_ring[:, kp, ri, 0:tiles, :],
                      wd_out[k][r * tiles * 128:(r + 1) * tiles * 128, :]
                      .rearrange("(it p) m -> p it m", p=128),
                      buf=f"wd{kp}{ri}", deps=[cc_toks[f"d{k}"]], stream=wd_str[(kp, ri)])
            for it_l in range(tiles):
                it = CHUNK_OFF[k] + it_l
                hsl = it % 4
                p.dma("scalar", hin_ring[:, hsl, :],
                      hint_dram[128 * it:128 * (it + 1), :],
                      buf=f"hin{hsl}",
                      deps=p.acc_deps(f"hd_{it}_0") + p.acc_deps(f"hd_{it}_1"),
                      stream=hin_str[hsl])
                last = (k == NCHUNK - 1) and (it_l == tiles - 1)
                for sh in range(2):
                    for ri in range(2):
                        for half in range(2):
                            oi = 4 * sh + 2 * ri + half
                            bname = PB[oi]
                            st = first_mm[oi]
                            dep = p.acc_deps(bname) if st else []
                            if st:
                                first_mm[oi] = False
                            tok = p.emit("tensor",
                                         lambda e, oi=oi, kp=kp, ri=ri, it_l=it_l, hsl=hsl,
                                         half=half, sh=sh, st=st, sp=last:
                                         e.matmul(pb[oi][:, :],
                                                  wd_ring[:, kp, ri, it_l, 128 * half:128 * (half + 1)],
                                                  hin_ring[:, hsl, 512 * sh:512 * (sh + 1)],
                                                  start=st, stop=sp),
                                         reads=[f"wd{kp}{ri}", f"hin{hsl}"],
                                         deps=dep)
                            if last:
                                p.set_write(bname, tok)
        for sh in range(2):
            for ri in range(2):
                for half in range(2):
                    oi = 4 * sh + 2 * ri + half
                    ot = 4 * og4 + 2 * ri + half
                    sl2 = oi % 2
                    p.emit("scalar", lambda e, oi=oi, sl2=sl2: e.activation(
                        gsil[:, sl2, :], pb[oi][:, :], AF.Silu),
                        reads=[PB[oi]], writes=[f"gsil_{sl2}"])
                    p.emit("vector", lambda e, oi=oi, sl2=sl2: e.tensor_tensor(
                        sg_t[:, sl2, :], gsil[:, sl2, :], sTr[:, oi, :], ALU.mult),
                        reads=[f"gsil_{sl2}", f"sTr{oi}"], writes=[f"sg{sl2}"])
                    p.emit("vector", lambda e, oi=oi, sl2=sl2: e.tensor_tensor(
                        o_ring[:, sl2, :], xf_ring[:, oi, :], sg_t[:, sl2, :], ALU.add),
                        reads=[f"xf{oi}", f"sg{sl2}"], writes=[f"o{sl2}"])
                    tok = p.dma("sync", outT[128 * ot:128 * (ot + 1), 512 * sh:512 * (sh + 1)],
                                o_ring[:, sl2, :], buf=f"od_{ot}_{sh}", reads=[f"o{sl2}"],
                                stream=out_str[sl2])
                    out_toks.append(tok)

    final_deps = list(out_toks)
    if "dbg" in p.streams:
        final_deps.append(("d_dbg", p.streams["dbg"]))
    p.emit("gpsimd", lambda e: e.nop(), deps=final_deps)
    ffn_st.close()

    p.build()
    top.close()
    return nc, dbg



def _host_inputs(full):
    x = np.asarray(full["L0_attn_h"])[0]
    T = x.shape[0]
    inv = (1.0 / (10000.0 ** (np.arange(0, 64, 2, dtype=np.float64) / 64.0)))
    gate_idxT = np.ascontiguousarray(np.asarray(full["L0_ffn_gate_idx"]).T)
    up_idxT = np.ascontiguousarray(np.asarray(full["L0_ffn_up_idx"]).T)
    down_idxT = np.ascontiguousarray(np.asarray(full["L0_ffn_down_idx"]).T)
    ones_colB = np.ones((128, 1), dtype=BF)
    ones_row_f = np.ones((1, 128), dtype=np.float32)
    identB = np.eye(128, dtype=BF)
    identF = np.eye(128, dtype=np.float32)
    eps_tie = np.tile((-(np.arange(NB, dtype=np.float64) + 1.0) * 1e-25).astype(np.float32)[None, :],
                      (128, 1))
    kvc_biasT = np.ascontiguousarray(np.asarray(full["L0_attn_kvc_bias"]).T).astype(np.float32)
    idx_biasT = np.ascontiguousarray(np.asarray(full["L0_attn_idx_c_bias"]).T).astype(np.float32)

    in_maps = []
    for c in range(NCORE):
        t0 = TC * c
        gpos = np.arange(t0 - HALO, t0 + TC)
        xa = np.zeros((TCH, D), dtype=np.float32)
        lo = max(0, t0 - HALO)
        xa[HALO - (t0 - lo):] = x[lo:t0 + TC]
        ang = gpos[:, None].astype(np.float64) * inv[None, :]
        ropeCc = np.ascontiguousarray(np.cos(ang).T.astype(np.float32))
        ropeSc = np.ascontiguousarray(np.sin(ang).T.astype(np.float32))
        cadd = np.zeros((8, 128, NB), dtype=BF)
        cbin = np.zeros((8, 128, NB), dtype=BF)
        narr = np.arange(NB)
        for tt in range(8):
            tg = t0 + 128 * tt + np.arange(128)
            valid = (16 * narr[None, :] + 15) < tg[:, None]
            cadd[tt] = np.where(valid, np.float32(0.0), np.float32(-1e30)).astype(BF)
            cbin[tt] = valid.astype(BF)
        swm = np.zeros((4, 4, 128, WIN), dtype=BF)
        for wb in range(4):
            wbg = 4 * c + wb
            j = np.arange(512)
            tq = np.arange(WIN)
            ok = ((j[:, None] - tq[None, :]) > 0) & ((j[:, None] - tq[None, :]) <= WIN)
            if wbg == 0:
                ok &= (j[:, None] >= WIN)
            okb = ok.astype(BF)
            for kt in range(4):
                swm[wb, kt] = okb[128 * kt:128 * (kt + 1), :]
        m = {
            "xT": np.ascontiguousarray(xa.T),
            "xN": xa,
            "ropeC": ropeCc,
            "ropeS": ropeSc,
            "causal_add": cadd,
            "causal_bin": cbin,
            "sw_mask": swm,
            "eps_tie": eps_tie,
            "ones_colB": np.asarray(ones_colB),
            "ones_row_f": ones_row_f,
            "identB": np.asarray(identB),
            "identF": identF,
            "w_qc": np.asarray(full["L0_attn_w_qc"], dtype=np.float32),
            "w_qup": np.asarray(full["L0_attn_w_qup"], dtype=np.float32),
            "kvc_w": np.asarray(full["L0_attn_kvc_w"], dtype=np.float32),
            "kvc_wz": np.asarray(full["L0_attn_kvc_wz"], dtype=np.float32),
            "kvc_biasT": kvc_biasT,
            "k_proj_w": np.asarray(full["L0_attn_k_proj_w"], dtype=np.float32),
            "v_proj_w": np.asarray(full["L0_attn_v_proj_w"], dtype=np.float32),
            "idx_c_w": np.asarray(full["L0_attn_idx_c_w"], dtype=np.float32),
            "idx_c_wz": np.asarray(full["L0_attn_idx_c_wz"], dtype=np.float32),
            "idx_c_biasT": idx_biasT,
            "w_dq": np.asarray(full["L0_attn_w_dq"], dtype=np.float32),
            "w_iuq": np.asarray(full["L0_attn_w_iuq"], dtype=np.float32),
            "w_w": np.asarray(full["L0_attn_w_w"], dtype=np.float32),
            "q_norm_w": np.asarray(full["L0_attn_q_norm_w"], dtype=np.float32).reshape(HD, 1),
            "k_norm_w": np.asarray(full["L0_attn_k_norm_w"], dtype=np.float32).reshape(HD, 1),
            "group_w": np.asarray(full["L0_attn_group_w"], dtype=np.float32),
            "final_w": np.asarray(full["L0_attn_final_w"], dtype=np.float32),
            "lut": np.asarray(full["L0_ffn_lut"], dtype=np.float32).reshape(1, LUTN),
            "gidxT": np.ascontiguousarray(gate_idxT[256 * c:256 * (c + 1), :]),
            "uidxT": np.ascontiguousarray(up_idxT[256 * c:256 * (c + 1), :]),
            "didxT": np.ascontiguousarray(down_idxT[:, 256 * c:256 * (c + 1)]),
        }
        in_maps.append(m)
    return in_maps


_CACHED = {}


def kernel(**inputs):
    if "nc" not in _CACHED:
        nc, _ = build_program(debug=False)
        _CACHED["nc"] = nc
    nc = _CACHED["nc"]
    in_maps = _host_inputs(inputs)
    res = run_bass_kernel_spmd(nc, in_maps, core_ids=list(range(NCORE)))
    parts = [np.asarray(res.results[c]["outT"]).T for c in range(NCORE)]
    out = np.concatenate(parts, axis=0)[None, :, :]
    return out.astype(np.float32)


# version 17
# speedup vs baseline: 1.1238x; 1.0352x over previous
import math
import numpy as np
import ml_dtypes
import concourse.bass as bass
import concourse.mybir as mybir
from concourse.bass_utils import run_bass_kernel_spmd

F32 = mybir.dt.float32
BF16 = mybir.dt.bfloat16
I32 = mybir.dt.int32
AF = mybir.ActivationFunctionType
ALU = mybir.AluOpType
AXL = mybir.AxisListType
BF = ml_dtypes.bfloat16

NCORE = 8
TC = 1024
HALO = 256
TCH = TC + HALO
D = 2048
H, HD = 16, 128
QC = 512
M = 16
NB = 512
NBC = 64
NIH, CI = 4, 64
G, DG = 4, 512
WIN = 256
EPSA = 1e-6
EPSLN = 1e-5
INTER = 5504
NIT = 43
LUTN = 16
SCALE = 1.0 / math.sqrt(HD)
CHUNK_TILES = (11, 11, 11, 10)
CHUNK_OFF = (0, 11, 22, 33)
NCHUNK = 4
SEGS = ((0, 512), (512, 512), (1024, 256))

ENGINES = ("tensor", "vector", "scalar", "gpsimd", "sync")


class P:

    def __init__(self, nc):
        self.nc = nc
        self.ops = {e: [] for e in ENGINES}
        self.cnt = {e: 0 for e in ENGINES}
        self.streams = {}
        self.last_w = {}
        self.readers = {}
        self.pre = {}
        self._sn = 0

    def stream(self, pref="s"):
        self._sn += 1
        return f"{pref}{self._sn}"

    def emit(self, eng, fn, reads=(), writes=(), deps=(), stream=None, inc=None, shared=False):
        dep = list(deps)
        for b in reads:
            t = self.last_w.get(b)
            if t is not None:
                dep.append(t)
        for b in writes:
            t = self.last_w.get(b)
            if t is not None:
                dep.append(t)
            dep.extend(self.readers.get(b, ()))
            dep.extend(self.pre.pop(b, ()))
        rdep = []
        for (dk, dv) in dep:
            if dv is None:
                dv = self.streams[dk[2:]]
            rdep.append((dk, dv))
        dep = rdep
        if stream is None:
            key = "c_" + eng
            self.cnt[eng] += 1
            tok = (key, self.cnt[eng])
            use_inc = 1
        else:
            key = "d_" + stream
            use_inc = 16 if inc is None else inc
            v = self.streams.get(stream, 0) + use_inc
            self.streams[stream] = v
            tok = (key, None) if shared else (key, v)
        self.ops[eng].append((fn, tuple(dep), key, use_inc))
        for b in reads:
            self.readers.setdefault(b, []).append(tok)
        for b in writes:
            self.last_w[b] = tok
            self.readers[b] = []
        return tok

    def dma(self, eng, dst, src, buf=None, reads=(), deps=(), stream=None, shared=False):
        st = stream or self.stream("q")
        wr = [buf] if buf else []
        return self.emit(eng, lambda e, d=dst, s=src: e.dma_start(out=d, in_=s),
                         reads=reads, writes=wr, deps=deps, stream=st, shared=shared)

    def join_tokens(self, bufs):
        out = []
        for b in bufs:
            t = self.last_w.get(b)
            if t is not None:
                out.append(t)
            out.extend(self.readers.get(b, ()))
        seen = set()
        res = []
        for t in out:
            if t not in seen:
                seen.add(t)
                res.append(t)
        return res

    def seed(self, bufs_to, tokens):
        for b in bufs_to:
            self.pre.setdefault(b, []).extend(tokens)

    def acc_deps(self, buf):
        dep = []
        t = self.last_w.get(buf)
        if t is not None:
            dep.append(t)
        dep.extend(self.readers.get(buf, ()))
        return dep

    def set_write(self, buf, tok):
        self.last_w[buf] = tok
        self.readers[buf] = []

    def add_read(self, buf, tok):
        self.readers.setdefault(buf, []).append(tok)

    def mm_group(self, bank_buf, mlist, reads, deps=()):
        dep = list(deps) + self.acc_deps(bank_buf)
        for b in reads:
            t = self.last_w.get(b)
            if t is not None:
                dep.append(t)
        n = len(mlist)
        tok = None
        for i, (o, l, r) in enumerate(mlist):
            tok = self.emit("tensor",
                            lambda e, o=o, l=l, r=r, st=(i == 0), sp=(i == n - 1):
                            e.matmul(o, l, r, start=st, stop=sp),
                            deps=dep if i == 0 else [])
            dep = []
        for b in reads:
            self.add_read(b, tok)
        self.set_write(bank_buf, tok)
        return tok

    def build(self):
        nc = self.nc
        import contextlib
        stack = contextlib.ExitStack()
        sems = {}
        for e in ENGINES:
            sems["c_" + e] = stack.enter_context(nc.semaphore("c_" + e))
        for s in self.streams:
            sems["d_" + s] = stack.enter_context(nc.semaphore("d_" + s))
        blk = stack.enter_context(nc.Block())
        for e in ENGINES:
            ops = self.ops[e]
            if not ops:
                continue

            def make(engine_ops):
                def body(eng):
                    waited = {}
                    for fn, deps, key, inc in engine_ops:
                        for (dk, dv) in deps:
                            if waited.get(dk, -1) >= dv:
                                continue
                            eng.wait_ge(sems[dk], dv)
                            waited[dk] = dv
                        fn(eng).then_inc(sems[key], inc)
                return body
            getattr(blk, e)(make(ops))
        stack.close()


def build_program(debug=False):
    nc = bass.Bass("TRN2", target_bir_lowering=False)
    import contextlib

    def din(name, shape, dt=F32):
        return nc.dram_tensor(name, list(shape), dt, kind="ExternalInput").ap()

    xT = din("xT", (D, TCH))
    xN = din("xN", (TCH, D))
    ropeC = din("ropeC", (32, TCH))
    ropeS = din("ropeS", (32, TCH))
    causal_add = din("causal_add", (8, 128, NB), BF16)
    causal_bin = din("causal_bin", (8, 128, NB), BF16)
    sw_mask = din("sw_mask", (4, 4, 128, WIN), BF16)
    eps_tie = din("eps_tie", (128, NB))
    ones_colB = din("ones_colB", (128, 1), BF16)
    ones_row_f = din("ones_row_f", (1, 128))
    identB = din("identB", (128, 128), BF16)
    identF = din("identF", (128, 128))
    w_qc = din("w_qc", (D, QC))
    w_qup = din("w_qup", (QC, D))
    kvc_w = din("kvc_w", (D, HD))
    kvc_wz = din("kvc_wz", (D, HD))
    kvc_biasT = din("kvc_biasT", (HD, M))
    k_proj_w = din("k_proj_w", (D, HD))
    v_proj_w = din("v_proj_w", (D, HD))
    idx_c_w = din("idx_c_w", (D, CI))
    idx_c_wz = din("idx_c_wz", (D, CI))
    idx_c_biasT = din("idx_c_biasT", (CI, M))
    w_dq = din("w_dq", (D, CI))
    w_iuq = din("w_iuq", (CI, NIH * CI))
    w_w = din("w_w", (D, NIH))
    q_norm_w = din("q_norm_w", (HD, 1))
    k_norm_w = din("k_norm_w", (HD, 1))
    group_w = din("group_w", (G, DG, DG))
    final_w = din("final_w", (D, D))
    lut = din("lut", (1, LUTN))
    gidxT = din("gidxT", (256, INTER), I32)
    uidxT = din("uidxT", (256, INTER), I32)
    didxT = din("didxT", (INTER, 256), I32)

    outT = nc.dram_tensor("outT", [D, TC], F32, kind="ExternalOutput").ap()

    dbg = {}

    def dout(name, shape, dt):
        t = nc.dram_tensor(name, list(shape), dt, kind="ExternalOutput").ap()
        dbg[name] = t
        return t

    cc_cmp_in = nc.dram_tensor("cc_cmp_in", [NBC, HD + CI], F32).ap()
    cc_cmp_out = nc.dram_tensor("cc_cmp_out", [NB, HD + CI], F32).ap()
    wg_in = [nc.dram_tensor(f"wg_in{k}", [256, CHUNK_TILES[k] * 128], BF16).ap() for k in range(NCHUNK)]
    wu_in = [nc.dram_tensor(f"wu_in{k}", [256, CHUNK_TILES[k] * 128], BF16).ap() for k in range(NCHUNK)]
    wd_in = [nc.dram_tensor(f"wd_in{k}", [CHUNK_TILES[k] * 128, 256], BF16).ap() for k in range(NCHUNK)]
    wg_out = [nc.dram_tensor(f"wg_out{k}", [D, CHUNK_TILES[k] * 128], BF16).ap() for k in range(NCHUNK)]
    wu_out = [nc.dram_tensor(f"wu_out{k}", [D, CHUNK_TILES[k] * 128], BF16).ap() for k in range(NCHUNK)]
    wd_out = [nc.dram_tensor(f"wd_out{k}", [NCORE * CHUNK_TILES[k] * 128, 256], BF16).ap() for k in range(NCHUNK)]
    sT_dram = nc.dram_tensor("sT_dram", [D, TC], BF16).ap()
    grp_dram = nc.dram_tensor("grp_dram", [D, TC], BF16).ap()
    fw_bf = nc.dram_tensor("fw_bf", [D, D], BF16).ap()
    wqup_bf = nc.dram_tensor("wqup_bf", [QC, D], BF16).ap()
    hint_dram = nc.dram_tensor("hint_dram", [INTER, TC], BF16).ap()
    stat_dram = [nc.dram_tensor(f"stat_dram{i}", [TCH], F32).ap() for i in range(2)]

    p = P(nc)
    top = contextlib.ExitStack()

    def sb(st, name, shape, dt=BF16):
        return st.enter_context(nc.sbuf_tensor(name, list(shape), dt))

    xnT = sb(top, "xnT", (128, 16, TCH))
    kTt = sb(top, "kTt", (128, TCH))
    vtok = sb(top, "vtok", (128, 10, HD))
    s_kvcw = sb(top, "s_kvcw", (128, 16, HD))
    s_kvcwz = sb(top, "s_kvcwz", (128, 16, HD))
    s_kproj = sb(top, "s_kproj", (128, 16, HD))
    s_vproj = sb(top, "s_vproj", (128, 16, HD))
    s_idxw = sb(top, "s_idxw", (128, 16, CI))
    s_idxwz = sb(top, "s_idxwz", (128, 16, CI))
    s_wdq = sb(top, "s_wdq", (128, 16, CI))
    s_wiuq = sb(top, "s_wiuq", (64, NIH * CI))
    s_ww = sb(top, "s_ww", (128, 16, NIH))
    s_kvcbT = sb(top, "s_kvcbT", (128, M), F32)
    s_idxbT = sb(top, "s_idxbT", (64, M), F32)
    s_qnw_s = sb(top, "s_qnw_s", (128, 1), F32)
    s_knw = sb(top, "s_knw", (128, 1), F32)
    s_knwb = sb(top, "s_knwb", (128, HD), F32)
    s_knwrow = sb(top, "s_knwrow", (1, 128), F32)
    s_onesB = sb(top, "s_onesB", (128, 1), BF16)
    s_onesRF = sb(top, "s_onesRF", (1, 128), F32)
    s_identB = sb(top, "s_identB", (128, 128), BF16)
    s_identF = sb(top, "s_identF", (128, 128), F32)
    s_lutrow = sb(top, "s_lutrow", (1, LUTN), F32)
    s_lutpp = sb(top, "s_lutpp", (128, LUTN), F32)
    s_CC = sb(top, "s_CC", (64, TCH))
    s_SS = sb(top, "s_SS", (64, TCH))
    s_negC = sb(top, "s_negC", (128, 1), F32)
    s_epsc = sb(top, "s_epsc", (128, 2), F32)
    s_scal = sb(top, "s_scal", (1, 8), F32)
    rope_scr = sb(top, "rope_scr", (64, TCH))
    kvnrm = sb(top, "kvnrm", (128, 4, HD))
    kvnrmT = sb(top, "kvnrmT", (128, NB))
    kcompT = sb(top, "kcompT", (128, NB))
    cc_sb = sb(top, "cc_sb", (64, HD + CI), F32)
    kvtok = sb(top, "kvtok", (128, 4, HD + CI), F32)
    gidx_ring = sb(top, "gidx_ring", (128, 1, 1408))
    gacc = sb(top, "gacc", (128, 1408))
    gtmp = sb(top, "gtmp", (128, 1408))
    gidx_ringG = sb(top, "gidx_ringG", (128, 11, 256))
    gaccG = sb(top, "gaccG", (128, 11, 256))
    gtmpG = sb(top, "gtmpG", (128, 11, 256))

    pb = [top.enter_context(nc.psum_tensor(f"pb{i}", [128, 512], F32)) for i in range(7)]
    pbT = top.enter_context(nc.psum_tensor("pbT", [128, 1024], BF16))
    pb.append(pbT[:, :].bitcast(F32))
    PB = [f"pb{i}" for i in range(7)] + ["pbT"]

    def kt16(w):
        return w.rearrange("(kt p) m -> p kt m", p=128)

    p.dma("gpsimd", s_kvcw[:, :, :], kt16(kvc_w), buf="s_kvcw", stream="clw", shared=True)
    p.dma("gpsimd", s_kvcwz[:, :, :], kt16(kvc_wz), buf="s_kvcwz", stream="clw", shared=True)
    p.dma("gpsimd", s_kproj[:, :, :], kt16(k_proj_w), buf="s_kproj", stream="clw", shared=True)
    p.dma("gpsimd", s_vproj[:, :, :], kt16(v_proj_w), buf="s_vproj", stream="clw", shared=True)
    p.dma("gpsimd", s_idxw[:, :, :], kt16(idx_c_w), buf="s_idxw", stream="clw", shared=True)
    p.dma("gpsimd", s_idxwz[:, :, :], kt16(idx_c_wz), buf="s_idxwz", stream="clw", shared=True)
    p.dma("gpsimd", s_wdq[:, :, :], kt16(w_dq), buf="s_wdq", stream="clw", shared=True)
    p.dma("gpsimd", s_wiuq[:, :], w_iuq[:, :], buf="s_wiuq", stream="clw", shared=True)
    p.dma("gpsimd", s_ww[:, :, :], kt16(w_w), buf="s_ww", stream="clw", shared=True)
    p.dma("gpsimd", s_onesB[:, :], ones_colB[:, :], buf="s_onesB", stream="clg", shared=True)
    p.dma("gpsimd", s_identB[:, :], identB[:, :], buf="s_identB", stream="clg", shared=True)
    p.dma("sync", s_identF[:, :], identF[:, :], buf="s_identF", stream="cls", shared=True)
    p.dma("sync", s_kvcbT[:, :], kvc_biasT[:, :], buf="s_kvcbT", stream="cls", shared=True)
    p.dma("sync", s_idxbT[:, :], idx_c_biasT[:, :], buf="s_idxbT", stream="cls", shared=True)
    p.dma("sync", s_onesRF[:, :], ones_row_f[:, :], buf="s_onesRF", stream="cls", shared=True)
    p.dma("sync", s_lutrow[:, :], lut[:, :], buf="s_lutrow", stream="cls", shared=True)
    p.dma("gpsimd", s_CC[0:32, :], ropeC[:, :], buf="s_CC", stream="clg", shared=True)
    p.dma("gpsimd", s_CC[32:64, :], ropeC[:, :], buf="s_CC", stream="clg", shared=True)
    p.dma("gpsimd", s_SS[0:32, :], ropeS[:, :], buf="s_SS", stream="clg", shared=True)
    p.dma("gpsimd", s_SS[32:64, :], ropeS[:, :], buf="s_SS", stream="clg", shared=True)
    p.dma("sync", s_qnw_s[:, :], q_norm_w[:, :], buf="s_qnw_s", stream="cls", shared=True)
    p.dma("sync", s_knw[:, :], k_norm_w[:, :], buf="s_knw", stream="cls", shared=True)

    p.emit("vector", lambda e: e.memset(s_epsc[:, 0:1], EPSLN), writes=["s_epsc"])
    p.emit("vector", lambda e: e.memset(s_epsc[:, 1:2], EPSA), writes=["s_epsc"])
    p.mm_group(PB[0], [(pb[0][:, 0:LUTN], s_onesRF[:, :], s_lutrow[:, :])],
               reads=["s_onesRF", "s_lutrow"])
    p.emit("vector", lambda e: e.tensor_copy(s_lutpp[:, :], pb[0][:, 0:LUTN]),
           reads=[PB[0]], writes=["s_lutpp"])
    p.emit("vector", lambda e: e.tensor_scalar(s_qnw_s[:, :], s_qnw_s[:, :], SCALE, None, op0=ALU.mult),
           reads=["s_qnw_s"], writes=["s_qnw_s"])
    p.emit("vector", lambda e: e.tensor_scalar(s_SS[0:32, :], s_SS[0:32, :], -1.0, None, op0=ALU.mult),
           reads=["s_SS"], writes=["s_SS"])
    p.emit("tensor", lambda e: e.transpose(pb[1][0:1, 0:128], s_knw[:, :], s_identF[:, :]),
           reads=["s_knw", "s_identF"], writes=[PB[1]])
    p.emit("vector", lambda e: e.tensor_copy(s_knwrow[:, :], pb[1][0:1, 0:128]),
           reads=[PB[1]], writes=["s_knwrow"])
    p.mm_group(PB[1], [(pb[1][:, 0:128], s_onesRF[:, :], s_knwrow[:, :])],
               reads=["s_onesRF", "s_knwrow"])
    p.emit("vector", lambda e: e.tensor_copy(s_knwb[:, :], pb[1][:, 0:128]),
           reads=[PB[1]], writes=["s_knwb"])
    p.emit("tensor", lambda e: e.transpose(pb[2][0:1, 0:128], s_qnw_s[:, :], s_identF[:, :]),
           reads=["s_qnw_s", "s_identF"], writes=[PB[2]])
    p.emit("vector", lambda e: e.tensor_reduce(s_scal[:, 0:1], pb[2][0:1, 0:128],
                                               axis=AXL.X, op=ALU.max, apply_absolute_value=True),
           reads=[PB[2]], writes=["s_scal"])
    p.emit("vector", lambda e: e.tensor_reduce(s_scal[:, 1:2], s_knwrow[:, :],
                                               axis=AXL.X, op=ALU.max, apply_absolute_value=True),
           reads=["s_knwrow"], writes=["s_scal"])
    p.emit("vector", lambda e: e.tensor_tensor(s_scal[:, 2:3], s_scal[:, 0:1], s_scal[:, 1:2], ALU.mult),
           reads=["s_scal"], writes=["s_scal"])
    p.emit("vector", lambda e: e.tensor_scalar(s_scal[:, 3:4], s_scal[:, 2:3], -134.4, None, op0=ALU.mult),
           reads=["s_scal"], writes=["s_scal"])
    p.mm_group(PB[2], [(pb[2][:, 0:1], s_onesRF[:, :], s_scal[:, 3:4])],
               reads=["s_onesRF", "s_scal"])
    p.emit("vector", lambda e: e.tensor_copy(s_negC[:, :], pb[2][:, 0:1]),
           reads=[PB[2]], writes=["s_negC"])

    def full_barrier():
        skip = ("cc", "gl", "gs", "gv")
        toks = [("c_" + e, p.cnt[e]) for e in ENGINES if p.cnt[e] > 0]
        toks += [("d_" + s, v) for s, v in p.streams.items()
                 if not any(s.startswith(x) for x in skip)]
        for e in ENGINES:
            p.emit(e, lambda en: en.nop(), deps=toks)

    ln = contextlib.ExitStack()
    xnat_ring = sb(ln, "xnat_ring", (128, 2, D), F32)
    xn_stat = sb(ln, "xn_stat", (128, 10, 2), F32)
    xn_rows = sb(ln, "xn_rows", (1, 2, TCH), F32)
    stat_t = sb(ln, "stat_t", (10, 2, 128), F32)
    ln_row = sb(ln, "ln_row", (1, TCH), F32)
    x_ring = sb(ln, "x_ring", (128, 2, TCH), F32)
    ln_t = sb(ln, "ln_t", (128, 2, 512), F32)

    xnstr = [p.stream("xn") for _ in range(4)]
    for i in range(10):
        sl = i % 2
        p.dma("sync", xnat_ring[:, sl, :], xN[128 * i:128 * (i + 1), :],
              buf=f"xnat{sl}", stream=xnstr[i % 4])
        p.emit("vector", lambda e, i=i, sl=sl: e.tensor_reduce(
            xn_stat[:, i, 0:1], xnat_ring[:, sl, :], axis=AXL.X, op=ALU.add),
            reads=[f"xnat{sl}"], writes=["xn_stat"])
        p.emit("scalar", lambda e, i=i, sl=sl: e.activation(
            xnat_ring[:, sl, :], xnat_ring[:, sl, :], AF.Square, accum_out=xn_stat[:, i, 1:2]),
            reads=[f"xnat{sl}"], writes=[f"xnat{sl}", "xn_stat"])
    for j in range(2):
        p.emit("tensor", lambda e, j=j: e.transpose(
            pb[6][0:10, 0:128], xn_stat[:, :, j], s_identF[:, :]),
            reads=["xn_stat", "s_identF"], writes=[PB[6]])
        p.emit("vector", lambda e, j=j: e.tensor_copy(stat_t[:, j, :], pb[6][0:10, 0:128]),
               reads=[PB[6]], writes=[f"stat_t{j}"])
        p.dma("sync", xn_rows[0:1, j, :].rearrange("o (i q) -> o i q", q=128),
              stat_t[:, j, :], buf="xn_rows", reads=[f"stat_t{j}"], stream="xst", shared=True)
    p.emit("vector", lambda e: e.tensor_scalar(
        xn_rows[:, 0, :], xn_rows[:, 0, :], 1.0 / D, None, op0=ALU.mult),
        reads=["xn_rows"], writes=["xn_rows"])
    p.emit("vector", lambda e: e.tensor_scalar(
        xn_rows[:, 1, :], xn_rows[:, 1, :], 1.0 / D, None, op0=ALU.mult),
        reads=["xn_rows"], writes=["xn_rows"])
    p.emit("vector", lambda e: e.tensor_tensor(
        ln_row[:, :], xn_rows[:, 0, :], xn_rows[:, 0, :], ALU.mult),
        reads=["xn_rows"], writes=["ln_row"])
    p.emit("vector", lambda e: e.tensor_tensor(
        xn_rows[:, 1, :], xn_rows[:, 1, :], ln_row[:, :], ALU.subtract),
        reads=["xn_rows", "ln_row"], writes=["xn_rows"])
    p.emit("scalar", lambda e: e.activation(
        xn_rows[:, 1, :], xn_rows[:, 1, :], AF.Ln, bias=s_epsc[0:1, 0:1], scale=1.0),
        reads=["xn_rows", "s_epsc"], writes=["xn_rows"])
    p.emit("scalar", lambda e: e.activation(
        xn_rows[:, 1, :], xn_rows[:, 1, :], AF.Exp, scale=-0.5),
        reads=["xn_rows"], writes=["xn_rows"])
    for si, (o, n) in enumerate(SEGS):
        p.mm_group(PB[si], [(pb[si][:, 0:n], s_onesRF[:, :], xn_rows[:, 0, o:o + n])],
                   reads=["s_onesRF", "xn_rows"])
        p.mm_group(PB[3 + si], [(pb[3 + si][:, 0:n], s_onesRF[:, :], xn_rows[:, 1, o:o + n])],
                   reads=["s_onesRF", "xn_rows"])
    xrstr = [p.stream("xr") for _ in range(2)]
    for dt in range(16):
        sl = dt % 2
        xb = f"xrr{sl}"
        eng = "vector" if dt % 2 == 0 else "vector"
        p.dma("sync", x_ring[:, sl, :], xT[128 * dt:128 * (dt + 1), :],
              buf=xb, stream=xrstr[sl])
        lb = f"ln_t{dt % 2}"
        lt = ln_t[:, dt % 2, :]
        for si, (o, n) in enumerate(SEGS):
            p.emit(eng, lambda e, lt=lt, sl=sl, si=si, o=o, n=n: e.tensor_tensor(
                lt[:, 0:n], x_ring[:, sl, o:o + n], pb[si][:, 0:n], ALU.subtract),
                reads=[xb, PB[si]], writes=[lb])
            p.emit(eng, lambda e, dt=dt, lt=lt, si=si, o=o, n=n: e.tensor_tensor(
                xnT[:, dt, o:o + n], lt[:, 0:n], pb[3 + si][:, 0:n], ALU.mult),
                reads=[lb, PB[3 + si]], writes=[f"xnT{dt}"])
    ln.close()
    XN = [f"xnT{dt}" for dt in range(16)]
    if debug:
        d_xnT = dout("dbg_xnT", (128, 16, TCH), BF16)
        for dt in range(16):
            p.dma("sync", d_xnT[:, dt, :], xnT[:, dt, :], reads=[f"xnT{dt}"], stream="dbg", shared=True)
    full_barrier()

    qc_st = contextlib.ExitStack()
    wqup_ring = sb(qc_st, "wqup_ring", (128, 2, 4, 128))
    AT = sb(qc_st, "AT", (128, 4, TC))
    gw_st = contextlib.ExitStack()
    swm = sb(gw_st, "swm", (128, 16, WIN))
    wgu_ring = sb(gw_st, "wgu_ring", (128, 2, 16, 128))
    hint_sb = sb(gw_st, "hint_sb", (128, 2, 512))
    gsil2 = sb(gw_st, "gsil2", (128, 512))
    sel_st = contextlib.ExitStack()
    selbinT = sb(sel_st, "selbinT", (128, 4, TC))
    p34_st = contextlib.ExitStack()
    s_wqc = sb(p34_st, "s_wqc", (128, 16, QC))
    k_sc = sb(p34_st, "k_sc", (128, TCH))
    k_row = sb(p34_st, "k_row", (1, 512), F32)
    dqT = sb(p34_st, "dqT", (64, TC))
    qiT = sb(p34_st, "qiT", (128, 2, TC))
    wih = sb(p34_st, "wih", (128, 8, NIH), F32)
    p.dma("gpsimd", s_wqc[:, :, :], kt16(w_qc), buf="s_wqc", stream="clw", shared=True)
    p.dma("sync", swm[:, :, :], sw_mask.rearrange("w k p t -> p (w k) t"), buf="swm", stream="cls", shared=True)

    cc_toks = {}

    def emit_cc(name, in_ap, out_ap, dep_bufs):
        tok = p.emit("gpsimd", lambda e, i=in_ap, o=out_ap: e.collective_compute(
            "AllGather", ALU.bypass, replica_groups=[list(range(NCORE))],
            ins=[i.opt()], outs=[o.opt()]),
            reads=list(dep_bufs), writes=[f"cc_{name}"],
            stream=p.stream("cc"), inc=1)
        cc_toks[name] = tok
        return tok

    units = []

    def add_dve_job(mat_i, k, tr):
        cols = CHUNK_TILES[k] * 128
        src = (gidxT if mat_i == 0 else uidxT)
        dst = (wg_in if mat_i == 0 else wu_in)[k]
        sl = 0
        ring_buf = f"gidx{sl}"
        name = f"g{mat_i}_{k}_{tr}"

        def load():
            p.dma("gpsimd", gidx_ring[:, sl, 0:cols],
                  src[128 * tr:128 * (tr + 1), CHUNK_OFF[k] * 128:CHUNK_OFF[k] * 128 + cols],
                  buf=ring_buf, stream=f"gl{sl}")
        units.append(load)
        for v in range(LUTN):
            if v == 0:
                units.append(lambda sl=sl, cols=cols, rb=ring_buf: p.emit(
                    "vector", lambda e: e.tensor_scalar(
                        gacc[:, 0:cols], gidx_ring[:, sl, 0:cols], 0.0,
                        s_lutpp[:, 0:1], op0=ALU.is_equal, op1=ALU.mult),
                    reads=[rb, "s_lutpp"], writes=["gacc"], stream="gv", inc=1))
            else:
                def u(v=v, sl=sl, cols=cols, rb=ring_buf):
                    p.emit("vector", lambda e: e.tensor_scalar(
                        gtmp[:, 0:cols], gidx_ring[:, sl, 0:cols], float(v),
                        s_lutpp[:, v:v + 1], op0=ALU.is_equal, op1=ALU.mult),
                        reads=[rb, "s_lutpp"], writes=["gtmp"], stream="gv", inc=1)
                    p.emit("vector", lambda e: e.tensor_tensor(
                        gacc[:, 0:cols], gacc[:, 0:cols], gtmp[:, 0:cols], ALU.add),
                        reads=["gtmp"], writes=["gacc"], stream="gv", inc=1)
                units.append(u)

        def store():
            p.dma("sync", dst[128 * tr:128 * (tr + 1), :], gacc[:, 0:cols],
                  buf=f"gst_{name}", reads=["gacc"], stream=f"gs{sl}")
        units.append(store)

    def add_down_job(k):
        tiles = CHUNK_TILES[k]
        cols = tiles * 256

        def load():
            p.dma("gpsimd", gidx_ringG[:, 0:tiles, :],
                  didxT[CHUNK_OFF[k] * 128:(CHUNK_OFF[k] + tiles) * 128, :]
                  .rearrange("(it p) m -> p it m", p=128),
                  buf="gidxG", stream="glG")
        units.append(load)
        gv = gidx_ringG[:, :, :].rearrange("p a b -> p (a b)")
        av = gaccG[:, :, :].rearrange("p a b -> p (a b)")
        tv = gtmpG[:, :, :].rearrange("p a b -> p (a b)")
        for v in range(LUTN):
            if v == 0:
                units.append(lambda cols=cols: p.emit(
                    "vector", lambda e: e.tensor_scalar(
                        av[:, 0:cols], gv[:, 0:cols], 0.0,
                        s_lutpp[:, 0:1], op0=ALU.is_equal, op1=ALU.mult),
                    reads=["gidxG", "s_lutpp"], writes=["gaccG"], stream="gv", inc=1))
            else:
                def u(v=v, cols=cols):
                    p.emit("vector", lambda e: e.tensor_scalar(
                        tv[:, 0:cols], gv[:, 0:cols], float(v),
                        s_lutpp[:, v:v + 1], op0=ALU.is_equal, op1=ALU.mult),
                        reads=["gidxG", "s_lutpp"], writes=["gtmpG"], stream="gv", inc=1)
                    p.emit("vector", lambda e: e.tensor_tensor(
                        av[:, 0:cols], av[:, 0:cols], tv[:, 0:cols], ALU.add),
                        reads=["gtmpG"], writes=["gaccG"], stream="gv", inc=1)
                units.append(u)

        def store():
            p.dma("sync", wd_in[k].rearrange("(it p) m -> p it m", p=128),
                  gaccG[:, 0:tiles, :],
                  buf=f"gstG{k}", reads=["gaccG"], stream="gsG")
        units.append(store)

    for k in range(NCHUNK):
        add_down_job(k)
        for mat_i in range(2):
            for tr in range(2):
                add_dve_job(mat_i, k, tr)

        def ccs(k=k):
            emit_cc(f"d{k}", wd_in[k], wd_out[k], [f"gstG{k}"])
            emit_cc(f"g{k}", wg_in[k], wg_out[k], [f"gst_g0_{k}_0", f"gst_g0_{k}_1"])
            emit_cc(f"u{k}", wu_in[k], wu_out[k], [f"gst_g1_{k}_0", f"gst_g1_{k}_1"])
        units.append(ccs)

    _pu = {"i": 0}

    def pump(n):
        done = 0
        while done < n and _pu["i"] < len(units):
            units[_pu["i"]]()
            _pu["i"] += 1
            done += 1

    def pump_all():
        pump(len(units))

    f_units = []
    wgu_str2 = [p.stream("w2"), p.stream("w2")]
    hst_str = [p.stream("hs"), p.stream("hs")]

    def add_f_unit(k, it_l, sh):
        it = CHUNK_OFF[k] + it_l
        xcol = HALO + 512 * sh
        sl = it % 2

        def u():
            gu_dep = [cc_toks[f"g{k}"], cc_toks[f"u{k}"]]
            p.dma("scalar", wgu_ring[:, 0, :, :],
                  wg_out[k][:, 128 * it_l:128 * (it_l + 1)].rearrange("(kt p) m -> p kt m", p=128),
                  buf="wgug", deps=gu_dep, stream=wgu_str2[0])
            p.dma("scalar", wgu_ring[:, 1, :, :],
                  wu_out[k][:, 128 * it_l:128 * (it_l + 1)].rearrange("(kt p) m -> p kt m", p=128),
                  buf="wguu", deps=gu_dep, stream=wgu_str2[1])
            p.mm_group(PB[4], [(pb[4][:, :], wgu_ring[:, 0, kt, :],
                                xnT[:, kt, xcol:xcol + 512]) for kt in range(16)],
                       reads=["wgug"] + XN)
            p.mm_group(PB[5], [(pb[5][:, :], wgu_ring[:, 1, kt, :],
                                xnT[:, kt, xcol:xcol + 512]) for kt in range(16)],
                       reads=["wguu"] + XN)
            p.emit("scalar", lambda e: e.activation(gsil2[:, :], pb[4][:, :], AF.Silu),
                   reads=[PB[4]], writes=["gsil2"])
            p.emit("vector", lambda e, sl=sl: e.tensor_tensor(
                hint_sb[:, sl, :], gsil2[:, :], pb[5][:, :], ALU.mult),
                reads=["gsil2", PB[5]], writes=[f"hsb{sl}"])
            p.dma("sync", hint_dram[128 * it:128 * (it + 1), 512 * sh:512 * (sh + 1)],
                  hint_sb[:, sl, :], buf=f"hd_{it}_{sh}", reads=[f"hsb{sl}"],
                  stream=hst_str[sl])
        f_units.append(u)

    for k in range(NCHUNK):
        for it_l in range(CHUNK_TILES[k]):
            for sh in range(2):
                add_f_unit(k, it_l, sh)

    _fu = {"i": 0}

    def fpump(n):
        done = 0
        while done < n and _fu["i"] < len(f_units):
            f_units[_fu["i"]]()
            _fu["i"] += 1
            done += 1

    def fpump_all():
        fpump(len(f_units))

    cmp_st = contextlib.ExitStack()
    cmp_c = sb(cmp_st, "cmp_c", (128, TC), F32)
    cmp_z = sb(cmp_st, "cmp_z", (128, TC), F32)
    cmp_s1 = sb(cmp_st, "cmp_s1", (128, NBC), F32)
    cmp_s2 = sb(cmp_st, "cmp_s2", (128, NBC), F32)
    kv_own = sb(cmp_st, "kv_own", (128, NBC), F32)
    ic_own = sb(cmp_st, "ic_own", (64, NBC), F32)

    def compress(wsb_c, buf_c, wsb_z, buf_z, bias_sb, bias_buf, pdim, out_sb, out_buf):
        for which, (wsb, wbuf, dst, dbuf) in enumerate(
                [(wsb_c, buf_c, cmp_c, "cmp_c"), (wsb_z, buf_z, cmp_z, "cmp_z")]):
            for nch in range(2):
                bank = nch + 2 * which
                p.mm_group(PB[bank],
                           [(pb[bank][0:pdim, :], wsb[:, kt, :],
                             xnT[:, kt, HALO + 512 * nch:HALO + 512 * (nch + 1)])
                            for kt in range(16)],
                           reads=[wbuf] + XN)
                p.emit("scalar", lambda e, b=bank, d=dst, nch=nch, pdim=pdim: e.activation(
                    d[0:pdim, 512 * nch:512 * (nch + 1)], pb[b][0:pdim, :], AF.Identity),
                    reads=[PB[bank]], writes=[dbuf])
        zv = cmp_z[0:pdim, :].rearrange("p (a b) -> p a b", b=M)
        cv = cmp_c[0:pdim, :].rearrange("p (a b) -> p a b", b=M)
        p.emit("vector", lambda e: e.tensor_tensor(
            zv, zv, bias_sb[0:pdim, :].unsqueeze(1).to_broadcast([pdim, NBC, M]), ALU.add),
            reads=["cmp_z", bias_buf], writes=["cmp_z"])
        p.emit("vector", lambda e: e.tensor_reduce(
            cmp_s1[0:pdim, :], zv, axis=AXL.X, op=ALU.max),
            reads=["cmp_z"], writes=["cmp_s1"])
        p.emit("vector", lambda e: e.tensor_tensor(
            zv, zv, cmp_s1[0:pdim, :].unsqueeze(2).to_broadcast([pdim, NBC, M]), ALU.subtract),
            reads=["cmp_z", "cmp_s1"], writes=["cmp_z"])
        p.emit("scalar", lambda e: e.activation(zv, zv, AF.Exp),
               reads=["cmp_z"], writes=["cmp_z"])
        p.emit("vector", lambda e: e.tensor_reduce(
            cmp_s1[0:pdim, :], zv, axis=AXL.X, op=ALU.add),
            reads=["cmp_z"], writes=["cmp_s1"])
        p.emit("vector", lambda e: e.reciprocal(cmp_s2[0:pdim, :], cmp_s1[0:pdim, :]),
               reads=["cmp_s1"], writes=["cmp_s2"])
        p.emit("vector", lambda e: e.tensor_tensor(zv, zv, cv, ALU.mult),
               reads=["cmp_z", "cmp_c"], writes=["cmp_z"])
        p.emit("vector", lambda e: e.tensor_reduce(
            cmp_s1[0:pdim, :], zv, axis=AXL.X, op=ALU.add),
            reads=["cmp_z"], writes=["cmp_s1"])
        p.emit("vector", lambda e: e.tensor_tensor(
            out_sb, cmp_s1[0:pdim, :], cmp_s2[0:pdim, :], ALU.mult),
            reads=["cmp_s1", "cmp_s2"], writes=[out_buf])

    compress(s_kvcw, "s_kvcw", s_kvcwz, "s_kvcwz", s_kvcbT, "s_kvcbT", 128, kv_own[:, :], "kv_own")
    compress(s_idxw, "s_idxw", s_idxwz, "s_idxwz", s_idxbT, "s_idxbT", 64, ic_own[0:64, :], "ic_own")
    p.emit("tensor", lambda e: e.transpose(pb[4][0:64, 0:128], kv_own[:, :], s_identF[:, :]),
           reads=["kv_own", "s_identF"], writes=[PB[4]])
    p.emit("vector", lambda e: e.tensor_copy(cc_sb[:, 0:HD], pb[4][0:64, 0:128]),
           reads=[PB[4]], writes=["cc_sb"])
    p.emit("tensor", lambda e: e.transpose(pb[4][0:64, 0:64], ic_own[0:64, :], s_identF[0:64, 0:64]),
           reads=["ic_own", "s_identF"], writes=[PB[4]])
    p.emit("vector", lambda e: e.tensor_copy(cc_sb[:, HD:HD + CI], pb[4][0:64, 0:64]),
           reads=[PB[4]], writes=["cc_sb"])
    p.dma("sync", cc_cmp_in[:, :], cc_sb[:, :], buf="cc_cmp_st", reads=["cc_sb"], stream="cls", shared=True)
    emit_cc("cmp", cc_cmp_in, cc_cmp_out, ["cc_cmp_st"])
    cmp_st.close()
    _joinC = p.join_tokens(["cmp_c", "cmp_z", "cmp_s1", "cmp_s2", "kv_own", "ic_own",
                            "kv_sc", "kv_ms", "kct"])
    p.seed(["hsr", "isc", "iwork0", "iwork1", "mx8", "vstar", "selbin",
            "cmA0", "cmA1", "cmB0", "cmB1", "s_epstie"]
           + [f"iscA{tt}" for tt in range(8)], _joinC)

    p.dma("gpsimd", wqup_bf[:, :], w_qup[:, :], buf="wqup_bf", stream="glq", shared=True)
    pump(40)


    for si, (o, n) in enumerate(SEGS):
        bank = si % 2
        p.mm_group(PB[bank], [(pb[bank][:, 0:n], s_kproj[:, kt, :], xnT[:, kt, o:o + n])
                              for kt in range(16)], reads=["s_kproj"] + XN)
        p.emit("scalar", lambda e, b=bank, o=o, n=n: e.activation(
            kTt[:, o:o + n], pb[b][:, 0:n], AF.Identity),
            reads=[PB[bank]], writes=["kT"])
        p.emit("scalar", lambda e, o=o, n=n: e.activation(
            k_sc[:, o:o + n], kTt[:, o:o + n], AF.Square),
            reads=["kT"], writes=["k_sc"])
        p.mm_group(PB[2 + bank], [(pb[2 + bank][0:1, 0:n], s_onesB[:, :], k_sc[:, o:o + n])],
                   reads=["s_onesB", "k_sc"])
        p.emit("scalar", lambda e, b=2 + bank, n=n: e.activation(
            pb[b][0:1, 0:n], pb[b][0:1, 0:n], AF.Ln, bias=s_epsc[0:1, 1:2], scale=1.0 / HD),
            reads=[PB[2 + bank], "s_epsc"], writes=[PB[2 + bank]])
        p.emit("scalar", lambda e, b=2 + bank, n=n: e.activation(
            pb[b][0:1, 0:n], pb[b][0:1, 0:n], AF.Exp, scale=-0.5),
            reads=[PB[2 + bank]], writes=[PB[2 + bank]])
        p.emit("vector", lambda e, b=2 + bank, n=n: e.tensor_copy(
            k_row[:, 0:n], pb[b][0:1, 0:n]),
            reads=[PB[2 + bank]], writes=["k_row"])
        p.mm_group(PB[4], [(pb[4][:, 0:n], s_onesRF[:, :], k_row[:, 0:n])],
                   reads=["s_onesRF", "k_row"])
        p.emit("vector", lambda e, o=o, n=n: e.scalar_tensor_tensor(
            kTt[:, o:o + n], kTt[:, o:o + n], s_knw[:, :], pb[4][:, 0:n],
            op0=ALU.mult, op1=ALU.mult),
            reads=["kT", "s_knw", PB[4]], writes=["kT"])
    p.emit("vector", lambda e: e.tensor_copy(rope_scr[0:32, :], kTt[32:64, :]),
           reads=["kT"], writes=["rope_scr"])
    p.emit("vector", lambda e: e.tensor_copy(rope_scr[32:64, :], kTt[0:32, :]),
           reads=["kT"], writes=["rope_scr"])
    p.emit("vector", lambda e: e.tensor_tensor(kTt[0:64, :], kTt[0:64, :], s_CC[:, :], ALU.mult),
           reads=["kT", "s_CC"], writes=["kT"])
    p.emit("vector", lambda e: e.tensor_tensor(rope_scr[:, :], rope_scr[:, :], s_SS[:, :], ALU.mult),
           reads=["rope_scr", "s_SS"], writes=["rope_scr"])
    p.emit("vector", lambda e: e.tensor_tensor(kTt[0:64, :], kTt[0:64, :], rope_scr[:, :], ALU.add),
           reads=["kT", "rope_scr"], writes=["kT"])
    if debug:
        d_k = dout("dbg_k", (128, TCH), BF16)
        p.dma("sync", d_k[:, :], kTt[:, :], reads=["kT"], stream="dbg", shared=True)

    for mt in range(10):
        bank = 5 + (mt % 2)
        p.mm_group(PB[bank], [(pb[bank][:, 0:HD], xnT[:, kt, 128 * mt:128 * (mt + 1)],
                               s_vproj[:, kt, :]) for kt in range(16)],
                   reads=["s_vproj"] + XN)
        p.emit("scalar", lambda e, b=bank, mt=mt: e.activation(
            vtok[:, mt, :], pb[b][:, 0:HD], AF.Identity),
            reads=[PB[bank]], writes=["vtok"])

    p.dma("sync", kvtok[:, :, 0:HD],
          cc_cmp_out[:, 0:HD].rearrange("(a p) d -> p a d", p=128),
          buf="kvtok", deps=[cc_toks["cmp"]], stream="cls", shared=True)
    p.dma("sync", kvtok[:, :, HD:HD + CI],
          cc_cmp_out[:, HD:HD + CI].rearrange("(a p) d -> p a d", p=128),
          buf="kvtok2", deps=[cc_toks["cmp"]], stream="cls", shared=True)
    kvs = contextlib.ExitStack()
    kv_sc = sb(kvs, "kv_sc", (128, 4, HD), F32)
    kv_ms = sb(kvs, "kv_ms", (128, 4, 1), F32)
    kct = sb(kvs, "kct", (128, 4, CI))
    kvv = kvtok[:, :, 0:HD]
    p.emit("vector", lambda e: e.tensor_tensor(kv_sc[:, :, :], kvv, kvv, ALU.mult),
           reads=["kvtok"], writes=["kv_sc"])
    p.emit("vector", lambda e: e.tensor_reduce(kv_ms[:, :, :], kv_sc[:, :, :], axis=AXL.X, op=ALU.add),
           reads=["kv_sc"], writes=["kv_ms"])
    p.emit("scalar", lambda e: e.activation(kv_ms[:, :, :], kv_ms[:, :, :], AF.Ln,
                                            bias=s_epsc[:, 1:2], scale=1.0 / HD),
           reads=["kv_ms", "s_epsc"], writes=["kv_ms"])
    p.emit("scalar", lambda e: e.activation(kv_ms[:, :, :], kv_ms[:, :, :], AF.Exp, scale=-0.5),
           reads=["kv_ms"], writes=["kv_ms"])
    p.emit("vector", lambda e: e.tensor_tensor(
        kv_sc[:, :, :], kvv, kv_ms[:, :, :].to_broadcast([128, 4, HD]), ALU.mult),
        reads=["kvtok", "kv_ms"], writes=["kv_sc"])
    p.emit("vector", lambda e: e.tensor_tensor(
        kvnrm[:, :, :], kv_sc[:, :, :],
        s_knwb[:, :].unsqueeze(1).to_broadcast([128, 4, HD]), ALU.mult),
        reads=["kv_sc", "s_knwb"], writes=["kvnrm"])
    for a in range(4):
        p.emit("tensor", lambda e, a=a: e.transpose(pbT[:, 0:128], kvnrm[:, a, :], s_identB[:, :]),
               reads=["kvnrm", "s_identB"], writes=["pbT"])
        p.emit("vector", lambda e, a=a: e.tensor_copy(kvnrmT[:, 128 * a:128 * (a + 1)], pbT[:, 0:128]),
               reads=["pbT"], writes=["kvnrmT"])
    p.emit("vector", lambda e: e.tensor_copy(kct[:, :, :], kvtok[:, :, HD:HD + CI]),
           reads=["kvtok2"], writes=["kct"])
    for a in range(4):
        p.emit("tensor", lambda e, a=a: e.transpose(pbT[0:64, 0:128], kct[:, a, :], s_identB[:, :]),
               reads=["kct", "s_identB"], writes=["pbT"])
        p.emit("vector", lambda e, a=a: e.tensor_copy(kcompT[0:64, 128 * a:128 * (a + 1)], pbT[0:64, 0:128]),
               reads=["pbT"], writes=["kcompT"])
    p.dma("sync", kcompT[64:128, :], kcompT[0:64, :], buf="kcompT_hi", reads=["kcompT"],
          stream="cls", shared=True)
    if debug:
        d_kvn = dout("dbg_kvn", (128, 4, HD), BF16)
        d_kc = dout("dbg_kc", (128, 4, CI), F32)
        p.dma("sync", d_kvn[:, :, :], kvnrm[:, :, :], reads=["kvnrm"], stream="dbg", shared=True)
        p.dma("sync", d_kc[:, :, :], kvtok[:, :, HD:HD + CI], reads=["kvtok2"], stream="dbg", shared=True)
    kvs.close()

    for ci in range(4):
        p.dma("gpsimd", fw_bf[512 * ci:512 * (ci + 1), :], final_w[512 * ci:512 * (ci + 1), :],
              buf="fw_bf", stream="glf", shared=True)
    pump(40)

    for mt in range(4):
        for ncl in range(2):
            bank = ncl
            p.mm_group(PB[bank], [(pb[bank][:, :], s_wqc[:, kt, 128 * mt:128 * (mt + 1)],
                                   xnT[:, kt, HALO + 512 * ncl:HALO + 512 * (ncl + 1)])
                                  for kt in range(16)], reads=["s_wqc"] + XN)
            p.emit("scalar", lambda e, b=bank, mt=mt, ncl=ncl: e.activation(
                AT[:, mt, 512 * ncl:512 * (ncl + 1)], pb[b][:, :], AF.Identity),
                reads=[PB[bank]], writes=["AT"])
    for ncl in range(2):
        bank = 2 + ncl
        p.mm_group(PB[bank], [(pb[bank][0:64, :], s_wdq[:, kt, :],
                               xnT[:, kt, HALO + 512 * ncl:HALO + 512 * (ncl + 1)])
                              for kt in range(16)], reads=["s_wdq"] + XN)
        p.emit("scalar", lambda e, b=bank, ncl=ncl: e.activation(
            dqT[:, 512 * ncl:512 * (ncl + 1)], pb[b][0:64, :], AF.Identity),
            reads=[PB[bank]], writes=["dqT"])
    for mt in range(2):
        for ncl in range(2):
            bank = 4 + ncl
            p.mm_group(PB[bank], [(pb[bank][:, :], s_wiuq[:, 128 * mt:128 * (mt + 1)],
                                   dqT[:, 512 * ncl:512 * (ncl + 1)])],
                       reads=["s_wiuq", "dqT"])
            p.emit("scalar", lambda e, b=bank, mt=mt, ncl=ncl: e.activation(
                qiT[:, mt, 512 * ncl:512 * (ncl + 1)], pb[b][:, :], AF.Identity),
                reads=[PB[bank]], writes=["qiT"])
    for mt in range(8):
        p.mm_group(PB[6], [(pb[6][:, 0:NIH], xnT[:, kt, HALO + 128 * mt:HALO + 128 * (mt + 1)],
                            s_ww[:, kt, :]) for kt in range(16)],
                   reads=["s_ww"] + XN)
        p.emit("vector", lambda e, mt=mt: e.tensor_copy(wih[:, mt, :], pb[6][:, 0:NIH]),
               reads=[PB[6]], writes=["wih"])

    pump(40)

    isc_st = contextlib.ExitStack()
    hsr = sb(isc_st, "hsr", (128, NB), F32)
    isc = sb(isc_st, "isc", (128, NB), F32)
    isc_all = sb(isc_st, "isc_all", (128, 8, NB), F32)
    iwork = sb(isc_st, "iwork", (128, 2, NB), F32)
    mx8 = sb(isc_st, "mx8", (128, 8), F32)
    vstar = sb(isc_st, "vstar", (128, 1), F32)
    selbin = sb(isc_st, "selbin", (128, NB))
    mask_ring = sb(isc_st, "mask_ring", (128, 4, NB))
    s_epstie = sb(isc_st, "s_epstie", (128, NB), F32)
    p.dma("sync", s_epstie[:, :], eps_tie[:, :], buf="s_epstie", stream="cls", shared=True)
    d_isc = dout("dbg_isc", (8, 128, NB), F32) if debug else None
    d_sel = dout("dbg_sel", (8, 128, NB), BF16) if debug else None

    cm_str = [p.stream("cm") for _ in range(4)]
    for tt in range(8):
        for ih in range(NIH):
            bank = ih % 2
            base = 64 * (ih % 2)
            p.mm_group(PB[bank], [(pb[bank][:, 0:512],
                                   qiT[base:base + 64, ih // 2, 128 * tt:128 * (tt + 1)],
                                   kcompT[base:base + 64, 0:512])],
                       reads=["qiT", "kcompT" if ih % 2 == 0 else "kcompT_hi"])
            p.emit("scalar", lambda e, b=bank: e.activation(hsr[:, :], pb[b][:, 0:512], AF.Relu),
                   reads=[PB[bank]], writes=["hsr"])
            if ih == 0:
                p.emit("vector", lambda e, tt=tt, ih=ih: e.tensor_scalar(
                    isc[:, :], hsr[:, :], wih[:, tt, ih:ih + 1], None, op0=ALU.mult),
                    reads=["hsr", "wih"], writes=["isc"])
            elif ih < NIH - 1:
                p.emit("vector", lambda e, tt=tt, ih=ih: e.scalar_tensor_tensor(
                    isc[:, :], hsr[:, :], wih[:, tt, ih:ih + 1], isc[:, :],
                    op0=ALU.mult, op1=ALU.add),
                    reads=["hsr", "wih", "isc"], writes=["isc"])
            else:
                p.emit("vector", lambda e, tt=tt, ih=ih: e.scalar_tensor_tensor(
                    isc_all[:, tt, :], hsr[:, :], wih[:, tt, ih:ih + 1], isc[:, :],
                    op0=ALU.mult, op1=ALU.add),
                    reads=["hsr", "wih", "isc"], writes=[f"iscA{tt}"])
        pump(6)
        fpump(1)
    for tt in range(8):
        sl = tt % 2
        p.dma("sync", mask_ring[:, sl, :], causal_add[tt, :, :], buf=f"cmA{sl}",
              stream=cm_str[sl])
        p.dma("sync", mask_ring[:, 2 + sl, :], causal_bin[tt, :, :], buf=f"cmB{sl}",
              stream=cm_str[2 + sl])
        p.emit("vector", lambda e, tt=tt: e.tensor_scalar(
            iwork[:, 1, :], isc_all[:, tt, :], 0.0, None, op0=ALU.is_equal),
            reads=[f"iscA{tt}"], writes=["iwork1"])
        p.emit("vector", lambda e: e.tensor_tensor(
            iwork[:, 1, :], iwork[:, 1, :], s_epstie[:, :], ALU.mult),
            reads=["iwork1", "s_epstie"], writes=["iwork1"])
        p.emit("vector", lambda e, tt=tt: e.tensor_tensor(
            isc[:, :], isc_all[:, tt, :], iwork[:, 1, :], ALU.add),
            reads=[f"iscA{tt}", "iwork1"], writes=["isc"])
        p.emit("vector", lambda e, sl=sl: e.tensor_tensor(
            isc[:, :], isc[:, :], mask_ring[:, sl, :], ALU.add),
            reads=["isc", f"cmA{sl}"], writes=["isc"])
        if debug:
            p.dma("sync", d_isc[tt, :, :], isc[:, :], reads=["isc"], stream="dbg", shared=True)
        p.emit("vector", lambda e: e.tensor_copy(iwork[:, 0, :], isc[:, :]),
               reads=["isc"], writes=["iwork0"])
        for r in range(8):
            p.emit("vector", lambda e: e.max(out=mx8[:, :], in_=iwork[:, 0, :]),
                   reads=["iwork0"], writes=["mx8"])
            if r < 7:
                p.emit("vector", lambda e: e.match_replace(
                    out=iwork[:, 1, :], in_to_replace=mx8[:, :], in_values=iwork[:, 0, :],
                    imm_value=-1e38),
                    reads=["mx8", "iwork0"], writes=["iwork1"])
                p.emit("vector", lambda e: e.tensor_copy(iwork[:, 0, :], iwork[:, 1, :]),
                       reads=["iwork1"], writes=["iwork0"])
        p.emit("vector", lambda e: e.tensor_reduce(vstar[:, :], mx8[:, :], axis=AXL.X, op=ALU.min),
               reads=["mx8"], writes=["vstar"])
        p.emit("vector", lambda e: e.tensor_scalar(
            selbin[:, :], isc[:, :], vstar[:, :], None, op0=ALU.is_ge),
            reads=["isc", "vstar"], writes=["selbin"])
        p.emit("vector", lambda e, sl=sl: e.tensor_tensor(
            selbin[:, :], selbin[:, :], mask_ring[:, 2 + sl, :], ALU.mult),
            reads=["selbin", f"cmB{sl}"], writes=["selbin"])
        if debug:
            p.dma("sync", d_sel[tt, :, :], selbin[:, :], reads=["selbin"], stream="dbg", shared=True)
        for nt in range(4):
            p.emit("tensor", lambda e, nt=nt: e.transpose(
                pbT[:, 0:128], selbin[:, 128 * nt:128 * (nt + 1)], s_identB[:, :]),
                reads=["selbin", "s_identB"], writes=["pbT"])
            p.emit("vector", lambda e, nt=nt, tt=tt: e.tensor_copy(
                selbinT[:, nt, 128 * tt:128 * (tt + 1)], pbT[:, 0:128]),
                reads=["pbT"], writes=["selbinT"])
        pump(6)
        fpump(2)
    isc_st.close()
    p34_st.close()
    _join = p.join_tokens(
        ["hsr", "isc", "iwork0", "iwork1", "mx8", "vstar", "selbin"]
        + [f"iscA{tt}" for tt in range(8)] +
        ["placeholder_unused",
         "cmA0", "cmA1", "cmB0", "cmB1", "s_epstie",
         "s_wqc", "k_sc", "k_row", "dqT", "qiT", "wih"])
    p.seed(["qh0", "qh1", "q_scr", "q_row", "Esw0", "Esw1", "Esp0", "Esp1",
            "numsb0", "numsb1", "spt", "gwR", "grpt0", "grpt1",
            "nrowf0", "nrowf1", "nrow10", "nrow11", "spt0", "spt1", "fw0", "fw1", "grd0", "grd1",
            "sTt0", "sTt1"]
           + [f"attn{g}_{h}" for g in range(2) for h in range(4)], _join)

    at_st = contextlib.ExitStack()
    qT_ring = sb(at_st, "qT_ring", (128, 2, TC))
    q_scr = sb(at_st, "q_scr", (128, 512))
    q_row = sb(at_st, "q_row", (1, TC), F32)
    Esw = sb(at_st, "Esw", (128, 2, 4, WIN))
    Esp = sb(at_st, "Esp", (128, 2, 4, 512))
    numsb = sb(at_st, "numsb", (128, 768))
    spt = sb(at_st, "spt", (128, 2, 512))
    attn_g = sb(at_st, "attn_g", (128, 2, 4, TC))
    gw_ring = sb(at_st, "gw_ring", (128, 1, 4, 512))
    grp_tile = sb(at_st, "grp_tile", (128, 2, 512))
    nrowf = sb(at_st, "nrowf", (1, 1536), F32)

    wq_str = [p.stream("wq"), p.stream("wq")]
    d_q = dout("dbg_q", (128, H, TC), BF16) if debug else None
    d_attn = dout("dbg_attn", (128, H, TC), BF16) if debug else None

    def emit_qhead(h):
        sl = h % 2
        qb = f"qh{sl}"
        QB = PB[3]
        wsl = h % 2
        p.dma("scalar", wqup_ring[:, wsl, :, :],
              wqup_bf[:, 128 * h:128 * (h + 1)].rearrange("(kt p) m -> p kt m", p=128),
              buf=f"wq{wsl}", reads=["wqup_bf"], stream=wq_str[wsl])
        for ncl in range(2):
            p.mm_group(QB, [(pb[3][:, :], wqup_ring[:, wsl, kt, :],
                             AT[:, kt, 512 * ncl:512 * (ncl + 1)]) for kt in range(4)],
                       reads=[f"wq{wsl}", "AT"])
            p.emit("scalar", lambda e, ncl=ncl, sl=sl: e.activation(
                qT_ring[:, sl, 512 * ncl:512 * (ncl + 1)], pb[3][:, :], AF.Identity),
                reads=[QB], writes=[qb])
            p.emit("scalar", lambda e, sl=sl, ncl=ncl: e.activation(
                q_scr[:, :], qT_ring[:, sl, 512 * ncl:512 * (ncl + 1)], AF.Square),
                reads=[qb], writes=["q_scr"])
            p.mm_group(QB, [(pb[3][0:1, :], s_onesB[:, :], q_scr[:, :])],
                       reads=["s_onesB", "q_scr"])
            p.emit("scalar", lambda e: e.activation(
                pb[3][0:1, :], pb[3][0:1, :], AF.Ln, bias=s_epsc[0:1, 1:2], scale=1.0 / HD),
                reads=[QB, "s_epsc"], writes=[QB])
            p.emit("scalar", lambda e: e.activation(
                pb[3][0:1, :], pb[3][0:1, :], AF.Exp, scale=-0.5),
                reads=[QB], writes=[QB])
            p.emit("vector", lambda e, ncl=ncl: e.tensor_copy(
                q_row[:, 512 * ncl:512 * (ncl + 1)], pb[3][0:1, :]),
                reads=[QB], writes=["q_row"])
        p.emit("vector", lambda e, sl=sl: e.tensor_copy(rope_scr[0:32, 0:TC], qT_ring[32:64, sl, :]),
               reads=[qb], writes=["rope_scr"])
        p.emit("vector", lambda e, sl=sl: e.tensor_copy(rope_scr[32:64, 0:TC], qT_ring[0:32, sl, :]),
               reads=[qb], writes=["rope_scr"])
        p.emit("vector", lambda e, sl=sl: e.tensor_tensor(
            qT_ring[0:64, sl, :], qT_ring[0:64, sl, :], s_CC[:, HALO:], ALU.mult),
            reads=[qb, "s_CC"], writes=[qb])
        p.emit("vector", lambda e: e.tensor_tensor(
            rope_scr[:, 0:TC], rope_scr[:, 0:TC], s_SS[:, HALO:], ALU.mult),
            reads=["rope_scr", "s_SS"], writes=["rope_scr"])
        p.emit("vector", lambda e, sl=sl: e.tensor_tensor(
            qT_ring[0:64, sl, :], qT_ring[0:64, sl, :], rope_scr[:, 0:TC], ALU.add),
            reads=[qb, "rope_scr"], writes=[qb])
        for ncl in range(2):
            p.mm_group(QB, [(pb[3][:, :], s_onesRF[:, :],
                             q_row[:, 512 * ncl:512 * (ncl + 1)])],
                       reads=["s_onesRF", "q_row"])
            p.emit("vector", lambda e, ncl=ncl, sl=sl: e.scalar_tensor_tensor(
                qT_ring[:, sl, 512 * ncl:512 * (ncl + 1)],
                qT_ring[:, sl, 512 * ncl:512 * (ncl + 1)],
                s_qnw_s[:, :], pb[3][:, :], op0=ALU.mult, op1=ALU.mult),
                reads=[qb, "s_qnw_s", QB], writes=[qb])
        if debug:
            p.dma("sync", d_q[:, h, :], qT_ring[:, sl, :], reads=[qb], stream="dbg", shared=True)
        return qb

    def emit_sw(h, wb, qb):
        sl = h % 2
        g, hr = h // 4, h % 4
        gsl = g % 2
        B0, B1, B2 = (0, 1, 2)
        ebuf = f"Esw{sl}"
        for kt in range(4):
            bank = B0 + kt // 2
            off = 256 * (kt % 2)
            p.mm_group(PB[bank],
                       [(pb[bank][:, off:off + 256],
                         kTt[:, 256 * wb + 128 * kt:256 * wb + 128 * kt + 128],
                         qT_ring[:, sl, WIN * wb:WIN * (wb + 1)])],
                       reads=["kT", qb])
            p.emit("scalar", lambda e, b=bank, off=off, kt=kt, sl=sl: e.activation(
                Esw[:, sl, kt, :], pb[b][:, off:off + 256], AF.Exp, bias=s_negC[:, :]),
                reads=[PB[bank], "s_negC"], writes=[ebuf])
            p.emit("vector", lambda e, kt=kt, wb=wb, sl=sl: e.tensor_tensor(
                Esw[:, sl, kt, :], Esw[:, sl, kt, :], swm[:, 4 * wb + kt, :], ALU.mult),
                reads=[ebuf, "swm"], writes=[ebuf])
        p.mm_group(PB[B2], [(pb[B2][0:1, 0:256], s_onesB[:, :], Esw[:, sl, kt, :])
                            for kt in range(4)], reads=["s_onesB", ebuf])
        p.mm_group(PB[B1], [(pb[B1][:, 256:512], vtok[:, 2 * wb + kt, :], Esw[:, sl, kt, :])
                            for kt in range(4)], reads=["vtok", ebuf])
        p.emit("scalar", lambda e, B2=B2, sl=sl: e.activation(
            nrowf[:, 256 * sl:256 * sl + 256], pb[B2][0:1, 0:256], AF.Ln),
            reads=[PB[B2]], writes=[f"nrowf{sl}"])
        p.emit("scalar", lambda e, sl=sl: e.activation(
            nrowf[:, 256 * sl:256 * sl + 256], nrowf[:, 256 * sl:256 * sl + 256], AF.Exp, scale=-1.0),
            reads=[f"nrowf{sl}"], writes=[f"nrowf{sl}"])
        p.mm_group(PB[B2], [(pb[B2][:, 256:512], s_onesRF[:, :], nrowf[:, 256 * sl:256 * sl + 256])],
                   reads=["s_onesRF", f"nrowf{sl}"])
        p.emit("scalar", lambda e, B1=B1, sl=sl: e.activation(
            numsb[:, 256 * sl:256 * sl + 256], pb[B1][:, 256:512], AF.Identity),
            reads=[PB[B1]], writes=[f"numsb{sl}"])
        p.emit("vector", lambda e, gsl=gsl, hr=hr, wb=wb, B2=B2, sl=sl: e.tensor_tensor(
            attn_g[:, gsl, hr, WIN * wb:WIN * (wb + 1)], numsb[:, 256 * sl:256 * sl + 256],
            pb[B2][:, 256:512], ALU.mult),
            reads=[f"numsb{sl}", PB[B2]], writes=[f"attn{gsl}_{hr}"])

    def emit_sparse(h, ch, qb):
        sl = h % 2
        g, hr = h // 4, h % 4
        gsl = g % 2
        B0, B1, B2 = (0, 1, 2)
        ebuf = f"Esp{sl}"
        for nt in range(4):
            bank = B0 + nt % 2
            p.mm_group(PB[bank], [(pb[bank][:, :], kvnrmT[:, 128 * nt:128 * (nt + 1)],
                                   qT_ring[:, sl, 512 * ch:512 * (ch + 1)])],
                       reads=["kvnrmT", qb])
            p.emit("scalar", lambda e, b=bank, nt=nt, sl=sl: e.activation(
                Esp[:, sl, nt, :], pb[b][:, :], AF.Exp, bias=s_negC[:, :]),
                reads=[PB[bank], "s_negC"], writes=[ebuf])
            p.emit("vector", lambda e, nt=nt, ch=ch, sl=sl: e.tensor_tensor(
                Esp[:, sl, nt, :], Esp[:, sl, nt, :],
                selbinT[:, nt, 512 * ch:512 * (ch + 1)], ALU.mult),
                reads=[ebuf, "selbinT"], writes=[ebuf])
        p.mm_group(PB[B2], [(pb[B2][0:1, :], s_onesB[:, :], Esp[:, sl, nt, :]) for nt in range(4)],
                   reads=["s_onesB", ebuf])
        p.mm_group(PB[B0], [(pb[B0][:, :], kvnrm[:, nt, :], Esp[:, sl, nt, :]) for nt in range(4)],
                   reads=["kvnrm", ebuf])
        p.emit("vector", lambda e, B2=B2, sl=sl: e.tensor_scalar(
            nrowf[:, 512 + 256 * sl:512 + 256 * sl + 256]
            if False else nrowf[:, 512 + 384 * sl:512 + 384 * sl + 384][:, 0:512 - 0][:, 0:512][:, 0:512],
            pb[B2][0:1, :], 1e-30, None, op0=ALU.add),
            reads=[PB[B2]], writes=[f"nrow1{sl}"]) if False else None
        off = 512 + 512 * sl
        p.emit("vector", lambda e, B2=B2, off=off: e.tensor_scalar(
            nrowf[:, off:off + 512], pb[B2][0:1, :], 1e-30, None, op0=ALU.add),
            reads=[PB[B2]], writes=[f"nrow1{sl}"])
        p.emit("scalar", lambda e, off=off: e.activation(nrowf[:, off:off + 512], nrowf[:, off:off + 512], AF.Ln),
               reads=[f"nrow1{sl}"], writes=[f"nrow1{sl}"])
        p.emit("scalar", lambda e, off=off: e.activation(nrowf[:, off:off + 512], nrowf[:, off:off + 512], AF.Exp, scale=-1.0),
               reads=[f"nrow1{sl}"], writes=[f"nrow1{sl}"])
        p.mm_group(PB[B2], [(pb[B2][:, :], s_onesRF[:, :], nrowf[:, off:off + 512])],
                   reads=["s_onesRF", f"nrow1{sl}"])
        p.emit("scalar", lambda e, B0=B0, sl=sl: e.activation(
            numsb[:, 512 + 512 * 0:1024] if False else spt[:, :], pb[B0][:, :], AF.Identity),
            reads=[PB[B0]], writes=[f"spt{sl}"]) if False else None
        p.emit("scalar", lambda e, B0=B0, sl=sl: e.activation(
            spt[:, sl, :], pb[B0][:, :], AF.Identity),
            reads=[PB[B0]], writes=[f"spt{sl}"])
        p.emit("vector", lambda e, B2=B2, sl=sl: e.tensor_tensor(
            spt[:, sl, :], spt[:, sl, :], pb[B2][:, :], ALU.mult),
            reads=[f"spt{sl}", PB[B2]], writes=[f"spt{sl}"])
        p.emit("vector", lambda e, gsl=gsl, hr=hr, ch=ch, sl=sl: e.tensor_tensor(
            attn_g[:, gsl, hr, 512 * ch:512 * (ch + 1)],
            attn_g[:, gsl, hr, 512 * ch:512 * (ch + 1)], spt[:, sl, :], ALU.add),
            reads=[f"spt{sl}", f"attn{gsl}_{hr}"], writes=[f"attn{gsl}_{hr}"])

    gw_str = [p.stream("gw")]
    gd_str = [p.stream("gd"), p.stream("gd")]

    def emit_group(g):
        gsl = g % 2
        gwsl = 0
        deps = [f"attn{gsl}_{hr}" for hr in range(4)]
        p.dma("gpsimd", gw_ring[:, 0, :, :],
              group_w[g, :, :].rearrange("(kt p) m -> p kt m", p=128),
              buf="gwR", stream=gw_str[0])
        for mt in range(4):
            for ncl in range(2):
                bank = ncl
                p.mm_group(PB[bank], [(pb[bank][:, :], gw_ring[:, 0, kt, 128 * mt:128 * (mt + 1)],
                                       attn_g[:, gsl, kt, 512 * ncl:512 * (ncl + 1)])
                                      for kt in range(4)],
                           reads=["gwR"] + deps)
                gslot = (mt + ncl) % 2
                p.emit("scalar", lambda e, b=bank, gslot=gslot: e.activation(
                    grp_tile[:, gslot, :], pb[b][:, :], AF.Identity),
                    reads=[PB[bank]], writes=[f"grpt{gslot}"])
                p.dma("sync", grp_dram[128 * (4 * g + mt):128 * (4 * g + mt + 1),
                                       512 * ncl:512 * (ncl + 1)],
                      grp_tile[:, gslot, :], buf=f"grouped{g}",
                      reads=[f"grpt{gslot}"], stream=gd_str[gslot])

    for h in range(H):
        qb = emit_qhead(h)
        for wb in range(4):
            emit_sw(h, wb, qb)
        for ch in range(2):
            emit_sparse(h, ch, qb)
        if debug:
            gsl, hr = (h // 4) % 2, h % 4
            p.dma("sync", d_attn[:, h, :], attn_g[:, gsl, hr, :], reads=[f"attn{gsl}_{hr}"], stream="dbg", shared=True)
        if h % 4 == 3:
            emit_group(h // 4)
        pump(30)
        fpump(4 if h % 8 < 3 else 3)

    fpump_all()

    if debug:
        d_hint = dout("dbg_hint", (128, NIT, 512), BF16)
        hdeps = []
        for it in range(NIT):
            hdeps.extend(p.acc_deps(f"hd_{it}_1"))
        p.dma("sync", d_hint[:, :, :],
              hint_dram[:, 512:1024].rearrange("(it p) m -> p it m", p=128),
              deps=hdeps, stream="dbg", shared=True)

    at_st.close()
    sel_st.close()
    gw_st.close()
    qc_st.close()


    pump_all()
    _join2 = p.join_tokens(
        ["qh0", "qh1", "q_scr", "q_row", "Esw0", "Esw1", "Esp0", "Esp1",
         "numsb0", "numsb1", "spt", "gwR", "grpt0", "grpt1",
         "nrowf0", "nrowf1", "nrow10", "nrow11", "spt0", "spt1", "fw0", "fw1", "grd0", "grd1",
         "sTt0", "sTt1", "selbinT", "swm", "s_wqup", "AT",
         "wgug", "wguu", "gsil2", "hsb0", "hsb1"]
        + [f"attn{g}_{h}" for g in range(2) for h in range(4)])
    p.seed(["wd00", "wd01", "wd10", "wd11", "gsil_0", "gsil_1", "sg0", "sg1",
            "sTr0", "sTr1", "xf0", "xf1", "o0", "o1",
            "hin0", "hin1", "hin2", "hin3"], _join2)

    p.seed([f"fw{i}" for i in range(6)] + [f"grd{i}" for i in range(6)]
           + ["sTt0", "sTt1"], _join2)

    fin_st = contextlib.ExitStack()
    fw_ring = sb(fin_st, "fw_ring", (128, 6, 1024))
    grd_ring = sb(fin_st, "grd_ring", (128, 6, 512))
    sT_tile = sb(fin_st, "sT_tile", (128, 2, 512))
    fw_str = [p.stream("fw") for _ in range(6)]
    gr_str = [p.stream("gr") for _ in range(6)]
    st_str = [p.stream("st"), p.stream("st")]
    for sc in range(2):
        for pass_ in range(2):
            o0 = 8 * pass_
            for kt in range(16):
                sl = kt % 6
                fsl = kt % 6
                p.dma("scalar", fw_ring[:, fsl, :],
                      fw_bf[128 * kt:128 * (kt + 1), 128 * o0:128 * (o0 + 8)],
                      buf=f"fw{fsl}", reads=["fw_bf"], stream=fw_str[fsl])
                p.dma("sync", grd_ring[:, sl, :],
                      grp_dram[128 * kt:128 * (kt + 1), 512 * sc:512 * (sc + 1)],
                      buf=f"grd{sl}", reads=[f"grouped{kt // 4}"], stream=gr_str[sl])
                for oi in range(8):
                    bname = PB[oi]
                    dep = p.acc_deps(bname) if kt == 0 else []
                    tok = p.emit("tensor",
                                 lambda e, oi=oi, sl=sl, fsl=fsl,
                                 st=(kt == 0), sp=(kt == 15):
                                 e.matmul(pb[oi][:, :], fw_ring[:, fsl, 128 * oi:128 * (oi + 1)],
                                          grd_ring[:, sl, :],
                                          start=st, stop=sp),
                                 reads=[f"fw{fsl}", f"grd{sl}"],
                                 deps=dep)
                    if kt == 15:
                        p.set_write(bname, tok)
            for oi in range(8):
                ot = o0 + oi
                bname = PB[oi]
                sl2 = oi % 2
                p.emit("scalar", lambda e, oi=oi, sl2=sl2: e.activation(
                    sT_tile[:, sl2, :], pb[oi][:, :], AF.Identity),
                    reads=[bname], writes=[f"sTt{sl2}"])
                p.dma("sync", sT_dram[128 * ot:128 * (ot + 1), 512 * sc:512 * (sc + 1)],
                      sT_tile[:, sl2, :], buf=f"sTd_{ot}_{sc}", reads=[f"sTt{sl2}"],
                      stream=st_str[sl2])
    if debug:
        d_sT = dout("dbg_sT", (D, TC), BF16)
        for ot in range(16):
            for sc in range(2):
                p.dma("sync", d_sT[128 * ot:128 * (ot + 1), 512 * sc:512 * (sc + 1)],
                      sT_dram[128 * ot:128 * (ot + 1), 512 * sc:512 * (sc + 1)],
                      deps=p.acc_deps(f"sTd_{ot}_{sc}"), stream="dbg", shared=True)
    fin_st.close()
    ffn_st = contextlib.ExitStack()
    wd_ring = sb(ffn_st, "wd_ring", (128, 2, 2, 11, 256))
    hin_ring = sb(ffn_st, "hin_ring", (128, 4, 1024))
    gsil = sb(ffn_st, "gsil", (128, 2, 512))
    sg_t = sb(ffn_st, "sg_t", (128, 2, 512))
    sTr = sb(ffn_st, "sTr", (128, 8, 512))
    xf_ring = sb(ffn_st, "xf_ring", (128, 8, 512), F32)
    o_ring = sb(ffn_st, "o_ring", (128, 2, 512), F32)

    wd_str = {(kp, ri): p.stream("wd") for kp in range(2) for ri in range(2)}
    hin_str = [p.stream("hi") for _ in range(4)]
    sTr_str = [p.stream("sr") for _ in range(8)]
    xf_str = [p.stream("x2") for _ in range(8)]
    out_str = [p.stream("ou"), p.stream("ou")]
    out_toks = []

    _join2b = p.join_tokens([f"fw{i}" for i in range(6)] + [f"grd{i}" for i in range(6)]
                        + ["sTt0", "sTt1"])
    p.seed(["wd00", "wd01", "wd10", "wd11", "gsil_0", "gsil_1", "sg0", "sg1",
            "o0", "o1", "hin0", "hin1", "hin2", "hin3"]
           + [f"sTr{i}" for i in range(8)] + [f"xf{i}" for i in range(8)], _join2b)

    for og4 in range(4):
        ranks = (2 * og4, 2 * og4 + 1)
        first_mm = [True] * 8
        for sh in range(2):
            xcol = HALO + 512 * sh
            for ri in range(2):
                for half in range(2):
                    oi = 4 * sh + 2 * ri + half
                    ot = 4 * og4 + 2 * ri + half
                    p.dma("sync", sTr[:, oi, :],
                          sT_dram[128 * ot:128 * (ot + 1), 512 * sh:512 * (sh + 1)],
                          buf=f"sTr{oi}", deps=p.acc_deps(f"sTd_{ot}_{sh}"), stream=sTr_str[oi])
                    p.dma("sync", xf_ring[:, oi, :], xT[128 * ot:128 * (ot + 1), xcol:xcol + 512],
                          buf=f"xf{oi}", stream=xf_str[oi])
        for k in range(NCHUNK):
            kp = k % 2
            tiles = CHUNK_TILES[k]
            for ri, r in enumerate(ranks):
                p.dma("scalar" if ri == 0 else "sync", wd_ring[:, kp, ri, 0:tiles, :],
                      wd_out[k][r * tiles * 128:(r + 1) * tiles * 128, :]
                      .rearrange("(it p) m -> p it m", p=128),
                      buf=f"wd{kp}{ri}", deps=[cc_toks[f"d{k}"]], stream=wd_str[(kp, ri)])
            for it_l in range(tiles):
                it = CHUNK_OFF[k] + it_l
                hsl = it % 4
                p.dma("scalar", hin_ring[:, hsl, :],
                      hint_dram[128 * it:128 * (it + 1), :],
                      buf=f"hin{hsl}",
                      deps=p.acc_deps(f"hd_{it}_0") + p.acc_deps(f"hd_{it}_1"),
                      stream=hin_str[hsl])
                last = (k == NCHUNK - 1) and (it_l == tiles - 1)
                for sh in range(2):
                    for ri in range(2):
                        for half in range(2):
                            oi = 4 * sh + 2 * ri + half
                            bname = PB[oi]
                            st = first_mm[oi]
                            dep = p.acc_deps(bname) if st else []
                            if st:
                                first_mm[oi] = False
                            tok = p.emit("tensor",
                                         lambda e, oi=oi, kp=kp, ri=ri, it_l=it_l, hsl=hsl,
                                         half=half, sh=sh, st=st, sp=last:
                                         e.matmul(pb[oi][:, :],
                                                  wd_ring[:, kp, ri, it_l, 128 * half:128 * (half + 1)],
                                                  hin_ring[:, hsl, 512 * sh:512 * (sh + 1)],
                                                  start=st, stop=sp),
                                         reads=[f"wd{kp}{ri}", f"hin{hsl}"],
                                         deps=dep)
                            if last:
                                p.set_write(bname, tok)
        for sh in range(2):
            for ri in range(2):
                for half in range(2):
                    oi = 4 * sh + 2 * ri + half
                    ot = 4 * og4 + 2 * ri + half
                    sl2 = oi % 2
                    p.emit("scalar", lambda e, oi=oi, sl2=sl2: e.activation(
                        gsil[:, sl2, :], pb[oi][:, :], AF.Silu),
                        reads=[PB[oi]], writes=[f"gsil_{sl2}"])
                    p.emit("vector", lambda e, oi=oi, sl2=sl2: e.tensor_tensor(
                        sg_t[:, sl2, :], gsil[:, sl2, :], sTr[:, oi, :], ALU.mult),
                        reads=[f"gsil_{sl2}", f"sTr{oi}"], writes=[f"sg{sl2}"])
                    p.emit("vector", lambda e, oi=oi, sl2=sl2: e.tensor_tensor(
                        o_ring[:, sl2, :], xf_ring[:, oi, :], sg_t[:, sl2, :], ALU.add),
                        reads=[f"xf{oi}", f"sg{sl2}"], writes=[f"o{sl2}"])
                    tok = p.dma("sync", outT[128 * ot:128 * (ot + 1), 512 * sh:512 * (sh + 1)],
                                o_ring[:, sl2, :], buf=f"od_{ot}_{sh}", reads=[f"o{sl2}"],
                                stream=out_str[sl2])
                    out_toks.append(tok)

    final_deps = list(out_toks)
    if "dbg" in p.streams:
        final_deps.append(("d_dbg", p.streams["dbg"]))
    p.emit("gpsimd", lambda e: e.nop(), deps=final_deps)
    ffn_st.close()

    p.build()
    top.close()
    return nc, dbg



def _host_inputs(full):
    x = np.asarray(full["L0_attn_h"])[0]
    T = x.shape[0]
    inv = (1.0 / (10000.0 ** (np.arange(0, 64, 2, dtype=np.float64) / 64.0)))
    gate_idxT = np.ascontiguousarray(np.asarray(full["L0_ffn_gate_idx"]).T)
    up_idxT = np.ascontiguousarray(np.asarray(full["L0_ffn_up_idx"]).T)
    down_idxT = np.ascontiguousarray(np.asarray(full["L0_ffn_down_idx"]).T)
    ones_colB = np.ones((128, 1), dtype=BF)
    ones_row_f = np.ones((1, 128), dtype=np.float32)
    identB = np.eye(128, dtype=BF)
    identF = np.eye(128, dtype=np.float32)
    eps_tie = np.tile((-(np.arange(NB, dtype=np.float64) + 1.0) * 1e-25).astype(np.float32)[None, :],
                      (128, 1))
    kvc_biasT = np.ascontiguousarray(np.asarray(full["L0_attn_kvc_bias"]).T).astype(np.float32)
    idx_biasT = np.ascontiguousarray(np.asarray(full["L0_attn_idx_c_bias"]).T).astype(np.float32)

    in_maps = []
    for c in range(NCORE):
        t0 = TC * c
        gpos = np.arange(t0 - HALO, t0 + TC)
        xa = np.zeros((TCH, D), dtype=np.float32)
        lo = max(0, t0 - HALO)
        xa[HALO - (t0 - lo):] = x[lo:t0 + TC]
        ang = gpos[:, None].astype(np.float64) * inv[None, :]
        ropeCc = np.ascontiguousarray(np.cos(ang).T.astype(np.float32))
        ropeSc = np.ascontiguousarray(np.sin(ang).T.astype(np.float32))
        cadd = np.zeros((8, 128, NB), dtype=BF)
        cbin = np.zeros((8, 128, NB), dtype=BF)
        narr = np.arange(NB)
        for tt in range(8):
            tg = t0 + 128 * tt + np.arange(128)
            valid = (16 * narr[None, :] + 15) < tg[:, None]
            cadd[tt] = np.where(valid, np.float32(0.0), np.float32(-1e30)).astype(BF)
            cbin[tt] = valid.astype(BF)
        swm = np.zeros((4, 4, 128, WIN), dtype=BF)
        for wb in range(4):
            wbg = 4 * c + wb
            j = np.arange(512)
            tq = np.arange(WIN)
            ok = ((j[:, None] - tq[None, :]) > 0) & ((j[:, None] - tq[None, :]) <= WIN)
            if wbg == 0:
                ok &= (j[:, None] >= WIN)
            okb = ok.astype(BF)
            for kt in range(4):
                swm[wb, kt] = okb[128 * kt:128 * (kt + 1), :]
        m = {
            "xT": np.ascontiguousarray(xa.T),
            "xN": xa,
            "ropeC": ropeCc,
            "ropeS": ropeSc,
            "causal_add": cadd,
            "causal_bin": cbin,
            "sw_mask": swm,
            "eps_tie": eps_tie,
            "ones_colB": np.asarray(ones_colB),
            "ones_row_f": ones_row_f,
            "identB": np.asarray(identB),
            "identF": identF,
            "w_qc": np.asarray(full["L0_attn_w_qc"], dtype=np.float32),
            "w_qup": np.asarray(full["L0_attn_w_qup"], dtype=np.float32),
            "kvc_w": np.asarray(full["L0_attn_kvc_w"], dtype=np.float32),
            "kvc_wz": np.asarray(full["L0_attn_kvc_wz"], dtype=np.float32),
            "kvc_biasT": kvc_biasT,
            "k_proj_w": np.asarray(full["L0_attn_k_proj_w"], dtype=np.float32),
            "v_proj_w": np.asarray(full["L0_attn_v_proj_w"], dtype=np.float32),
            "idx_c_w": np.asarray(full["L0_attn_idx_c_w"], dtype=np.float32),
            "idx_c_wz": np.asarray(full["L0_attn_idx_c_wz"], dtype=np.float32),
            "idx_c_biasT": idx_biasT,
            "w_dq": np.asarray(full["L0_attn_w_dq"], dtype=np.float32),
            "w_iuq": np.asarray(full["L0_attn_w_iuq"], dtype=np.float32),
            "w_w": np.asarray(full["L0_attn_w_w"], dtype=np.float32),
            "q_norm_w": np.asarray(full["L0_attn_q_norm_w"], dtype=np.float32).reshape(HD, 1),
            "k_norm_w": np.asarray(full["L0_attn_k_norm_w"], dtype=np.float32).reshape(HD, 1),
            "group_w": np.asarray(full["L0_attn_group_w"], dtype=np.float32),
            "final_w": np.asarray(full["L0_attn_final_w"], dtype=np.float32),
            "lut": np.asarray(full["L0_ffn_lut"], dtype=np.float32).reshape(1, LUTN),
            "gidxT": np.ascontiguousarray(gate_idxT[256 * c:256 * (c + 1), :]),
            "uidxT": np.ascontiguousarray(up_idxT[256 * c:256 * (c + 1), :]),
            "didxT": np.ascontiguousarray(down_idxT[:, 256 * c:256 * (c + 1)]),
        }
        in_maps.append(m)
    return in_maps


_CACHED = {}


def kernel(**inputs):
    if "nc" not in _CACHED:
        nc, _ = build_program(debug=False)
        _CACHED["nc"] = nc
    nc = _CACHED["nc"]
    in_maps = _host_inputs(inputs)
    res = run_bass_kernel_spmd(nc, in_maps, core_ids=list(range(NCORE)))
    parts = [np.asarray(res.results[c]["outT"]).T for c in range(NCORE)]
    out = np.concatenate(parts, axis=0)[None, :, :]
    return out.astype(np.float32)


# version 18
# speedup vs baseline: 1.1380x; 1.0126x over previous
import math
import numpy as np
import ml_dtypes
import concourse.bass as bass
import concourse.mybir as mybir
from concourse.bass_utils import run_bass_kernel_spmd

F32 = mybir.dt.float32
BF16 = mybir.dt.bfloat16
I32 = mybir.dt.int32
AF = mybir.ActivationFunctionType
ALU = mybir.AluOpType
AXL = mybir.AxisListType
BF = ml_dtypes.bfloat16

NCORE = 8
TC = 1024
HALO = 256
TCH = TC + HALO
D = 2048
H, HD = 16, 128
QC = 512
M = 16
NB = 512
NBC = 64
NIH, CI = 4, 64
G, DG = 4, 512
WIN = 256
EPSA = 1e-6
EPSLN = 1e-5
INTER = 5504
NIT = 43
LUTN = 16
SCALE = 1.0 / math.sqrt(HD)
CHUNK_TILES = (11, 11, 11, 10)
CHUNK_OFF = (0, 11, 22, 33)
NCHUNK = 4
SEGS = ((0, 512), (512, 512), (1024, 256))

ENGINES = ("tensor", "vector", "scalar", "gpsimd", "sync")


class P:

    def __init__(self, nc):
        self.nc = nc
        self.ops = {e: [] for e in ENGINES}
        self.cnt = {e: 0 for e in ENGINES}
        self.streams = {}
        self.last_w = {}
        self.readers = {}
        self.pre = {}
        self._sn = 0

    def stream(self, pref="s"):
        self._sn += 1
        return f"{pref}{self._sn}"

    def emit(self, eng, fn, reads=(), writes=(), deps=(), stream=None, inc=None, shared=False):
        dep = list(deps)
        for b in reads:
            t = self.last_w.get(b)
            if t is not None:
                dep.append(t)
        for b in writes:
            t = self.last_w.get(b)
            if t is not None:
                dep.append(t)
            dep.extend(self.readers.get(b, ()))
            dep.extend(self.pre.pop(b, ()))
        rdep = []
        for (dk, dv) in dep:
            if dv is None:
                dv = self.streams[dk[2:]]
            rdep.append((dk, dv))
        dep = rdep
        if stream is None:
            key = "c_" + eng
            self.cnt[eng] += 1
            tok = (key, self.cnt[eng])
            use_inc = 1
        else:
            key = "d_" + stream
            use_inc = 16 if inc is None else inc
            v = self.streams.get(stream, 0) + use_inc
            self.streams[stream] = v
            tok = (key, None) if shared else (key, v)
        self.ops[eng].append((fn, tuple(dep), key, use_inc))
        for b in reads:
            self.readers.setdefault(b, []).append(tok)
        for b in writes:
            self.last_w[b] = tok
            self.readers[b] = []
        return tok

    def dma(self, eng, dst, src, buf=None, reads=(), deps=(), stream=None, shared=False):
        st = stream or self.stream("q")
        wr = [buf] if buf else []
        return self.emit(eng, lambda e, d=dst, s=src: e.dma_start(out=d, in_=s),
                         reads=reads, writes=wr, deps=deps, stream=st, shared=shared)

    def join_tokens(self, bufs):
        out = []
        for b in bufs:
            t = self.last_w.get(b)
            if t is not None:
                out.append(t)
            out.extend(self.readers.get(b, ()))
        seen = set()
        res = []
        for t in out:
            if t not in seen:
                seen.add(t)
                res.append(t)
        return res

    def seed(self, bufs_to, tokens):
        for b in bufs_to:
            self.pre.setdefault(b, []).extend(tokens)

    def acc_deps(self, buf):
        dep = []
        t = self.last_w.get(buf)
        if t is not None:
            dep.append(t)
        dep.extend(self.readers.get(buf, ()))
        return dep

    def set_write(self, buf, tok):
        self.last_w[buf] = tok
        self.readers[buf] = []

    def add_read(self, buf, tok):
        self.readers.setdefault(buf, []).append(tok)

    def mm_group(self, bank_buf, mlist, reads, deps=()):
        dep = list(deps) + self.acc_deps(bank_buf)
        for b in reads:
            t = self.last_w.get(b)
            if t is not None:
                dep.append(t)
        n = len(mlist)
        tok = None
        for i, (o, l, r) in enumerate(mlist):
            tok = self.emit("tensor",
                            lambda e, o=o, l=l, r=r, st=(i == 0), sp=(i == n - 1):
                            e.matmul(o, l, r, start=st, stop=sp),
                            deps=dep if i == 0 else [])
            dep = []
        for b in reads:
            self.add_read(b, tok)
        self.set_write(bank_buf, tok)
        return tok

    def build(self):
        nc = self.nc
        import contextlib
        stack = contextlib.ExitStack()
        sems = {}
        for e in ENGINES:
            sems["c_" + e] = stack.enter_context(nc.semaphore("c_" + e))
        for s in self.streams:
            sems["d_" + s] = stack.enter_context(nc.semaphore("d_" + s))
        blk = stack.enter_context(nc.Block())
        for e in ENGINES:
            ops = self.ops[e]
            if not ops:
                continue

            def make(engine_ops):
                def body(eng):
                    waited = {}
                    for fn, deps, key, inc in engine_ops:
                        for (dk, dv) in deps:
                            if waited.get(dk, -1) >= dv:
                                continue
                            eng.wait_ge(sems[dk], dv)
                            waited[dk] = dv
                        fn(eng).then_inc(sems[key], inc)
                return body
            getattr(blk, e)(make(ops))
        stack.close()


def build_program(debug=False):
    nc = bass.Bass("TRN2", target_bir_lowering=False)
    import contextlib

    def din(name, shape, dt=F32):
        return nc.dram_tensor(name, list(shape), dt, kind="ExternalInput").ap()

    xT = din("xT", (D, TCH))
    xN = din("xN", (TCH, D))
    ropeC = din("ropeC", (32, TCH))
    ropeS = din("ropeS", (32, TCH))
    causal_add = din("causal_add", (8, 128, NB), BF16)
    causal_bin = din("causal_bin", (8, 128, NB), BF16)
    sw_mask = din("sw_mask", (4, 4, 128, WIN), BF16)
    eps_tie = din("eps_tie", (128, NB))
    ones_colB = din("ones_colB", (128, 1), BF16)
    ones_row_f = din("ones_row_f", (1, 128))
    identB = din("identB", (128, 128), BF16)
    identF = din("identF", (128, 128))
    w_qc = din("w_qc", (D, QC))
    w_qup = din("w_qup", (QC, D))
    kvc_w = din("kvc_w", (D, HD))
    kvc_wz = din("kvc_wz", (D, HD))
    kvc_biasT = din("kvc_biasT", (HD, M))
    k_proj_w = din("k_proj_w", (D, HD))
    v_proj_w = din("v_proj_w", (D, HD))
    idx_c_w = din("idx_c_w", (D, CI))
    idx_c_wz = din("idx_c_wz", (D, CI))
    idx_c_biasT = din("idx_c_biasT", (CI, M))
    w_dq = din("w_dq", (D, CI))
    w_iuq = din("w_iuq", (CI, NIH * CI))
    w_w = din("w_w", (D, NIH))
    q_norm_w = din("q_norm_w", (HD, 1))
    k_norm_w = din("k_norm_w", (HD, 1))
    group_w = din("group_w", (G, DG, DG))
    final_w = din("final_w", (D, D))
    lut = din("lut", (1, LUTN))
    gidxT = din("gidxT", (256, INTER), I32)
    uidxT = din("uidxT", (256, INTER), I32)
    didxT = din("didxT", (INTER, 256), I32)

    outT = nc.dram_tensor("outT", [D, TC], F32, kind="ExternalOutput").ap()

    dbg = {}

    def dout(name, shape, dt):
        t = nc.dram_tensor(name, list(shape), dt, kind="ExternalOutput").ap()
        dbg[name] = t
        return t

    cc_cmp_in = nc.dram_tensor("cc_cmp_in", [NBC, HD + CI], F32).ap()
    cc_cmp_out = nc.dram_tensor("cc_cmp_out", [NB, HD + CI], F32).ap()
    wg_in = [nc.dram_tensor(f"wg_in{k}", [256, CHUNK_TILES[k] * 128], BF16).ap() for k in range(NCHUNK)]
    wu_in = [nc.dram_tensor(f"wu_in{k}", [256, CHUNK_TILES[k] * 128], BF16).ap() for k in range(NCHUNK)]
    wd_in = [nc.dram_tensor(f"wd_in{k}", [CHUNK_TILES[k] * 128, 256], BF16).ap() for k in range(NCHUNK)]
    wg_out = [nc.dram_tensor(f"wg_out{k}", [D, CHUNK_TILES[k] * 128], BF16).ap() for k in range(NCHUNK)]
    wu_out = [nc.dram_tensor(f"wu_out{k}", [D, CHUNK_TILES[k] * 128], BF16).ap() for k in range(NCHUNK)]
    wd_out = [nc.dram_tensor(f"wd_out{k}", [NCORE * CHUNK_TILES[k] * 128, 256], BF16).ap() for k in range(NCHUNK)]
    sT_dram = nc.dram_tensor("sT_dram", [D, TC], BF16).ap()
    grp_dram = nc.dram_tensor("grp_dram", [D, TC], BF16).ap()
    fw_bf = nc.dram_tensor("fw_bf", [D, D], BF16).ap()
    wqup_bf = nc.dram_tensor("wqup_bf", [QC, D], BF16).ap()
    hint_dram = nc.dram_tensor("hint_dram", [INTER, TC], BF16).ap()
    stat_dram = [nc.dram_tensor(f"stat_dram{i}", [TCH], F32).ap() for i in range(2)]

    p = P(nc)
    top = contextlib.ExitStack()

    def sb(st, name, shape, dt=BF16):
        return st.enter_context(nc.sbuf_tensor(name, list(shape), dt))

    xnT = sb(top, "xnT", (128, 16, TCH))
    kTt = sb(top, "kTt", (128, TCH))
    vtok = sb(top, "vtok", (128, 10, HD))
    s_kvcw = sb(top, "s_kvcw", (128, 16, HD))
    s_kvcwz = sb(top, "s_kvcwz", (128, 16, HD))
    s_kproj = sb(top, "s_kproj", (128, 16, HD))
    s_vproj = sb(top, "s_vproj", (128, 16, HD))
    s_idxw = sb(top, "s_idxw", (128, 16, CI))
    s_idxwz = sb(top, "s_idxwz", (128, 16, CI))
    s_wdq = sb(top, "s_wdq", (128, 16, CI))
    s_wiuq = sb(top, "s_wiuq", (64, NIH * CI))
    s_ww = sb(top, "s_ww", (128, 16, NIH))
    s_kvcbT = sb(top, "s_kvcbT", (128, M), F32)
    s_idxbT = sb(top, "s_idxbT", (64, M), F32)
    s_qnw_s = sb(top, "s_qnw_s", (128, 1), F32)
    s_knw = sb(top, "s_knw", (128, 1), F32)
    s_knwb = sb(top, "s_knwb", (128, HD), F32)
    s_knwrow = sb(top, "s_knwrow", (1, 128), F32)
    s_onesB = sb(top, "s_onesB", (128, 1), BF16)
    s_onesRF = sb(top, "s_onesRF", (1, 128), F32)
    s_identB = sb(top, "s_identB", (128, 128), BF16)
    s_identF = sb(top, "s_identF", (128, 128), F32)
    s_lutrow = sb(top, "s_lutrow", (1, LUTN), F32)
    s_lutpp = sb(top, "s_lutpp", (128, LUTN), F32)
    s_CC = sb(top, "s_CC", (64, TCH))
    s_SS = sb(top, "s_SS", (64, TCH))
    s_negC = sb(top, "s_negC", (128, 1), F32)
    s_epsc = sb(top, "s_epsc", (128, 2), F32)
    s_scal = sb(top, "s_scal", (1, 8), F32)
    rope_scr = sb(top, "rope_scr", (64, TCH))
    kvnrm = sb(top, "kvnrm", (128, 4, HD))
    kvnrmT = sb(top, "kvnrmT", (128, NB))
    kcompT = sb(top, "kcompT", (128, NB))
    cc_sb = sb(top, "cc_sb", (64, HD + CI), F32)
    kvtok = sb(top, "kvtok", (128, 4, HD + CI), F32)
    gidx_ring = sb(top, "gidx_ring", (128, 1, 1408))
    gacc = sb(top, "gacc", (128, 1408))
    gtmp = sb(top, "gtmp", (128, 1408))
    gidx_ringG = sb(top, "gidx_ringG", (128, 11, 256))
    gaccG = sb(top, "gaccG", (128, 11, 256))
    gtmpG = sb(top, "gtmpG", (128, 11, 256))

    pb = [top.enter_context(nc.psum_tensor(f"pb{i}", [128, 512], F32)) for i in range(7)]
    pbT = top.enter_context(nc.psum_tensor("pbT", [128, 1024], BF16))
    pb.append(pbT[:, :].bitcast(F32))
    PB = [f"pb{i}" for i in range(7)] + ["pbT"]

    def kt16(w):
        return w.rearrange("(kt p) m -> p kt m", p=128)

    p.dma("gpsimd", s_kvcw[:, :, :], kt16(kvc_w), buf="s_kvcw", stream="clw", shared=True)
    p.dma("gpsimd", s_kvcwz[:, :, :], kt16(kvc_wz), buf="s_kvcwz", stream="clw", shared=True)
    p.dma("gpsimd", s_kproj[:, :, :], kt16(k_proj_w), buf="s_kproj", stream="clw", shared=True)
    p.dma("gpsimd", s_vproj[:, :, :], kt16(v_proj_w), buf="s_vproj", stream="clw", shared=True)
    p.dma("gpsimd", s_idxw[:, :, :], kt16(idx_c_w), buf="s_idxw", stream="clw", shared=True)
    p.dma("gpsimd", s_idxwz[:, :, :], kt16(idx_c_wz), buf="s_idxwz", stream="clw", shared=True)
    p.dma("gpsimd", s_wdq[:, :, :], kt16(w_dq), buf="s_wdq", stream="clw", shared=True)
    p.dma("gpsimd", s_wiuq[:, :], w_iuq[:, :], buf="s_wiuq", stream="clw", shared=True)
    p.dma("gpsimd", s_ww[:, :, :], kt16(w_w), buf="s_ww", stream="clw", shared=True)
    p.dma("gpsimd", s_onesB[:, :], ones_colB[:, :], buf="s_onesB", stream="clg", shared=True)
    p.dma("gpsimd", s_identB[:, :], identB[:, :], buf="s_identB", stream="clg", shared=True)
    p.dma("sync", s_identF[:, :], identF[:, :], buf="s_identF", stream="cls", shared=True)
    p.dma("sync", s_kvcbT[:, :], kvc_biasT[:, :], buf="s_kvcbT", stream="cls", shared=True)
    p.dma("sync", s_idxbT[:, :], idx_c_biasT[:, :], buf="s_idxbT", stream="cls", shared=True)
    p.dma("sync", s_onesRF[:, :], ones_row_f[:, :], buf="s_onesRF", stream="cls", shared=True)
    p.dma("sync", s_lutrow[:, :], lut[:, :], buf="s_lutrow", stream="cls", shared=True)
    p.dma("gpsimd", s_CC[0:32, :], ropeC[:, :], buf="s_CC", stream="clg", shared=True)
    p.dma("gpsimd", s_CC[32:64, :], ropeC[:, :], buf="s_CC", stream="clg", shared=True)
    p.dma("gpsimd", s_SS[0:32, :], ropeS[:, :], buf="s_SS", stream="clg", shared=True)
    p.dma("gpsimd", s_SS[32:64, :], ropeS[:, :], buf="s_SS", stream="clg", shared=True)
    p.dma("sync", s_qnw_s[:, :], q_norm_w[:, :], buf="s_qnw_s", stream="cls", shared=True)
    p.dma("sync", s_knw[:, :], k_norm_w[:, :], buf="s_knw", stream="cls", shared=True)

    p.emit("vector", lambda e: e.memset(s_epsc[:, 0:1], EPSLN), writes=["s_epsc"])
    p.emit("vector", lambda e: e.memset(s_epsc[:, 1:2], EPSA), writes=["s_epsc"])
    p.mm_group(PB[0], [(pb[0][:, 0:LUTN], s_onesRF[:, :], s_lutrow[:, :])],
               reads=["s_onesRF", "s_lutrow"])
    p.emit("vector", lambda e: e.tensor_copy(s_lutpp[:, :], pb[0][:, 0:LUTN]),
           reads=[PB[0]], writes=["s_lutpp"])
    p.emit("vector", lambda e: e.tensor_scalar(s_qnw_s[:, :], s_qnw_s[:, :], SCALE, None, op0=ALU.mult),
           reads=["s_qnw_s"], writes=["s_qnw_s"])
    p.emit("vector", lambda e: e.tensor_scalar(s_SS[0:32, :], s_SS[0:32, :], -1.0, None, op0=ALU.mult),
           reads=["s_SS"], writes=["s_SS"])
    p.emit("tensor", lambda e: e.transpose(pb[1][0:1, 0:128], s_knw[:, :], s_identF[:, :]),
           reads=["s_knw", "s_identF"], writes=[PB[1]])
    p.emit("vector", lambda e: e.tensor_copy(s_knwrow[:, :], pb[1][0:1, 0:128]),
           reads=[PB[1]], writes=["s_knwrow"])
    p.mm_group(PB[1], [(pb[1][:, 0:128], s_onesRF[:, :], s_knwrow[:, :])],
               reads=["s_onesRF", "s_knwrow"])
    p.emit("vector", lambda e: e.tensor_copy(s_knwb[:, :], pb[1][:, 0:128]),
           reads=[PB[1]], writes=["s_knwb"])
    p.emit("tensor", lambda e: e.transpose(pb[2][0:1, 0:128], s_qnw_s[:, :], s_identF[:, :]),
           reads=["s_qnw_s", "s_identF"], writes=[PB[2]])
    p.emit("vector", lambda e: e.tensor_reduce(s_scal[:, 0:1], pb[2][0:1, 0:128],
                                               axis=AXL.X, op=ALU.max, apply_absolute_value=True),
           reads=[PB[2]], writes=["s_scal"])
    p.emit("vector", lambda e: e.tensor_reduce(s_scal[:, 1:2], s_knwrow[:, :],
                                               axis=AXL.X, op=ALU.max, apply_absolute_value=True),
           reads=["s_knwrow"], writes=["s_scal"])
    p.emit("vector", lambda e: e.tensor_tensor(s_scal[:, 2:3], s_scal[:, 0:1], s_scal[:, 1:2], ALU.mult),
           reads=["s_scal"], writes=["s_scal"])
    p.emit("vector", lambda e: e.tensor_scalar(s_scal[:, 3:4], s_scal[:, 2:3], -134.4, None, op0=ALU.mult),
           reads=["s_scal"], writes=["s_scal"])
    p.mm_group(PB[2], [(pb[2][:, 0:1], s_onesRF[:, :], s_scal[:, 3:4])],
               reads=["s_onesRF", "s_scal"])
    p.emit("vector", lambda e: e.tensor_copy(s_negC[:, :], pb[2][:, 0:1]),
           reads=[PB[2]], writes=["s_negC"])

    def full_barrier():
        skip = ("cc", "gl", "gs", "gv")
        toks = [("c_" + e, p.cnt[e]) for e in ENGINES if p.cnt[e] > 0]
        toks += [("d_" + s, v) for s, v in p.streams.items()
                 if not any(s.startswith(x) for x in skip)]
        for e in ENGINES:
            p.emit(e, lambda en: en.nop(), deps=toks)

    ln = contextlib.ExitStack()
    xnat_ring = sb(ln, "xnat_ring", (128, 2, D), F32)
    xn_stat = sb(ln, "xn_stat", (128, 10, 2), F32)
    xn_rows = sb(ln, "xn_rows", (1, 2, TCH), F32)
    stat_t = sb(ln, "stat_t", (10, 2, 128), F32)
    ln_row = sb(ln, "ln_row", (1, TCH), F32)
    x_ring = sb(ln, "x_ring", (128, 2, TCH), F32)
    ln_t = sb(ln, "ln_t", (128, 2, 512), F32)

    xnstr = [p.stream("xn") for _ in range(4)]
    for i in range(10):
        sl = i % 2
        p.dma("sync", xnat_ring[:, sl, :], xN[128 * i:128 * (i + 1), :],
              buf=f"xnat{sl}", stream=xnstr[i % 4])
        p.emit("vector", lambda e, i=i, sl=sl: e.tensor_reduce(
            xn_stat[:, i, 0:1], xnat_ring[:, sl, :], axis=AXL.X, op=ALU.add),
            reads=[f"xnat{sl}"], writes=["xn_stat"])
        p.emit("scalar", lambda e, i=i, sl=sl: e.activation(
            xnat_ring[:, sl, :], xnat_ring[:, sl, :], AF.Square, accum_out=xn_stat[:, i, 1:2]),
            reads=[f"xnat{sl}"], writes=[f"xnat{sl}", "xn_stat"])
    for j in range(2):
        p.emit("tensor", lambda e, j=j: e.transpose(
            pb[6][0:10, 0:128], xn_stat[:, :, j], s_identF[:, :]),
            reads=["xn_stat", "s_identF"], writes=[PB[6]])
        p.emit("vector", lambda e, j=j: e.tensor_copy(stat_t[:, j, :], pb[6][0:10, 0:128]),
               reads=[PB[6]], writes=[f"stat_t{j}"])
        p.dma("sync", xn_rows[0:1, j, :].rearrange("o (i q) -> o i q", q=128),
              stat_t[:, j, :], buf="xn_rows", reads=[f"stat_t{j}"], stream="xst", shared=True)
    p.emit("vector", lambda e: e.tensor_scalar(
        xn_rows[:, 0, :], xn_rows[:, 0, :], 1.0 / D, None, op0=ALU.mult),
        reads=["xn_rows"], writes=["xn_rows"])
    p.emit("vector", lambda e: e.tensor_scalar(
        xn_rows[:, 1, :], xn_rows[:, 1, :], 1.0 / D, None, op0=ALU.mult),
        reads=["xn_rows"], writes=["xn_rows"])
    p.emit("vector", lambda e: e.tensor_tensor(
        ln_row[:, :], xn_rows[:, 0, :], xn_rows[:, 0, :], ALU.mult),
        reads=["xn_rows"], writes=["ln_row"])
    p.emit("vector", lambda e: e.tensor_tensor(
        xn_rows[:, 1, :], xn_rows[:, 1, :], ln_row[:, :], ALU.subtract),
        reads=["xn_rows", "ln_row"], writes=["xn_rows"])
    p.emit("scalar", lambda e: e.activation(
        xn_rows[:, 1, :], xn_rows[:, 1, :], AF.Ln, bias=s_epsc[0:1, 0:1], scale=1.0),
        reads=["xn_rows", "s_epsc"], writes=["xn_rows"])
    p.emit("scalar", lambda e: e.activation(
        xn_rows[:, 1, :], xn_rows[:, 1, :], AF.Exp, scale=-0.5),
        reads=["xn_rows"], writes=["xn_rows"])
    for si, (o, n) in enumerate(SEGS):
        p.mm_group(PB[si], [(pb[si][:, 0:n], s_onesRF[:, :], xn_rows[:, 0, o:o + n])],
                   reads=["s_onesRF", "xn_rows"])
        p.mm_group(PB[3 + si], [(pb[3 + si][:, 0:n], s_onesRF[:, :], xn_rows[:, 1, o:o + n])],
                   reads=["s_onesRF", "xn_rows"])
    xrstr = [p.stream("xr") for _ in range(2)]
    for dt in range(16):
        sl = dt % 2
        xb = f"xrr{sl}"
        eng = "vector" if dt % 2 == 0 else "vector"
        p.dma("sync", x_ring[:, sl, :], xT[128 * dt:128 * (dt + 1), :],
              buf=xb, stream=xrstr[sl])
        lb = f"ln_t{dt % 2}"
        lt = ln_t[:, dt % 2, :]
        for si, (o, n) in enumerate(SEGS):
            p.emit(eng, lambda e, lt=lt, sl=sl, si=si, o=o, n=n: e.tensor_tensor(
                lt[:, 0:n], x_ring[:, sl, o:o + n], pb[si][:, 0:n], ALU.subtract),
                reads=[xb, PB[si]], writes=[lb])
            p.emit(eng, lambda e, dt=dt, lt=lt, si=si, o=o, n=n: e.tensor_tensor(
                xnT[:, dt, o:o + n], lt[:, 0:n], pb[3 + si][:, 0:n], ALU.mult),
                reads=[lb, PB[3 + si]], writes=[f"xnT{dt}"])
    ln.close()
    XN = [f"xnT{dt}" for dt in range(16)]
    if debug:
        d_xnT = dout("dbg_xnT", (128, 16, TCH), BF16)
        for dt in range(16):
            p.dma("sync", d_xnT[:, dt, :], xnT[:, dt, :], reads=[f"xnT{dt}"], stream="dbg", shared=True)
    full_barrier()

    qc_st = contextlib.ExitStack()
    wqup_ring = sb(qc_st, "wqup_ring", (128, 2, 4, 128))
    AT = sb(qc_st, "AT", (128, 4, TC))
    gw_st = contextlib.ExitStack()
    swm = sb(gw_st, "swm", (128, 16, WIN))
    wgu_ring = sb(gw_st, "wgu_ring", (128, 2, 16, 128))
    hint_sb = sb(gw_st, "hint_sb", (128, 2, 512))
    gsil2 = sb(gw_st, "gsil2", (128, 512))
    sel_st = contextlib.ExitStack()
    selbinT = sb(sel_st, "selbinT", (128, 4, TC))
    p34_st = contextlib.ExitStack()
    s_wqc = sb(p34_st, "s_wqc", (128, 16, QC))
    k_sc = sb(p34_st, "k_sc", (128, TCH))
    k_row = sb(p34_st, "k_row", (1, 512), F32)
    dqT = sb(p34_st, "dqT", (64, TC))
    qiT = sb(p34_st, "qiT", (128, 2, TC))
    wih = sb(p34_st, "wih", (128, 8, NIH), F32)
    p.dma("gpsimd", s_wqc[:, :, :], kt16(w_qc), buf="s_wqc", stream="clw", shared=True)
    p.dma("sync", swm[:, :, :], sw_mask.rearrange("w k p t -> p (w k) t"), buf="swm", stream="cls", shared=True)

    cc_toks = {}

    def emit_cc(name, in_ap, out_ap, dep_bufs):
        tok = p.emit("gpsimd", lambda e, i=in_ap, o=out_ap: e.collective_compute(
            "AllGather", ALU.bypass, replica_groups=[list(range(NCORE))],
            ins=[i.opt()], outs=[o.opt()]),
            reads=list(dep_bufs), writes=[f"cc_{name}"],
            stream=p.stream("cc"), inc=1)
        cc_toks[name] = tok
        return tok

    units = []

    def add_dve_job(mat_i, k, tr):
        cols = CHUNK_TILES[k] * 128
        src = (gidxT if mat_i == 0 else uidxT)
        dst = (wg_in if mat_i == 0 else wu_in)[k]
        sl = 0
        ring_buf = f"gidx{sl}"
        name = f"g{mat_i}_{k}_{tr}"

        def load():
            p.dma("gpsimd", gidx_ring[:, sl, 0:cols],
                  src[128 * tr:128 * (tr + 1), CHUNK_OFF[k] * 128:CHUNK_OFF[k] * 128 + cols],
                  buf=ring_buf, stream=f"gl{sl}")
        units.append(load)
        for v in range(LUTN):
            if v == 0:
                units.append(lambda sl=sl, cols=cols, rb=ring_buf: p.emit(
                    "vector", lambda e: e.tensor_scalar(
                        gacc[:, 0:cols], gidx_ring[:, sl, 0:cols], 0.0,
                        s_lutpp[:, 0:1], op0=ALU.is_equal, op1=ALU.mult),
                    reads=[rb, "s_lutpp"], writes=["gacc"], stream="gv", inc=1))
            else:
                def u(v=v, sl=sl, cols=cols, rb=ring_buf):
                    p.emit("vector", lambda e: e.tensor_scalar(
                        gtmp[:, 0:cols], gidx_ring[:, sl, 0:cols], float(v),
                        s_lutpp[:, v:v + 1], op0=ALU.is_equal, op1=ALU.mult),
                        reads=[rb, "s_lutpp"], writes=["gtmp"], stream="gv", inc=1)
                    p.emit("vector", lambda e: e.tensor_tensor(
                        gacc[:, 0:cols], gacc[:, 0:cols], gtmp[:, 0:cols], ALU.add),
                        reads=["gtmp"], writes=["gacc"], stream="gv", inc=1)
                units.append(u)

        def store():
            p.dma("sync", dst[128 * tr:128 * (tr + 1), :], gacc[:, 0:cols],
                  buf=f"gst_{name}", reads=["gacc"], stream=f"gs{sl}")
        units.append(store)

    def add_down_job(k):
        tiles = CHUNK_TILES[k]
        cols = tiles * 256

        def load():
            p.dma("gpsimd", gidx_ringG[:, 0:tiles, :],
                  didxT[CHUNK_OFF[k] * 128:(CHUNK_OFF[k] + tiles) * 128, :]
                  .rearrange("(it p) m -> p it m", p=128),
                  buf="gidxG", stream="glG")
        units.append(load)
        gv = gidx_ringG[:, :, :].rearrange("p a b -> p (a b)")
        av = gaccG[:, :, :].rearrange("p a b -> p (a b)")
        tv = gtmpG[:, :, :].rearrange("p a b -> p (a b)")
        for v in range(LUTN):
            if v == 0:
                units.append(lambda cols=cols: p.emit(
                    "vector", lambda e: e.tensor_scalar(
                        av[:, 0:cols], gv[:, 0:cols], 0.0,
                        s_lutpp[:, 0:1], op0=ALU.is_equal, op1=ALU.mult),
                    reads=["gidxG", "s_lutpp"], writes=["gaccG"], stream="gv", inc=1))
            else:
                def u(v=v, cols=cols):
                    p.emit("vector", lambda e: e.tensor_scalar(
                        tv[:, 0:cols], gv[:, 0:cols], float(v),
                        s_lutpp[:, v:v + 1], op0=ALU.is_equal, op1=ALU.mult),
                        reads=["gidxG", "s_lutpp"], writes=["gtmpG"], stream="gv", inc=1)
                    p.emit("vector", lambda e: e.tensor_tensor(
                        av[:, 0:cols], av[:, 0:cols], tv[:, 0:cols], ALU.add),
                        reads=["gtmpG"], writes=["gaccG"], stream="gv", inc=1)
                units.append(u)

        def store():
            p.dma("sync", wd_in[k].rearrange("(it p) m -> p it m", p=128),
                  gaccG[:, 0:tiles, :],
                  buf=f"gstG{k}", reads=["gaccG"], stream="gsG")
        units.append(store)

    for k in range(NCHUNK):
        add_down_job(k)
        for mat_i in range(2):
            for tr in range(2):
                add_dve_job(mat_i, k, tr)

        def ccs(k=k):
            emit_cc(f"d{k}", wd_in[k], wd_out[k], [f"gstG{k}"])
            emit_cc(f"g{k}", wg_in[k], wg_out[k], [f"gst_g0_{k}_0", f"gst_g0_{k}_1"])
            emit_cc(f"u{k}", wu_in[k], wu_out[k], [f"gst_g1_{k}_0", f"gst_g1_{k}_1"])
        units.append(ccs)

    _pu = {"i": 0}

    def pump(n):
        done = 0
        while done < n and _pu["i"] < len(units):
            units[_pu["i"]]()
            _pu["i"] += 1
            done += 1

    def pump_all():
        pump(len(units))

    f_units = []
    wgu_str2 = [p.stream("w2"), p.stream("w2")]
    hst_str = [p.stream("hs"), p.stream("hs")]

    def add_f_unit(k, it_l, sh):
        it = CHUNK_OFF[k] + it_l
        xcol = HALO + 512 * sh
        sl = it % 2

        def u():
            gu_dep = [cc_toks[f"g{k}"], cc_toks[f"u{k}"]]
            p.dma("scalar", wgu_ring[:, 0, :, :],
                  wg_out[k][:, 128 * it_l:128 * (it_l + 1)].rearrange("(kt p) m -> p kt m", p=128),
                  buf="wgug", deps=gu_dep, stream=wgu_str2[0])
            p.dma("scalar", wgu_ring[:, 1, :, :],
                  wu_out[k][:, 128 * it_l:128 * (it_l + 1)].rearrange("(kt p) m -> p kt m", p=128),
                  buf="wguu", deps=gu_dep, stream=wgu_str2[1])
            p.mm_group(PB[4], [(pb[4][:, :], wgu_ring[:, 0, kt, :],
                                xnT[:, kt, xcol:xcol + 512]) for kt in range(16)],
                       reads=["wgug"] + XN)
            p.mm_group(PB[5], [(pb[5][:, :], wgu_ring[:, 1, kt, :],
                                xnT[:, kt, xcol:xcol + 512]) for kt in range(16)],
                       reads=["wguu"] + XN)
            p.emit("scalar", lambda e: e.activation(gsil2[:, :], pb[4][:, :], AF.Silu),
                   reads=[PB[4]], writes=["gsil2"])
            p.emit("vector", lambda e, sl=sl: e.tensor_tensor(
                hint_sb[:, sl, :], gsil2[:, :], pb[5][:, :], ALU.mult),
                reads=["gsil2", PB[5]], writes=[f"hsb{sl}"])
            p.dma("sync", hint_dram[128 * it:128 * (it + 1), 512 * sh:512 * (sh + 1)],
                  hint_sb[:, sl, :], buf=f"hd_{it}_{sh}", reads=[f"hsb{sl}"],
                  stream=hst_str[sl])
        f_units.append(u)

    for k in range(NCHUNK):
        for it_l in range(CHUNK_TILES[k]):
            for sh in range(2):
                add_f_unit(k, it_l, sh)

    _fu = {"i": 0}

    def fpump(n):
        done = 0
        while done < n and _fu["i"] < len(f_units):
            f_units[_fu["i"]]()
            _fu["i"] += 1
            done += 1

    def fpump_all():
        fpump(len(f_units))

    cmp_st = contextlib.ExitStack()
    cmp_c = sb(cmp_st, "cmp_c", (128, TC), F32)
    cmp_z = sb(cmp_st, "cmp_z", (128, TC), F32)
    cmp_s1 = sb(cmp_st, "cmp_s1", (128, NBC), F32)
    cmp_s2 = sb(cmp_st, "cmp_s2", (128, NBC), F32)
    kv_own = sb(cmp_st, "kv_own", (128, NBC), F32)
    ic_own = sb(cmp_st, "ic_own", (64, NBC), F32)

    def compress(wsb_c, buf_c, wsb_z, buf_z, bias_sb, bias_buf, pdim, out_sb, out_buf):
        for which, (wsb, wbuf, dst, dbuf) in enumerate(
                [(wsb_c, buf_c, cmp_c, "cmp_c"), (wsb_z, buf_z, cmp_z, "cmp_z")]):
            for nch in range(2):
                bank = nch + 2 * which
                p.mm_group(PB[bank],
                           [(pb[bank][0:pdim, :], wsb[:, kt, :],
                             xnT[:, kt, HALO + 512 * nch:HALO + 512 * (nch + 1)])
                            for kt in range(16)],
                           reads=[wbuf] + XN)
                p.emit("scalar", lambda e, b=bank, d=dst, nch=nch, pdim=pdim: e.activation(
                    d[0:pdim, 512 * nch:512 * (nch + 1)], pb[b][0:pdim, :], AF.Identity),
                    reads=[PB[bank]], writes=[dbuf])
        zv = cmp_z[0:pdim, :].rearrange("p (a b) -> p a b", b=M)
        cv = cmp_c[0:pdim, :].rearrange("p (a b) -> p a b", b=M)
        p.emit("vector", lambda e: e.tensor_tensor(
            zv, zv, bias_sb[0:pdim, :].unsqueeze(1).to_broadcast([pdim, NBC, M]), ALU.add),
            reads=["cmp_z", bias_buf], writes=["cmp_z"])
        p.emit("vector", lambda e: e.tensor_reduce(
            cmp_s1[0:pdim, :], zv, axis=AXL.X, op=ALU.max),
            reads=["cmp_z"], writes=["cmp_s1"])
        p.emit("vector", lambda e: e.tensor_tensor(
            zv, zv, cmp_s1[0:pdim, :].unsqueeze(2).to_broadcast([pdim, NBC, M]), ALU.subtract),
            reads=["cmp_z", "cmp_s1"], writes=["cmp_z"])
        p.emit("scalar", lambda e: e.activation(zv, zv, AF.Exp),
               reads=["cmp_z"], writes=["cmp_z"])
        p.emit("vector", lambda e: e.tensor_reduce(
            cmp_s1[0:pdim, :], zv, axis=AXL.X, op=ALU.add),
            reads=["cmp_z"], writes=["cmp_s1"])
        p.emit("vector", lambda e: e.reciprocal(cmp_s2[0:pdim, :], cmp_s1[0:pdim, :]),
               reads=["cmp_s1"], writes=["cmp_s2"])
        p.emit("vector", lambda e: e.tensor_tensor(zv, zv, cv, ALU.mult),
               reads=["cmp_z", "cmp_c"], writes=["cmp_z"])
        p.emit("vector", lambda e: e.tensor_reduce(
            cmp_s1[0:pdim, :], zv, axis=AXL.X, op=ALU.add),
            reads=["cmp_z"], writes=["cmp_s1"])
        p.emit("vector", lambda e: e.tensor_tensor(
            out_sb, cmp_s1[0:pdim, :], cmp_s2[0:pdim, :], ALU.mult),
            reads=["cmp_s1", "cmp_s2"], writes=[out_buf])

    compress(s_kvcw, "s_kvcw", s_kvcwz, "s_kvcwz", s_kvcbT, "s_kvcbT", 128, kv_own[:, :], "kv_own")
    compress(s_idxw, "s_idxw", s_idxwz, "s_idxwz", s_idxbT, "s_idxbT", 64, ic_own[0:64, :], "ic_own")
    p.emit("tensor", lambda e: e.transpose(pb[4][0:64, 0:128], kv_own[:, :], s_identF[:, :]),
           reads=["kv_own", "s_identF"], writes=[PB[4]])
    p.emit("vector", lambda e: e.tensor_copy(cc_sb[:, 0:HD], pb[4][0:64, 0:128]),
           reads=[PB[4]], writes=["cc_sb"])
    p.emit("tensor", lambda e: e.transpose(pb[4][0:64, 0:64], ic_own[0:64, :], s_identF[0:64, 0:64]),
           reads=["ic_own", "s_identF"], writes=[PB[4]])
    p.emit("vector", lambda e: e.tensor_copy(cc_sb[:, HD:HD + CI], pb[4][0:64, 0:64]),
           reads=[PB[4]], writes=["cc_sb"])
    p.dma("sync", cc_cmp_in[:, :], cc_sb[:, :], buf="cc_cmp_st", reads=["cc_sb"], stream="cls", shared=True)
    emit_cc("cmp", cc_cmp_in, cc_cmp_out, ["cc_cmp_st"])
    cmp_st.close()
    _joinC = p.join_tokens(["cmp_c", "cmp_z", "cmp_s1", "cmp_s2", "kv_own", "ic_own",
                            "kv_sc", "kv_ms", "kct"])
    p.seed(["hsr", "isc", "iwork0", "iwork1", "mx8", "vstar", "selbin",
            "cmA0", "cmA1", "cmB0", "cmB1", "s_epstie"]
           + [f"iscA{tt}" for tt in range(8)], _joinC)

    p.dma("gpsimd", wqup_bf[:, :], w_qup[:, :], buf="wqup_bf", stream="glq", shared=True)
    pump(40)


    for si, (o, n) in enumerate(SEGS):
        bank = si % 2
        p.mm_group(PB[bank], [(pb[bank][:, 0:n], s_kproj[:, kt, :], xnT[:, kt, o:o + n])
                              for kt in range(16)], reads=["s_kproj"] + XN)
        p.emit("scalar", lambda e, b=bank, o=o, n=n: e.activation(
            kTt[:, o:o + n], pb[b][:, 0:n], AF.Identity),
            reads=[PB[bank]], writes=["kT"])
        p.emit("scalar", lambda e, o=o, n=n: e.activation(
            k_sc[:, o:o + n], kTt[:, o:o + n], AF.Square),
            reads=["kT"], writes=["k_sc"])
        p.mm_group(PB[2 + bank], [(pb[2 + bank][0:1, 0:n], s_onesB[:, :], k_sc[:, o:o + n])],
                   reads=["s_onesB", "k_sc"])
        p.emit("scalar", lambda e, b=2 + bank, n=n: e.activation(
            pb[b][0:1, 0:n], pb[b][0:1, 0:n], AF.Ln, bias=s_epsc[0:1, 1:2], scale=1.0 / HD),
            reads=[PB[2 + bank], "s_epsc"], writes=[PB[2 + bank]])
        p.emit("scalar", lambda e, b=2 + bank, n=n: e.activation(
            pb[b][0:1, 0:n], pb[b][0:1, 0:n], AF.Exp, scale=-0.5),
            reads=[PB[2 + bank]], writes=[PB[2 + bank]])
        p.emit("vector", lambda e, b=2 + bank, n=n: e.tensor_copy(
            k_row[:, 0:n], pb[b][0:1, 0:n]),
            reads=[PB[2 + bank]], writes=["k_row"])
        p.mm_group(PB[4], [(pb[4][:, 0:n], s_onesRF[:, :], k_row[:, 0:n])],
                   reads=["s_onesRF", "k_row"])
        p.emit("vector", lambda e, o=o, n=n: e.scalar_tensor_tensor(
            kTt[:, o:o + n], kTt[:, o:o + n], s_knw[:, :], pb[4][:, 0:n],
            op0=ALU.mult, op1=ALU.mult),
            reads=["kT", "s_knw", PB[4]], writes=["kT"])
    p.emit("vector", lambda e: e.tensor_copy(rope_scr[0:32, :], kTt[32:64, :]),
           reads=["kT"], writes=["rope_scr"])
    p.emit("vector", lambda e: e.tensor_copy(rope_scr[32:64, :], kTt[0:32, :]),
           reads=["kT"], writes=["rope_scr"])
    p.emit("vector", lambda e: e.tensor_tensor(kTt[0:64, :], kTt[0:64, :], s_CC[:, :], ALU.mult),
           reads=["kT", "s_CC"], writes=["kT"])
    p.emit("vector", lambda e: e.tensor_tensor(rope_scr[:, :], rope_scr[:, :], s_SS[:, :], ALU.mult),
           reads=["rope_scr", "s_SS"], writes=["rope_scr"])
    p.emit("vector", lambda e: e.tensor_tensor(kTt[0:64, :], kTt[0:64, :], rope_scr[:, :], ALU.add),
           reads=["kT", "rope_scr"], writes=["kT"])
    if debug:
        d_k = dout("dbg_k", (128, TCH), BF16)
        p.dma("sync", d_k[:, :], kTt[:, :], reads=["kT"], stream="dbg", shared=True)

    for mt in range(10):
        bank = 5 + (mt % 2)
        p.mm_group(PB[bank], [(pb[bank][:, 0:HD], xnT[:, kt, 128 * mt:128 * (mt + 1)],
                               s_vproj[:, kt, :]) for kt in range(16)],
                   reads=["s_vproj"] + XN)
        p.emit("scalar", lambda e, b=bank, mt=mt: e.activation(
            vtok[:, mt, :], pb[b][:, 0:HD], AF.Identity),
            reads=[PB[bank]], writes=["vtok"])

    p.dma("sync", kvtok[:, :, 0:HD],
          cc_cmp_out[:, 0:HD].rearrange("(a p) d -> p a d", p=128),
          buf="kvtok", deps=[cc_toks["cmp"]], stream="cls", shared=True)
    p.dma("sync", kvtok[:, :, HD:HD + CI],
          cc_cmp_out[:, HD:HD + CI].rearrange("(a p) d -> p a d", p=128),
          buf="kvtok2", deps=[cc_toks["cmp"]], stream="cls", shared=True)
    kvs = contextlib.ExitStack()
    kv_sc = sb(kvs, "kv_sc", (128, 4, HD), F32)
    kv_ms = sb(kvs, "kv_ms", (128, 4, 1), F32)
    kct = sb(kvs, "kct", (128, 4, CI))
    kvv = kvtok[:, :, 0:HD]
    p.emit("vector", lambda e: e.tensor_tensor(kv_sc[:, :, :], kvv, kvv, ALU.mult),
           reads=["kvtok"], writes=["kv_sc"])
    p.emit("vector", lambda e: e.tensor_reduce(kv_ms[:, :, :], kv_sc[:, :, :], axis=AXL.X, op=ALU.add),
           reads=["kv_sc"], writes=["kv_ms"])
    p.emit("scalar", lambda e: e.activation(kv_ms[:, :, :], kv_ms[:, :, :], AF.Ln,
                                            bias=s_epsc[:, 1:2], scale=1.0 / HD),
           reads=["kv_ms", "s_epsc"], writes=["kv_ms"])
    p.emit("scalar", lambda e: e.activation(kv_ms[:, :, :], kv_ms[:, :, :], AF.Exp, scale=-0.5),
           reads=["kv_ms"], writes=["kv_ms"])
    p.emit("vector", lambda e: e.tensor_tensor(
        kv_sc[:, :, :], kvv, kv_ms[:, :, :].to_broadcast([128, 4, HD]), ALU.mult),
        reads=["kvtok", "kv_ms"], writes=["kv_sc"])
    p.emit("vector", lambda e: e.tensor_tensor(
        kvnrm[:, :, :], kv_sc[:, :, :],
        s_knwb[:, :].unsqueeze(1).to_broadcast([128, 4, HD]), ALU.mult),
        reads=["kv_sc", "s_knwb"], writes=["kvnrm"])
    for a in range(4):
        p.emit("tensor", lambda e, a=a: e.transpose(pbT[:, 0:128], kvnrm[:, a, :], s_identB[:, :]),
               reads=["kvnrm", "s_identB"], writes=["pbT"])
        p.emit("vector", lambda e, a=a: e.tensor_copy(kvnrmT[:, 128 * a:128 * (a + 1)], pbT[:, 0:128]),
               reads=["pbT"], writes=["kvnrmT"])
    p.emit("vector", lambda e: e.tensor_copy(kct[:, :, :], kvtok[:, :, HD:HD + CI]),
           reads=["kvtok2"], writes=["kct"])
    for a in range(4):
        p.emit("tensor", lambda e, a=a: e.transpose(pbT[0:64, 0:128], kct[:, a, :], s_identB[:, :]),
               reads=["kct", "s_identB"], writes=["pbT"])
        p.emit("vector", lambda e, a=a: e.tensor_copy(kcompT[0:64, 128 * a:128 * (a + 1)], pbT[0:64, 0:128]),
               reads=["pbT"], writes=["kcompT"])
    p.dma("sync", kcompT[64:128, :], kcompT[0:64, :], buf="kcompT_hi", reads=["kcompT"],
          stream="cls", shared=True)
    if debug:
        d_kvn = dout("dbg_kvn", (128, 4, HD), BF16)
        d_kc = dout("dbg_kc", (128, 4, CI), F32)
        p.dma("sync", d_kvn[:, :, :], kvnrm[:, :, :], reads=["kvnrm"], stream="dbg", shared=True)
        p.dma("sync", d_kc[:, :, :], kvtok[:, :, HD:HD + CI], reads=["kvtok2"], stream="dbg", shared=True)
    kvs.close()

    for ci in range(4):
        p.dma("gpsimd", fw_bf[512 * ci:512 * (ci + 1), :], final_w[512 * ci:512 * (ci + 1), :],
              buf="fw_bf", stream="glf", shared=True)
    pump(40)

    for mt in range(4):
        for ncl in range(2):
            bank = ncl
            p.mm_group(PB[bank], [(pb[bank][:, :], s_wqc[:, kt, 128 * mt:128 * (mt + 1)],
                                   xnT[:, kt, HALO + 512 * ncl:HALO + 512 * (ncl + 1)])
                                  for kt in range(16)], reads=["s_wqc"] + XN)
            p.emit("scalar", lambda e, b=bank, mt=mt, ncl=ncl: e.activation(
                AT[:, mt, 512 * ncl:512 * (ncl + 1)], pb[b][:, :], AF.Identity),
                reads=[PB[bank]], writes=["AT"])
    for ncl in range(2):
        bank = 2 + ncl
        p.mm_group(PB[bank], [(pb[bank][0:64, :], s_wdq[:, kt, :],
                               xnT[:, kt, HALO + 512 * ncl:HALO + 512 * (ncl + 1)])
                              for kt in range(16)], reads=["s_wdq"] + XN)
        p.emit("scalar", lambda e, b=bank, ncl=ncl: e.activation(
            dqT[:, 512 * ncl:512 * (ncl + 1)], pb[b][0:64, :], AF.Identity),
            reads=[PB[bank]], writes=["dqT"])
    for mt in range(2):
        for ncl in range(2):
            bank = 4 + ncl
            p.mm_group(PB[bank], [(pb[bank][:, :], s_wiuq[:, 128 * mt:128 * (mt + 1)],
                                   dqT[:, 512 * ncl:512 * (ncl + 1)])],
                       reads=["s_wiuq", "dqT"])
            p.emit("scalar", lambda e, b=bank, mt=mt, ncl=ncl: e.activation(
                qiT[:, mt, 512 * ncl:512 * (ncl + 1)], pb[b][:, :], AF.Identity),
                reads=[PB[bank]], writes=["qiT"])
    for mt in range(8):
        p.mm_group(PB[6], [(pb[6][:, 0:NIH], xnT[:, kt, HALO + 128 * mt:HALO + 128 * (mt + 1)],
                            s_ww[:, kt, :]) for kt in range(16)],
                   reads=["s_ww"] + XN)
        p.emit("vector", lambda e, mt=mt: e.tensor_copy(wih[:, mt, :], pb[6][:, 0:NIH]),
               reads=[PB[6]], writes=["wih"])

    pump(40)

    isc_st = contextlib.ExitStack()
    hsr = sb(isc_st, "hsr", (128, NB), F32)
    isc = sb(isc_st, "isc", (128, NB), F32)
    isc_all = sb(isc_st, "isc_all", (128, 8, NB), F32)
    iwork = sb(isc_st, "iwork", (128, 2, NB), F32)
    mx8 = sb(isc_st, "mx8", (128, 8), F32)
    vstar = sb(isc_st, "vstar", (128, 1), F32)
    selbin = sb(isc_st, "selbin", (128, NB))
    mask_ring = sb(isc_st, "mask_ring", (128, 4, NB))
    s_epstie = sb(isc_st, "s_epstie", (128, NB), F32)
    p.dma("sync", s_epstie[:, :], eps_tie[:, :], buf="s_epstie", stream="cls", shared=True)
    d_isc = dout("dbg_isc", (8, 128, NB), F32) if debug else None
    d_sel = dout("dbg_sel", (8, 128, NB), BF16) if debug else None

    cm_str = [p.stream("cm") for _ in range(4)]
    for tt in range(8):
        for ih in range(NIH):
            bank = ih % 2
            base = 64 * (ih % 2)
            p.mm_group(PB[bank], [(pb[bank][:, 0:512],
                                   qiT[base:base + 64, ih // 2, 128 * tt:128 * (tt + 1)],
                                   kcompT[base:base + 64, 0:512])],
                       reads=["qiT", "kcompT" if ih % 2 == 0 else "kcompT_hi"])
            p.emit("scalar", lambda e, b=bank: e.activation(hsr[:, :], pb[b][:, 0:512], AF.Relu),
                   reads=[PB[bank]], writes=["hsr"])
            if ih == 0:
                p.emit("vector", lambda e, tt=tt, ih=ih: e.tensor_scalar(
                    isc[:, :], hsr[:, :], wih[:, tt, ih:ih + 1], None, op0=ALU.mult),
                    reads=["hsr", "wih"], writes=["isc"])
            elif ih < NIH - 1:
                p.emit("vector", lambda e, tt=tt, ih=ih: e.scalar_tensor_tensor(
                    isc[:, :], hsr[:, :], wih[:, tt, ih:ih + 1], isc[:, :],
                    op0=ALU.mult, op1=ALU.add),
                    reads=["hsr", "wih", "isc"], writes=["isc"])
            else:
                p.emit("vector", lambda e, tt=tt, ih=ih: e.scalar_tensor_tensor(
                    isc_all[:, tt, :], hsr[:, :], wih[:, tt, ih:ih + 1], isc[:, :],
                    op0=ALU.mult, op1=ALU.add),
                    reads=["hsr", "wih", "isc"], writes=[f"iscA{tt}"])
        pump(6)
        fpump(1)
    for tt in range(8):
        sl = tt % 2
        p.dma("sync", mask_ring[:, sl, :], causal_add[tt, :, :], buf=f"cmA{sl}",
              stream=cm_str[sl])
        p.dma("sync", mask_ring[:, 2 + sl, :], causal_bin[tt, :, :], buf=f"cmB{sl}",
              stream=cm_str[2 + sl])
        p.emit("vector", lambda e, tt=tt: e.tensor_scalar(
            iwork[:, 1, :], isc_all[:, tt, :], 0.0, None, op0=ALU.is_equal),
            reads=[f"iscA{tt}"], writes=["iwork1"])
        p.emit("vector", lambda e: e.tensor_tensor(
            iwork[:, 1, :], iwork[:, 1, :], s_epstie[:, :], ALU.mult),
            reads=["iwork1", "s_epstie"], writes=["iwork1"])
        p.emit("vector", lambda e, tt=tt: e.tensor_tensor(
            isc[:, :], isc_all[:, tt, :], iwork[:, 1, :], ALU.add),
            reads=[f"iscA{tt}", "iwork1"], writes=["isc"])
        p.emit("vector", lambda e, sl=sl: e.tensor_tensor(
            isc[:, :], isc[:, :], mask_ring[:, sl, :], ALU.add),
            reads=["isc", f"cmA{sl}"], writes=["isc"])
        if debug:
            p.dma("sync", d_isc[tt, :, :], isc[:, :], reads=["isc"], stream="dbg", shared=True)
        p.emit("vector", lambda e: e.tensor_copy(iwork[:, 0, :], isc[:, :]),
               reads=["isc"], writes=["iwork0"])
        for r in range(8):
            p.emit("vector", lambda e: e.max(out=mx8[:, :], in_=iwork[:, 0, :]),
                   reads=["iwork0"], writes=["mx8"])
            if r < 7:
                p.emit("vector", lambda e: e.match_replace(
                    out=iwork[:, 1, :], in_to_replace=mx8[:, :], in_values=iwork[:, 0, :],
                    imm_value=-1e38),
                    reads=["mx8", "iwork0"], writes=["iwork1"])
                p.emit("vector", lambda e: e.tensor_copy(iwork[:, 0, :], iwork[:, 1, :]),
                       reads=["iwork1"], writes=["iwork0"])
        p.emit("vector", lambda e: e.tensor_reduce(vstar[:, :], mx8[:, :], axis=AXL.X, op=ALU.min),
               reads=["mx8"], writes=["vstar"])
        p.emit("vector", lambda e: e.tensor_scalar(
            selbin[:, :], isc[:, :], vstar[:, :], None, op0=ALU.is_ge),
            reads=["isc", "vstar"], writes=["selbin"])
        p.emit("vector", lambda e, sl=sl: e.tensor_tensor(
            selbin[:, :], selbin[:, :], mask_ring[:, 2 + sl, :], ALU.mult),
            reads=["selbin", f"cmB{sl}"], writes=["selbin"])
        if debug:
            p.dma("sync", d_sel[tt, :, :], selbin[:, :], reads=["selbin"], stream="dbg", shared=True)
        for nt in range(4):
            p.emit("tensor", lambda e, nt=nt: e.transpose(
                pbT[:, 0:128], selbin[:, 128 * nt:128 * (nt + 1)], s_identB[:, :]),
                reads=["selbin", "s_identB"], writes=["pbT"])
            p.emit("vector", lambda e, nt=nt, tt=tt: e.tensor_copy(
                selbinT[:, nt, 128 * tt:128 * (tt + 1)], pbT[:, 0:128]),
                reads=["pbT"], writes=["selbinT"])
        pump(6)
        fpump(2)
    isc_st.close()
    p34_st.close()
    _join = p.join_tokens(
        ["hsr", "isc", "iwork0", "iwork1", "mx8", "vstar", "selbin"]
        + [f"iscA{tt}" for tt in range(8)] +
        ["placeholder_unused",
         "cmA0", "cmA1", "cmB0", "cmB1", "s_epstie",
         "s_wqc", "k_sc", "k_row", "dqT", "qiT", "wih"])
    p.seed(["qh0", "qh1", "q_scr", "q_row", "Esw0", "Esw1", "Esp0", "Esp1",
            "numsb0", "numsb1", "spt", "gwR", "grpt0", "grpt1",
            "nrowf0", "nrowf1", "nrow10", "nrow11", "spt0", "spt1", "fw0", "fw1", "grd0", "grd1",
            "sTt0", "sTt1"]
           + [f"attn{g}_{h}" for g in range(2) for h in range(4)], _join)

    at_st = contextlib.ExitStack()
    qT_ring = sb(at_st, "qT_ring", (128, 2, TC))
    q_scr = sb(at_st, "q_scr", (128, 512))
    q_row = sb(at_st, "q_row", (1, TC), F32)
    Esw = sb(at_st, "Esw", (128, 2, 4, WIN))
    Esp = sb(at_st, "Esp", (128, 2, 4, 512))
    numsb = sb(at_st, "numsb", (128, 768))
    spt = sb(at_st, "spt", (128, 2, 512))
    attn_g = sb(at_st, "attn_g", (128, 2, 4, TC))
    gw_ring = sb(at_st, "gw_ring", (128, 1, 4, 512))
    grp_tile = sb(at_st, "grp_tile", (128, 2, 512))
    nrowf = sb(at_st, "nrowf", (1, 1536), F32)

    wq_str = [p.stream("wq"), p.stream("wq")]
    d_q = dout("dbg_q", (128, H, TC), BF16) if debug else None
    d_attn = dout("dbg_attn", (128, H, TC), BF16) if debug else None

    def emit_qhead(h):
        sl = h % 2
        qb = f"qh{sl}"
        QB = PB[3]
        wsl = h % 2
        p.dma("scalar", wqup_ring[:, wsl, :, :],
              wqup_bf[:, 128 * h:128 * (h + 1)].rearrange("(kt p) m -> p kt m", p=128),
              buf=f"wq{wsl}", reads=["wqup_bf"], stream=wq_str[wsl])
        for ncl in range(2):
            p.mm_group(QB, [(pb[3][:, :], wqup_ring[:, wsl, kt, :],
                             AT[:, kt, 512 * ncl:512 * (ncl + 1)]) for kt in range(4)],
                       reads=[f"wq{wsl}", "AT"])
            p.emit("scalar", lambda e, ncl=ncl, sl=sl: e.activation(
                qT_ring[:, sl, 512 * ncl:512 * (ncl + 1)], pb[3][:, :], AF.Identity),
                reads=[QB], writes=[qb])
            p.emit("scalar", lambda e, sl=sl, ncl=ncl: e.activation(
                q_scr[:, :], qT_ring[:, sl, 512 * ncl:512 * (ncl + 1)], AF.Square),
                reads=[qb], writes=["q_scr"])
            p.mm_group(QB, [(pb[3][0:1, :], s_onesB[:, :], q_scr[:, :])],
                       reads=["s_onesB", "q_scr"])
            p.emit("scalar", lambda e: e.activation(
                pb[3][0:1, :], pb[3][0:1, :], AF.Ln, bias=s_epsc[0:1, 1:2], scale=1.0 / HD),
                reads=[QB, "s_epsc"], writes=[QB])
            p.emit("scalar", lambda e: e.activation(
                pb[3][0:1, :], pb[3][0:1, :], AF.Exp, scale=-0.5),
                reads=[QB], writes=[QB])
            p.emit("vector", lambda e, ncl=ncl: e.tensor_copy(
                q_row[:, 512 * ncl:512 * (ncl + 1)], pb[3][0:1, :]),
                reads=[QB], writes=["q_row"])
        p.emit("vector", lambda e, sl=sl: e.tensor_copy(rope_scr[0:32, 0:TC], qT_ring[32:64, sl, :]),
               reads=[qb], writes=["rope_scr"])
        p.emit("vector", lambda e, sl=sl: e.tensor_copy(rope_scr[32:64, 0:TC], qT_ring[0:32, sl, :]),
               reads=[qb], writes=["rope_scr"])
        p.emit("vector", lambda e, sl=sl: e.tensor_tensor(
            qT_ring[0:64, sl, :], qT_ring[0:64, sl, :], s_CC[:, HALO:], ALU.mult),
            reads=[qb, "s_CC"], writes=[qb])
        p.emit("vector", lambda e: e.tensor_tensor(
            rope_scr[:, 0:TC], rope_scr[:, 0:TC], s_SS[:, HALO:], ALU.mult),
            reads=["rope_scr", "s_SS"], writes=["rope_scr"])
        p.emit("vector", lambda e, sl=sl: e.tensor_tensor(
            qT_ring[0:64, sl, :], qT_ring[0:64, sl, :], rope_scr[:, 0:TC], ALU.add),
            reads=[qb, "rope_scr"], writes=[qb])
        for ncl in range(2):
            p.mm_group(QB, [(pb[3][:, :], s_onesRF[:, :],
                             q_row[:, 512 * ncl:512 * (ncl + 1)])],
                       reads=["s_onesRF", "q_row"])
            p.emit("vector", lambda e, ncl=ncl, sl=sl: e.scalar_tensor_tensor(
                qT_ring[:, sl, 512 * ncl:512 * (ncl + 1)],
                qT_ring[:, sl, 512 * ncl:512 * (ncl + 1)],
                s_qnw_s[:, :], pb[3][:, :], op0=ALU.mult, op1=ALU.mult),
                reads=[qb, "s_qnw_s", QB], writes=[qb])
        if debug:
            p.dma("sync", d_q[:, h, :], qT_ring[:, sl, :], reads=[qb], stream="dbg", shared=True)
        return qb

    def emit_sw(h, wb, qb):
        sl = h % 2
        g, hr = h // 4, h % 4
        gsl = g % 2
        B0, B1, B2 = (0, 1, 2)
        ebuf = f"Esw{sl}"
        for kt in range(4):
            bank = B0 + kt // 2
            off = 256 * (kt % 2)
            p.mm_group(PB[bank],
                       [(pb[bank][:, off:off + 256],
                         kTt[:, 256 * wb + 128 * kt:256 * wb + 128 * kt + 128],
                         qT_ring[:, sl, WIN * wb:WIN * (wb + 1)])],
                       reads=["kT", qb])
            p.emit("scalar", lambda e, b=bank, off=off, kt=kt, sl=sl: e.activation(
                Esw[:, sl, kt, :], pb[b][:, off:off + 256], AF.Exp, bias=s_negC[:, :]),
                reads=[PB[bank], "s_negC"], writes=[ebuf])
            p.emit("vector", lambda e, kt=kt, wb=wb, sl=sl: e.tensor_tensor(
                Esw[:, sl, kt, :], Esw[:, sl, kt, :], swm[:, 4 * wb + kt, :], ALU.mult),
                reads=[ebuf, "swm"], writes=[ebuf])
        p.mm_group(PB[B2], [(pb[B2][0:1, 0:256], s_onesB[:, :], Esw[:, sl, kt, :])
                            for kt in range(4)], reads=["s_onesB", ebuf])
        p.mm_group(PB[B1], [(pb[B1][:, 256:512], vtok[:, 2 * wb + kt, :], Esw[:, sl, kt, :])
                            for kt in range(4)], reads=["vtok", ebuf])
        p.emit("scalar", lambda e, B2=B2, sl=sl: e.activation(
            nrowf[:, 256 * sl:256 * sl + 256], pb[B2][0:1, 0:256], AF.Ln),
            reads=[PB[B2]], writes=[f"nrowf{sl}"])
        p.emit("scalar", lambda e, sl=sl: e.activation(
            nrowf[:, 256 * sl:256 * sl + 256], nrowf[:, 256 * sl:256 * sl + 256], AF.Exp, scale=-1.0),
            reads=[f"nrowf{sl}"], writes=[f"nrowf{sl}"])
        p.mm_group(PB[B2], [(pb[B2][:, 256:512], s_onesRF[:, :], nrowf[:, 256 * sl:256 * sl + 256])],
                   reads=["s_onesRF", f"nrowf{sl}"])
        p.emit("scalar", lambda e, B1=B1, sl=sl: e.activation(
            numsb[:, 256 * sl:256 * sl + 256], pb[B1][:, 256:512], AF.Identity),
            reads=[PB[B1]], writes=[f"numsb{sl}"])
        p.emit("vector", lambda e, gsl=gsl, hr=hr, wb=wb, B2=B2, sl=sl: e.tensor_tensor(
            attn_g[:, gsl, hr, WIN * wb:WIN * (wb + 1)], numsb[:, 256 * sl:256 * sl + 256],
            pb[B2][:, 256:512], ALU.mult),
            reads=[f"numsb{sl}", PB[B2]], writes=[f"attn{gsl}_{hr}"])

    def emit_sparse(h, ch, qb):
        sl = h % 2
        g, hr = h // 4, h % 4
        gsl = g % 2
        B0, B1, B2 = (0, 1, 2)
        ebuf = f"Esp{sl}"
        for nt in range(4):
            bank = B0 + nt % 2
            p.mm_group(PB[bank], [(pb[bank][:, :], kvnrmT[:, 128 * nt:128 * (nt + 1)],
                                   qT_ring[:, sl, 512 * ch:512 * (ch + 1)])],
                       reads=["kvnrmT", qb])
            p.emit("scalar", lambda e, b=bank, nt=nt, sl=sl: e.activation(
                Esp[:, sl, nt, :], pb[b][:, :], AF.Exp, bias=s_negC[:, :]),
                reads=[PB[bank], "s_negC"], writes=[ebuf])
            p.emit("vector", lambda e, nt=nt, ch=ch, sl=sl: e.tensor_tensor(
                Esp[:, sl, nt, :], Esp[:, sl, nt, :],
                selbinT[:, nt, 512 * ch:512 * (ch + 1)], ALU.mult),
                reads=[ebuf, "selbinT"], writes=[ebuf])
        p.mm_group(PB[B2], [(pb[B2][0:1, :], s_onesB[:, :], Esp[:, sl, nt, :]) for nt in range(4)],
                   reads=["s_onesB", ebuf])
        p.mm_group(PB[B0], [(pb[B0][:, :], kvnrm[:, nt, :], Esp[:, sl, nt, :]) for nt in range(4)],
                   reads=["kvnrm", ebuf])
        p.emit("vector", lambda e, B2=B2, sl=sl: e.tensor_scalar(
            nrowf[:, 512 + 256 * sl:512 + 256 * sl + 256]
            if False else nrowf[:, 512 + 384 * sl:512 + 384 * sl + 384][:, 0:512 - 0][:, 0:512][:, 0:512],
            pb[B2][0:1, :], 1e-30, None, op0=ALU.add),
            reads=[PB[B2]], writes=[f"nrow1{sl}"]) if False else None
        off = 512 + 512 * sl
        p.emit("vector", lambda e, B2=B2, off=off: e.tensor_scalar(
            nrowf[:, off:off + 512], pb[B2][0:1, :], 1e-30, None, op0=ALU.add),
            reads=[PB[B2]], writes=[f"nrow1{sl}"])
        p.emit("scalar", lambda e, off=off: e.activation(nrowf[:, off:off + 512], nrowf[:, off:off + 512], AF.Ln),
               reads=[f"nrow1{sl}"], writes=[f"nrow1{sl}"])
        p.emit("scalar", lambda e, off=off: e.activation(nrowf[:, off:off + 512], nrowf[:, off:off + 512], AF.Exp, scale=-1.0),
               reads=[f"nrow1{sl}"], writes=[f"nrow1{sl}"])
        p.mm_group(PB[B2], [(pb[B2][:, :], s_onesRF[:, :], nrowf[:, off:off + 512])],
                   reads=["s_onesRF", f"nrow1{sl}"])
        p.emit("scalar", lambda e, B0=B0, sl=sl: e.activation(
            numsb[:, 512 + 512 * 0:1024] if False else spt[:, :], pb[B0][:, :], AF.Identity),
            reads=[PB[B0]], writes=[f"spt{sl}"]) if False else None
        p.emit("scalar", lambda e, B0=B0, sl=sl: e.activation(
            spt[:, sl, :], pb[B0][:, :], AF.Identity),
            reads=[PB[B0]], writes=[f"spt{sl}"])
        p.emit("vector", lambda e, B2=B2, sl=sl: e.tensor_tensor(
            spt[:, sl, :], spt[:, sl, :], pb[B2][:, :], ALU.mult),
            reads=[f"spt{sl}", PB[B2]], writes=[f"spt{sl}"])
        p.emit("vector", lambda e, gsl=gsl, hr=hr, ch=ch, sl=sl: e.tensor_tensor(
            attn_g[:, gsl, hr, 512 * ch:512 * (ch + 1)],
            attn_g[:, gsl, hr, 512 * ch:512 * (ch + 1)], spt[:, sl, :], ALU.add),
            reads=[f"spt{sl}", f"attn{gsl}_{hr}"], writes=[f"attn{gsl}_{hr}"])

    gw_str = [p.stream("gw")]
    gd_str = [p.stream("gd"), p.stream("gd")]

    def emit_group(g):
        gsl = g % 2
        gwsl = 0
        deps = [f"attn{gsl}_{hr}" for hr in range(4)]
        p.dma("gpsimd", gw_ring[:, 0, :, :],
              group_w[g, :, :].rearrange("(kt p) m -> p kt m", p=128),
              buf="gwR", stream=gw_str[0])
        for mt in range(4):
            for ncl in range(2):
                bank = ncl
                p.mm_group(PB[bank], [(pb[bank][:, :], gw_ring[:, 0, kt, 128 * mt:128 * (mt + 1)],
                                       attn_g[:, gsl, kt, 512 * ncl:512 * (ncl + 1)])
                                      for kt in range(4)],
                           reads=["gwR"] + deps)
                gslot = (mt + ncl) % 2
                p.emit("scalar", lambda e, b=bank, gslot=gslot: e.activation(
                    grp_tile[:, gslot, :], pb[b][:, :], AF.Identity),
                    reads=[PB[bank]], writes=[f"grpt{gslot}"])
                p.dma("sync", grp_dram[128 * (4 * g + mt):128 * (4 * g + mt + 1),
                                       512 * ncl:512 * (ncl + 1)],
                      grp_tile[:, gslot, :], buf=f"grouped{g}",
                      reads=[f"grpt{gslot}"], stream=gd_str[gslot])

    for h in range(H):
        qb = emit_qhead(h)
        for wb in range(4):
            emit_sw(h, wb, qb)
        for ch in range(2):
            emit_sparse(h, ch, qb)
        if debug:
            gsl, hr = (h // 4) % 2, h % 4
            p.dma("sync", d_attn[:, h, :], attn_g[:, gsl, hr, :], reads=[f"attn{gsl}_{hr}"], stream="dbg", shared=True)
        if h % 4 == 3:
            emit_group(h // 4)
        pump(30)
        fpump(4 if h % 8 < 3 else 3)

    fpump_all()

    if debug:
        d_hint = dout("dbg_hint", (128, NIT, 512), BF16)
        hdeps = []
        for it in range(NIT):
            hdeps.extend(p.acc_deps(f"hd_{it}_1"))
        p.dma("sync", d_hint[:, :, :],
              hint_dram[:, 512:1024].rearrange("(it p) m -> p it m", p=128),
              deps=hdeps, stream="dbg", shared=True)

    at_st.close()
    sel_st.close()
    gw_st.close()
    qc_st.close()


    pump_all()
    _join2 = p.join_tokens(
        ["qh0", "qh1", "q_scr", "q_row", "Esw0", "Esw1", "Esp0", "Esp1",
         "numsb0", "numsb1", "spt", "gwR", "grpt0", "grpt1",
         "nrowf0", "nrowf1", "nrow10", "nrow11", "spt0", "spt1", "fw0", "fw1", "grd0", "grd1",
         "sTt0", "sTt1", "selbinT", "swm", "s_wqup", "AT",
         "wgug", "wguu", "gsil2", "hsb0", "hsb1"]
        + [f"attn{g}_{h}" for g in range(2) for h in range(4)])
    p.seed(["wd00", "wd01", "wd10", "wd11", "gsil_0", "gsil_1", "sg0", "sg1",
            "sTr0", "sTr1", "xf0", "xf1", "o0", "o1",
            "hin0", "hin1", "hin2", "hin3"], _join2)

    p.seed([f"fw{i}" for i in range(6)] + [f"grd{i}" for i in range(6)]
           + ["sTt0", "sTt1"], _join2)

    fin_st = contextlib.ExitStack()
    fw_ring = sb(fin_st, "fw_ring", (128, 6, 1024))
    grd_ring = sb(fin_st, "grd_ring", (128, 6, 512))
    sT_tile = sb(fin_st, "sT_tile", (128, 2, 512))
    fw_str = [p.stream("fw") for _ in range(6)]
    gr_str = [p.stream("gr") for _ in range(6)]
    st_str = [p.stream("st"), p.stream("st")]
    for sc in range(2):
        for pass_ in range(2):
            o0 = 8 * pass_
            for kt in range(16):
                sl = kt % 6
                fsl = kt % 6
                p.dma("scalar", fw_ring[:, fsl, :],
                      fw_bf[128 * kt:128 * (kt + 1), 128 * o0:128 * (o0 + 8)],
                      buf=f"fw{fsl}", reads=["fw_bf"], stream=fw_str[fsl])
                p.dma("sync", grd_ring[:, sl, :],
                      grp_dram[128 * kt:128 * (kt + 1), 512 * sc:512 * (sc + 1)],
                      buf=f"grd{sl}", reads=[f"grouped{kt // 4}"], stream=gr_str[sl])
                for oi in range(8):
                    bname = PB[oi]
                    dep = p.acc_deps(bname) if kt == 0 else []
                    tok = p.emit("tensor",
                                 lambda e, oi=oi, sl=sl, fsl=fsl,
                                 st=(kt == 0), sp=(kt == 15):
                                 e.matmul(pb[oi][:, :], fw_ring[:, fsl, 128 * oi:128 * (oi + 1)],
                                          grd_ring[:, sl, :],
                                          start=st, stop=sp),
                                 reads=[f"fw{fsl}", f"grd{sl}"],
                                 deps=dep)
                    if kt == 15:
                        p.set_write(bname, tok)
            for oi in range(8):
                ot = o0 + oi
                bname = PB[oi]
                sl2 = oi % 2
                p.emit("scalar", lambda e, oi=oi, sl2=sl2: e.activation(
                    sT_tile[:, sl2, :], pb[oi][:, :], AF.Identity),
                    reads=[bname], writes=[f"sTt{sl2}"])
                p.dma("sync", sT_dram[128 * ot:128 * (ot + 1), 512 * sc:512 * (sc + 1)],
                      sT_tile[:, sl2, :], buf=f"sTd_{ot}_{sc}", reads=[f"sTt{sl2}"],
                      stream=st_str[sl2])
    if debug:
        d_sT = dout("dbg_sT", (D, TC), BF16)
        for ot in range(16):
            for sc in range(2):
                p.dma("sync", d_sT[128 * ot:128 * (ot + 1), 512 * sc:512 * (sc + 1)],
                      sT_dram[128 * ot:128 * (ot + 1), 512 * sc:512 * (sc + 1)],
                      deps=p.acc_deps(f"sTd_{ot}_{sc}"), stream="dbg", shared=True)
    fin_st.close()
    ffn_st = contextlib.ExitStack()
    wd_ring = sb(ffn_st, "wd_ring", (128, 2, 2, 11, 256))
    hin_ring = sb(ffn_st, "hin_ring", (128, 4, 1024))
    gsil = sb(ffn_st, "gsil", (128, 2, 512))
    sg_t = sb(ffn_st, "sg_t", (128, 2, 512))
    sTr = sb(ffn_st, "sTr", (128, 8, 512))
    xf_ring = sb(ffn_st, "xf_ring", (128, 8, 512), F32)
    o_ring = sb(ffn_st, "o_ring", (128, 2, 512), F32)

    wd_str = {(kp, ri): p.stream("wd") for kp in range(2) for ri in range(2)}
    hin_str = [p.stream("hi") for _ in range(4)]
    sTr_str = [p.stream("sr") for _ in range(8)]
    xf_str = [p.stream("x2") for _ in range(8)]
    out_str = [p.stream("ou"), p.stream("ou")]
    out_toks = []

    _join2b = p.join_tokens([f"fw{i}" for i in range(6)] + [f"grd{i}" for i in range(6)]
                        + ["sTt0", "sTt1"])
    p.seed(["wd00", "wd01", "wd10", "wd11", "gsil_0", "gsil_1", "sg0", "sg1",
            "o0", "o1", "hin0", "hin1", "hin2", "hin3"]
           + [f"sTr{i}" for i in range(8)] + [f"xf{i}" for i in range(8)], _join2b)

    def wd_load(og4, k):
        kp = k % 2
        tiles = CHUNK_TILES[k]
        for ri, r in enumerate((2 * og4, 2 * og4 + 1)):
            p.dma("scalar" if ri == 0 else "sync", wd_ring[:, kp, ri, 0:tiles, :],
                  wd_out[k][r * tiles * 128:(r + 1) * tiles * 128, :]
                  .rearrange("(it p) m -> p it m", p=128),
                  buf=f"wd{kp}{ri}", deps=[cc_toks[f"d{k}"]], stream=wd_str[(kp, ri)])

    for og4 in range(4):
        ranks = (2 * og4, 2 * og4 + 1)
        first_mm = [True] * 8
        for sh in range(2):
            xcol = HALO + 512 * sh
            for ri in range(2):
                for half in range(2):
                    oi = 4 * sh + 2 * ri + half
                    ot = 4 * og4 + 2 * ri + half
                    p.dma("sync", sTr[:, oi, :],
                          sT_dram[128 * ot:128 * (ot + 1), 512 * sh:512 * (sh + 1)],
                          buf=f"sTr{oi}", deps=p.acc_deps(f"sTd_{ot}_{sh}"), stream=sTr_str[oi])
                    p.dma("sync", xf_ring[:, oi, :], xT[128 * ot:128 * (ot + 1), xcol:xcol + 512],
                          buf=f"xf{oi}", stream=xf_str[oi])
        for k in range(NCHUNK):
            kp = k % 2
            tiles = CHUNK_TILES[k]
            if not (k == 0 and og4 > 0):
                wd_load(og4, k)
            for it_l in range(tiles):
                it = CHUNK_OFF[k] + it_l
                hsl = it % 4
                p.dma("scalar", hin_ring[:, hsl, :],
                      hint_dram[128 * it:128 * (it + 1), :],
                      buf=f"hin{hsl}",
                      deps=p.acc_deps(f"hd_{it}_0") + p.acc_deps(f"hd_{it}_1"),
                      stream=hin_str[hsl])
                last = (k == NCHUNK - 1) and (it_l == tiles - 1)
                for sh in range(2):
                    for ri in range(2):
                        for half in range(2):
                            oi = 4 * sh + 2 * ri + half
                            bname = PB[oi]
                            st = first_mm[oi]
                            dep = p.acc_deps(bname) if st else []
                            if st:
                                first_mm[oi] = False
                            tok = p.emit("tensor",
                                         lambda e, oi=oi, kp=kp, ri=ri, it_l=it_l, hsl=hsl,
                                         half=half, sh=sh, st=st, sp=last:
                                         e.matmul(pb[oi][:, :],
                                                  wd_ring[:, kp, ri, it_l, 128 * half:128 * (half + 1)],
                                                  hin_ring[:, hsl, 512 * sh:512 * (sh + 1)],
                                                  start=st, stop=sp),
                                         reads=[f"wd{kp}{ri}", f"hin{hsl}"],
                                         deps=dep)
                            if last:
                                p.set_write(bname, tok)
        if og4 < 3:
            wd_load(og4 + 1, 0)
        for sh in range(2):
            for ri in range(2):
                for half in range(2):
                    oi = 4 * sh + 2 * ri + half
                    ot = 4 * og4 + 2 * ri + half
                    sl2 = oi % 2
                    p.emit("scalar", lambda e, oi=oi, sl2=sl2: e.activation(
                        gsil[:, sl2, :], pb[oi][:, :], AF.Silu),
                        reads=[PB[oi]], writes=[f"gsil_{sl2}"])
                    p.emit("vector", lambda e, oi=oi, sl2=sl2: e.tensor_tensor(
                        sg_t[:, sl2, :], gsil[:, sl2, :], sTr[:, oi, :], ALU.mult),
                        reads=[f"gsil_{sl2}", f"sTr{oi}"], writes=[f"sg{sl2}"])
                    p.emit("vector", lambda e, oi=oi, sl2=sl2: e.tensor_tensor(
                        o_ring[:, sl2, :], xf_ring[:, oi, :], sg_t[:, sl2, :], ALU.add),
                        reads=[f"xf{oi}", f"sg{sl2}"], writes=[f"o{sl2}"])
                    tok = p.dma("sync", outT[128 * ot:128 * (ot + 1), 512 * sh:512 * (sh + 1)],
                                o_ring[:, sl2, :], buf=f"od_{ot}_{sh}", reads=[f"o{sl2}"],
                                stream=out_str[sl2])
                    out_toks.append(tok)

    final_deps = list(out_toks)
    if "dbg" in p.streams:
        final_deps.append(("d_dbg", p.streams["dbg"]))
    p.emit("gpsimd", lambda e: e.nop(), deps=final_deps)
    ffn_st.close()

    p.build()
    top.close()
    return nc, dbg



def _host_inputs(full):
    x = np.asarray(full["L0_attn_h"])[0]
    T = x.shape[0]
    inv = (1.0 / (10000.0 ** (np.arange(0, 64, 2, dtype=np.float64) / 64.0)))
    gate_idxT = np.ascontiguousarray(np.asarray(full["L0_ffn_gate_idx"]).T)
    up_idxT = np.ascontiguousarray(np.asarray(full["L0_ffn_up_idx"]).T)
    down_idxT = np.ascontiguousarray(np.asarray(full["L0_ffn_down_idx"]).T)
    ones_colB = np.ones((128, 1), dtype=BF)
    ones_row_f = np.ones((1, 128), dtype=np.float32)
    identB = np.eye(128, dtype=BF)
    identF = np.eye(128, dtype=np.float32)
    eps_tie = np.tile((-(np.arange(NB, dtype=np.float64) + 1.0) * 1e-25).astype(np.float32)[None, :],
                      (128, 1))
    kvc_biasT = np.ascontiguousarray(np.asarray(full["L0_attn_kvc_bias"]).T).astype(np.float32)
    idx_biasT = np.ascontiguousarray(np.asarray(full["L0_attn_idx_c_bias"]).T).astype(np.float32)

    in_maps = []
    for c in range(NCORE):
        t0 = TC * c
        gpos = np.arange(t0 - HALO, t0 + TC)
        xa = np.zeros((TCH, D), dtype=np.float32)
        lo = max(0, t0 - HALO)
        xa[HALO - (t0 - lo):] = x[lo:t0 + TC]
        ang = gpos[:, None].astype(np.float64) * inv[None, :]
        ropeCc = np.ascontiguousarray(np.cos(ang).T.astype(np.float32))
        ropeSc = np.ascontiguousarray(np.sin(ang).T.astype(np.float32))
        cadd = np.zeros((8, 128, NB), dtype=BF)
        cbin = np.zeros((8, 128, NB), dtype=BF)
        narr = np.arange(NB)
        for tt in range(8):
            tg = t0 + 128 * tt + np.arange(128)
            valid = (16 * narr[None, :] + 15) < tg[:, None]
            cadd[tt] = np.where(valid, np.float32(0.0), np.float32(-1e30)).astype(BF)
            cbin[tt] = valid.astype(BF)
        swm = np.zeros((4, 4, 128, WIN), dtype=BF)
        for wb in range(4):
            wbg = 4 * c + wb
            j = np.arange(512)
            tq = np.arange(WIN)
            ok = ((j[:, None] - tq[None, :]) > 0) & ((j[:, None] - tq[None, :]) <= WIN)
            if wbg == 0:
                ok &= (j[:, None] >= WIN)
            okb = ok.astype(BF)
            for kt in range(4):
                swm[wb, kt] = okb[128 * kt:128 * (kt + 1), :]
        m = {
            "xT": np.ascontiguousarray(xa.T),
            "xN": xa,
            "ropeC": ropeCc,
            "ropeS": ropeSc,
            "causal_add": cadd,
            "causal_bin": cbin,
            "sw_mask": swm,
            "eps_tie": eps_tie,
            "ones_colB": np.asarray(ones_colB),
            "ones_row_f": ones_row_f,
            "identB": np.asarray(identB),
            "identF": identF,
            "w_qc": np.asarray(full["L0_attn_w_qc"], dtype=np.float32),
            "w_qup": np.asarray(full["L0_attn_w_qup"], dtype=np.float32),
            "kvc_w": np.asarray(full["L0_attn_kvc_w"], dtype=np.float32),
            "kvc_wz": np.asarray(full["L0_attn_kvc_wz"], dtype=np.float32),
            "kvc_biasT": kvc_biasT,
            "k_proj_w": np.asarray(full["L0_attn_k_proj_w"], dtype=np.float32),
            "v_proj_w": np.asarray(full["L0_attn_v_proj_w"], dtype=np.float32),
            "idx_c_w": np.asarray(full["L0_attn_idx_c_w"], dtype=np.float32),
            "idx_c_wz": np.asarray(full["L0_attn_idx_c_wz"], dtype=np.float32),
            "idx_c_biasT": idx_biasT,
            "w_dq": np.asarray(full["L0_attn_w_dq"], dtype=np.float32),
            "w_iuq": np.asarray(full["L0_attn_w_iuq"], dtype=np.float32),
            "w_w": np.asarray(full["L0_attn_w_w"], dtype=np.float32),
            "q_norm_w": np.asarray(full["L0_attn_q_norm_w"], dtype=np.float32).reshape(HD, 1),
            "k_norm_w": np.asarray(full["L0_attn_k_norm_w"], dtype=np.float32).reshape(HD, 1),
            "group_w": np.asarray(full["L0_attn_group_w"], dtype=np.float32),
            "final_w": np.asarray(full["L0_attn_final_w"], dtype=np.float32),
            "lut": np.asarray(full["L0_ffn_lut"], dtype=np.float32).reshape(1, LUTN),
            "gidxT": np.ascontiguousarray(gate_idxT[256 * c:256 * (c + 1), :]),
            "uidxT": np.ascontiguousarray(up_idxT[256 * c:256 * (c + 1), :]),
            "didxT": np.ascontiguousarray(down_idxT[:, 256 * c:256 * (c + 1)]),
        }
        in_maps.append(m)
    return in_maps


_CACHED = {}


def kernel(**inputs):
    if "nc" not in _CACHED:
        nc, _ = build_program(debug=False)
        _CACHED["nc"] = nc
    nc = _CACHED["nc"]
    in_maps = _host_inputs(inputs)
    res = run_bass_kernel_spmd(nc, in_maps, core_ids=list(range(NCORE)))
    parts = [np.asarray(res.results[c]["outT"]).T for c in range(NCORE)]
    out = np.concatenate(parts, axis=0)[None, :, :]
    return out.astype(np.float32)
